# version 06
# baseline (speedup 1.0000x reference)
import numpy as np
from contextlib import ExitStack

import concourse.bass as bass
import concourse.mybir as mybir
from concourse.bass_utils import run_bass_kernel_spmd

F32 = mybir.dt.float32
BF16 = mybir.dt.bfloat16
I32 = mybir.dt.int32
AF = mybir.ActivationFunctionType
OP = mybir.AluOpType

NCORE = 8
T = 2048
NK = 16
NTQ = 4
EPS_PRE = 1e-5
EPS_FFN = 1e-6
RG = [[0, 1, 2, 3, 4, 5, 6, 7]]
MAGIC = 0x5F3759DF + 1
INV_H = 1.0 / 2048.0

STAGE = "full"
DUMPS = []

TH_SEG = {1: "s0", 4: "s1", 7: "s2", 10: "s3", 13: "bc"}


class Sem:
    def __init__(self, name):
        self.name = name
        self.h = None
        self.n = 0


class DummyIns:
    def then_inc(self, *a, **k):
        return self


class DummyEng:
    def __getattr__(self, name):
        return lambda *a, **k: DummyIns()


class Rec:
    def __init__(self):
        self.ev = {}
        self.needed = set()
        self.plan = True

    serialized = ()

    def inc(self, sem, ins, key=None, v=1):
        if sem in self.serialized:
            if key is not None:
                if self.plan:
                    assert key not in self.ev, f"dup event {key}"
                    self.ev[key] = (sem, sem.n)
                else:
                    assert self.ev[key] == (sem, sem.n), f"replay drift at {key}"
            return ins
        sem.n += v
        if self.plan:
            if key is not None:
                assert key not in self.ev, f"dup event {key}"
                self.ev[key] = (sem, sem.n)
        else:
            ins.then_inc(sem.h, v)
            if key is not None:
                assert self.ev[key] == (sem, sem.n), f"replay drift at {key}"
        return ins

    def alias(self, key, src):
        if self.plan:
            assert key not in self.ev, key
            self.ev[key] = self.ev[src]

    def wait(self, e, key):
        if self.plan:
            self.needed.add(key)
        else:
            sem, val = self.ev[key]
            e.wait_ge(sem.h, val)

    def check(self):
        missing = self.needed - set(self.ev)
        assert not missing, \
            f"waits on unrecorded events: {sorted(map(str, missing))[:10]}"


def rsqrt_newton(e, out_f, src, scrA, scrB, scrC, eps, n_iter=3):
    e.tensor_scalar(out=scrB, in0=src, scalar1=eps, scalar2=-0.5,
                    op0=OP.add, op1=OP.mult)
    e.tensor_scalar(out=scrA, in0=scrB, scalar1=-2.0, scalar2=None,
                    op0=OP.mult)
    ai = scrA.bitcast(I32)
    oi = out_f.bitcast(I32)
    e.tensor_scalar(out=oi, in0=ai, scalar1=1, scalar2=None,
                    op0=OP.logical_shift_right)
    e.tensor_scalar(out=ai, in0=oi, scalar1=-1, scalar2=None,
                    op0=OP.bitwise_xor)
    e.tensor_scalar(out=oi, in0=ai, scalar1=MAGIC, scalar2=None, op0=OP.add)
    ins = None
    for _ in range(n_iter):
        e.tensor_tensor(out=scrA, in0=out_f, in1=out_f, op=OP.mult)
        e.tensor_tensor(out=scrA, in0=scrA, in1=scrB, op=OP.mult)
        e.tensor_scalar(out=scrC, in0=scrA, scalar1=1.5, scalar2=None, op0=OP.add)
        ins = e.tensor_tensor(out=out_f, in0=out_f, in1=scrC, op=OP.mult)
    return ins


SERIAL_OPS = {
    "tensor_copy", "tensor_tensor", "tensor_scalar", "tensor_single_scalar",
    "tensor_add", "tensor_sub", "tensor_mul", "tensor_max", "tensor_relu",
    "scalar_tensor_tensor", "tensor_tensor_scan", "tensor_reduce",
    "reduce_max", "reduce_sum", "memset", "memzero", "bn_stats", "bn_aggr",
    "reciprocal", "activation", "copy", "square", "sqrt", "mul", "add",
    "stream_shuffle", "select", "transpose",
}


class Ser:

    def __init__(self, e, sem, R):
        self._e, self._sem, self._R = e, sem, R

    def __getattr__(self, name):
        f = getattr(self._e, name)
        if name not in SERIAL_OPS:
            return f
        def g(*a, **k):
            if not self._R.plan:
                self._e.wait_ge(self._sem.h, self._sem.n)
            ins = f(*a, **k)
            self._sem.n += 1
            if not self._R.plan:
                ins.then_inc(self._sem.h, 1)
            return ins
        return g


def build_nc(stage):
    nc = bass.Bass("TRN2", target_bir_lowering=False)
    dtr = nc.dram_tensor
    ein, eout = "ExternalInput", "ExternalOutput"

    xs = dtr("xs", [T, 2048], F32, kind=ein).ap()
    W = {n: dtr(n, [2048, 2048], F32, kind=ein).ap()
         for n in ["w_f", "w_i", "w_v", "w_q", "w_ro", "w_up", "w_gt", "w_dn",
                   "agg_w", "bc_w", "gate_w"]}
    d1s = dtr("d1s", [2048, 1024], F32, kind=ein).ap()
    d2s = dtr("d2s", [1024, 2048], F32, kind=ein).ap()
    d1bs = dtr("d1bs", [2, 1024], F32, kind=ein).ap()
    FMW = dict(ingam=16, inbet=16, outgam=16, outbet=16, gateb=16, initf=16,
               aggb=32, bcb=32, d2b=32, png=16, pnb=16)
    fmv = {n: dtr(n, [128, w], F32, kind=ein).ap() for n, w in FMW.items()}
    msk = {n: dtr(n, [128, 1], F32, kind=ein).ap()
           for n in ["mkb0", "mkb1", "mb0", "mb1"]}
    predm = dtr("predm", [128, 8], F32, kind=ein).ap()
    predm1 = dtr("predm1", [128, 8], F32, kind=ein).ap()
    idn32 = dtr("idn32", [128, 128], F32, kind=ein).ap()
    idnb = dtr("idnb", [128, 128], BF16, kind=ein).ap()
    onesb = dtr("onesb", [128, 128], BF16, kind=ein).ap()
    out_o = dtr("o", [T, 2048], F32, kind=eout).ap()
    dump_sm = {t: dtr(f"dump_sm_{t}", [128, 592], F32, kind=eout).ap()
               for t in ("f1", "f2", "a")}
    dump_smb = dtr("dump_smb", [128, 768], BF16, kind=eout).ap()

    def idtr(name, shape, dt=F32):
        kind = "ExternalOutput" if name in DUMPS else "Internal"
        return dtr(name, shape, dt, kind=kind).ap()

    xf0 = idtr("xf0", [2048, T])
    x1f = idtr("x1f", [2048, T])
    x2f = idtr("x2f", [2048, T])
    x3f = idtr("x3f", [2048, T])
    Ssp = idtr("Ssp", [NK, 128, T])
    Psp = idtr("Psp", [NK, 128, T])
    xpre_sp = idtr("xpre_sp", [NK, 128, T])
    u_sp = idtr("u_sp", [NK, 128, T])
    gs_sp = idtr("gs_sp", [NK, 128, T])
    loc_sp = idtr("loc_sp", [NK, 128, T])
    d1bf = dtr("d1bf", [NK, 128, 1024], BF16).ap()
    d2bf = dtr("d2bf", [8, 128, 2048], BF16).ap()
    cc_bi = {t: dtr(f"cc_bi_{t}", [128, 32], F32).ap() for t in ("f1", "f2")}
    cc_bo = {t: dtr(f"cc_bo_{t}", [128, 32], F32).ap() for t in ("f1", "f2")}
    cc_ti = {t: dtr(f"cc_ti_{t}", [2, 2048], F32).ap() for t in ("f1", "f2")}
    cc_to = {t: dtr(f"cc_to_{t}", [2, 2048], F32).ap() for t in ("f1", "f2")}
    cc_gi = dtr("cc_gi", [128, 32], F32).ap()
    cc_go = dtr("cc_go", [NCORE * 128, 32], F32).ap()

    ctx = ExitStack()
    sbt = lambda n, s, d: ctx.enter_context(nc.sbuf_tensor(n, s, d))
    aA = sbt("aA", [128, 32768], BF16)
    aB = sbt("aB", [128, 32768], BF16)
    aC = sbt("aC", [128, 32768], BF16)
    d1brow = sbt("d1brow", [2, 1024], F32)
    sm = sbt("sm", [128, 592], F32)
    smb = sbt("smb", [128, 768], BF16)
    ident = sbt("ident", [128, 128], F32)
    identb = sbt("identb", [128, 128], BF16)
    onesbt = sbt("onesbt", [128, 128], BF16)
    psA = ctx.enter_context(nc.psum_tensor("psA", [128, 3584], F32))
    psB = ctx.enter_context(nc.psum_tensor("psB", [128, 1024], BF16))

    aCf = aC[:].bitcast(F32)
    psBf = psB[:].bitcast(F32)

    def bank(i):
        return psA[:, i * 512:(i + 1) * 512]

    def cf(a):
        return aCf[:, a:a + 2048]

    def cfw(a, w):
        return aCf[:, a:a + w]

    def cbw(a, w):
        return aC[:, a:a + w]

    SM = dict(lastX=0, xpl=16, xnorm=32, ccst=48, xnall=80, s_in=112, ABs=128,
              g2=160, b2=176, thf=192, bcf=224, bcmy=256, ftmp=272, ftmp2=288,
              mkb0=304, mkb1=305, mb0=306, mb1=307, predm=308, predm1=316,
              ingam=324, inbet=340, outgam=356, outbet=372, gateb=388,
              initf=404, aggb=420, bcb=452, d2b=484, png=516, pnb=532,
              tln=548, gbh=564)

    def smv(name, w=16):
        o = SM[name]
        return sm[:, o:o + w]

    def smc(name, j=0):
        return sm[:, SM[name] + j:SM[name] + j + 1]

    def smbv(o, w):
        return smb[:, o:o + w]

    def evenodd(ap32, b):
        return ap32.rearrange("p (c b) -> p c b", b=2)[:, :, b]

    bmw = aCf[0:2, 4096:6144]

    SP, SA, SV = (Sem(n) for n in ["SP", "SA", "SV"])
    CCs = {n: Sem("cc_" + n) for n in
           ["f1b", "f1s0", "f1s1", "f1s2", "f1s3",
            "f2b", "f2s0", "f2s1", "f2s2", "f2s3", "ag"]}
    _pools = {}

    def S2(name):
        if name not in _pools:
            _pools[name] = (Sem(name + "0"), Sem(name + "1"))
        return _pools[name]

    def S1(name):
        if name not in _pools:
            _pools[name] = Sem(name)
        return _pools[name]

    for nm in ["sld", "sldb", "sst", "sxl", "sxl2", "sxl2b", "sxps", "sgst", "sgl", "sglb", "sstob", "sxp2",
               "sust", "sul3", "sxl3", "ssto", "sssp", "spsp", "slst",
               "spfs", "spfp", "sxr", "slrl", "gwl", "gagl", "gdsl", "gbcl",
               "gql", "gb1l", "gb2l", "grol"]:
        S2(nm)
    for nm in ["scl", "gstg", "ggw", "gcc"]:
        S1(nm)
    allsems = [SP, SA, SV] + list(CCs.values())
    for v in _pools.values():
        if isinstance(v, tuple):
            allsems.extend(v)
        else:
            allsems.append(v)

    R = Rec()
    R.serialized = (SV, SA)

    def wsl_src(w, rc):
        return w[rc * 128:(rc + 1) * 128, :]

    def cc_run(e, name, kind, op, i, o):
        ins = e.collective_compute(kind, op, replica_groups=RG,
                                   ins=[i.opt()], outs=[o.opt()])
        if not R.plan:
            ins.then_inc(CCs[name].h)
            e.wait_ge(CCs[name].h, 1)

    def tin_sync(e):
        pairs = [(ident[:], idn32), (identb[:], idnb), (onesbt[:], onesb),
                 (smc("mkb0"), msk["mkb0"]), (smc("mkb1"), msk["mkb1"]),
                 (smc("mb0"), msk["mb0"]), (smc("mb1"), msk["mb1"]),
                 (smv("predm", 8), predm), (smv("predm1", 8), predm1),
                 (d1brow[:], d1bs)]
        for n in FMW:
            pairs.append((smv(n, FMW[n]), fmv[n]))
        R.wait(e, ("ti", "sminit"))
        for i, (dst, src) in enumerate(pairs):
            key = ("ti", "cl") if i == len(pairs) - 1 else None
            R.inc(S1("scl"), e.dma_start(out=dst, in_=src), key, 16)
        for hc in range(NK):
            par = hc % 2
            if hc < 2:
                R.wait(e, ("ti", "cl"))
            else:
                R.wait(e, ("ti", "pe", hc - 2, 3))
            src = xs[:, hc * 128:(hc + 1) * 128].rearrange(
                "(tc p) m -> p tc m", p=128)
            dst = cf(par * 2048).rearrange("p (tc m) -> p tc m", m=128)
            R.inc(S2("sld")[hc % 2], e.dma_start(out=dst, in_=src),
                  ("ti", "ld", hc), 16)
            R.wait(e, ("ti", "ev", hc, 3))
            R.inc(S2("sst")[hc % 2], e.dma_start(
                out=xf0[hc * 128:(hc + 1) * 128, :],
                in_=cf(4096 + par * 2048)), ("ti", "st", hc), 16)

    def tin_gpsimd(e):
        sg = S1("gstg")
        for k in range(NK):
            R.inc(sg, e.dma_start(out=aB[:, 0:1024],
                                  in_=d1s[k * 128:(k + 1) * 128, :]), None, 16)
            if not R.plan:
                e.wait_ge(sg.h, sg.n)
            R.inc(sg, e.dma_start(out=d1bf[k], in_=aB[:, 0:1024]), None, 16)
            if not R.plan:
                e.wait_ge(sg.h, sg.n)
        for k in range(8):
            R.inc(sg, e.dma_start(out=aB[:, 0:2048],
                                  in_=d2s[k * 128:(k + 1) * 128, :]), None, 16)
            if not R.plan:
                e.wait_ge(sg.h, sg.n)
            key = ("ti", "stg") if k == 7 else None
            R.inc(sg, e.dma_start(out=d2bf[k], in_=aB[:, 0:2048]), key, 16)
            if not R.plan:
                e.wait_ge(sg.h, sg.n)
        for k in range(NK):
            key = ("f1", "gw") if k == NK - 1 else None
            R.inc(S1("ggw"), e.dma_start(
                out=aB[:, k * 2048:(k + 1) * 2048],
                in_=W["gate_w"][k * 128:(k + 1) * 128, :]), key, 16)

    def tin_tensor(e):
        for hc in range(NK):
            par = hc % 2
            R.wait(e, ("ti", "ld", hc))
            for tc in range(NK):
                g4 = tc // 4
                gidx = hc * 4 + g4
                if gidx >= 2:
                    R.wait(e, ("ti", "ev", (gidx - 2) // 4, (gidx - 2) % 4))
                ins = e.transpose(
                    bank(g4 % 2)[:, (tc % 4) * 128:(tc % 4) * 128 + 128],
                    cf(par * 2048)[:, tc * 128:tc * 128 + 128],
                    ident[:])
                if tc % 4 == 3:
                    R.inc(SP, ins, ("ti", "pe", hc, g4))

    def tin_scalar(e):
        pass

    def tin_vector(e):
        e.memset(sm[:], 0.0)
        e.memset(smb[:], 0.0)
        R.inc(SV, e.nop(), ("ti", "sminit"))
        R.wait(e, ("ti", "cl"))
        e.tensor_scalar(out=smv("g2"), in0=smv("outgam"), scalar1=0.5,
                        scalar2=None, op0=OP.mult)
        e.tensor_scalar(out=smv("b2"), in0=smv("outbet"),
                        scalar1=0.5, scalar2=None, op0=OP.mult)
        R.inc(SV, e.tensor_scalar(out=smv("gbh"), in0=smv("gateb"),
                                  scalar1=0.5, scalar2=None, op0=OP.mult),
              ("ti", "pc"))
        for hc in range(NK):
            par = hc % 2
            for g4 in range(4):
                R.wait(e, ("ti", "pe", hc, g4))
                if hc >= 2 and g4 == 0:
                    R.wait(e, ("ti", "st", hc - 2))
                R.inc(SV, e.tensor_copy(
                    out=cf(4096 + par * 2048)[:, g4 * 512:g4 * 512 + 512],
                    in_=bank(g4 % 2)), ("ti", "ev", hc, g4))

    def make_ffn(t, X, O, in_evt, bank_evt, gw_evt, arena_evt):
        P = {}
        P["t"], P["X"], P["O"] = t, X, O
        P["in_evt"] = in_evt
        P["bank_evt"] = bank_evt
        P["gw_evt"] = gw_evt
        P["arena_evt"] = arena_evt
        P["xch"] = lambda p: cf(p * 2048)
        P["xch3"] = lambda rc: cf((rc % 3) * 2048 if rc % 3 < 2 else 10240)
        P["sqs"] = lambda p: cbw(8192 + p * 2048, 2048)
        P["dscr"] = cf(4096)
        P["mu"] = cf(6144)
        P["rs"] = cf(8192)
        P["gss"] = lambda p: cf(12288 + p * 2048)
        P["wsl2"] = lambda n: cbw(20480 + (n % 2) * 2048, 2048)
        P["ub"] = lambda p: aA[:, p * 2048:p * 2048 + 2048]
        P["ubsq"] = lambda p: P["sqs"](p)
        seg = []
        seg.append(("th", "agg"))
        for rc in range(NK):
            seg.append(("gate", rc))
            if rc in TH_SEG:
                seg.append(("th", TH_SEG[rc]))
        P["seg"] = seg
        return P

    def ffn_sync(P, e):
        t, X, O = P["t"], P["X"], P["O"]
        for k in range(NK):
            R.wait(e, P["in_evt"](k))
            if k == 0:
                R.wait(e, P["arena_evt"])
            if k >= 2:
                R.wait(e, (t, "cast", k - 2))
            R.inc(S2("sxl")[k % 2], e.dma_start(
                out=P["xch"](k % 2), in_=X[k * 128:(k + 1) * 128, :]),
                (t, "xl", k), 16)
        for k in range(NK):
            R.wait(e, (t, "cast", NK - 2 + k) if k < 2 else (t, "apl", k - 2))
            if k < 2:
                R.wait(e, (t, "stat"))
            R.inc(S2("sxl2")[k % 2], e.dma_start(
                out=P["xch"](k % 2), in_=X[k * 128:(k + 1) * 128, :]),
                (t, "xl2", k), 16)
            R.wait(e, (t, "apl", k))
            R.inc(S2("sxps")[k % 2], e.dma_start(
                out=xpre_sp[k], in_=P["gss"](k % 2)), (t, "xps", k), 16)
        for rc in range(NK):
            R.wait(e, (t, "gse", rc))
            R.inc(S2("sgst")[rc % 2], e.dma_start(
                out=gs_sp[rc], in_=P["gss"](rc % 2)), (t, "gst", rc), 16)
        for rc in range(NK):
            if rc < 2:
                R.wait(e, (t, "gst", NK - 2 + rc))
            elif rc == 2:
                R.wait(e, (t, "bcr"))
                R.wait(e, (t, "ust", 0))
            else:
                R.wait(e, (t, "uc", rc - 3))
                R.wait(e, (t, "ust", rc - 2))
            R.inc((S2("sgl") + S2("sglb"))[rc % 3], e.dma_start(
                out=P["xch3"](rc), in_=gs_sp[rc]), (t, "gl", rc), 16)
            R.inc(S2("sxp2")[rc % 2], e.dma_start(
                out=P["gss"](rc % 2), in_=xpre_sp[rc]), (t, "xpl2", rc), 16)
            R.wait(e, (t, "uc", rc))
            R.inc(S2("sust")[rc % 2], e.dma_start(
                out=u_sp[rc], in_=P["gss"](rc % 2)), (t, "ust", rc), 16)
        for rc in range(NK):
            if rc < 3:
                R.wait(e, (t, "cst"))
            else:
                R.wait(e, (t, "sto", rc - 3))
            R.inc((S2("sul3") + S2("sglb"))[rc % 3 if rc % 3 < 2 else 3],
                  e.dma_start(out=P["xch3"](rc), in_=u_sp[rc]),
                  (t, "ul3", rc), 16)
            if rc >= 2:
                R.wait(e, (t, "fbo", rc - 2))
            R.inc(S2("sxl3")[rc % 2], e.dma_start(
                out=P["gss"](rc % 2), in_=X[rc * 128:(rc + 1) * 128, :]),
                (t, "xl3", rc), 16)
            R.wait(e, (t, "fbo", rc))
            R.inc((S2("ssto") + S2("sstob"))[rc % 3], e.dma_start(
                out=O[rc * 128:(rc + 1) * 128, :], in_=P["xch3"](rc)),
                (t, "sto", rc), 16)

    def ffn_gpsimd(P, e):
        t = P["t"]
        if P["gw_evt"] is None:
            R.wait(e, P.get("gwl_evt") or P["arena_evt"])
            for k in range(NK):
                key = (t, "gw") if k == NK - 1 else None
                R.inc(S1("ggw"), e.dma_start(
                    out=aB[:, k * 2048:(k + 1) * 2048],
                    in_=W["gate_w"][k * 128:(k + 1) * 128, :]), key, 16)
            gw_evt = (t, "gw")
        else:
            gw_evt = P["gw_evt"]
        P["gw_key"] = gw_evt
        R.wait(e, (t, "lnl"))
        if t == "f1":
            R.inc(S1("gcc"), e.dma_start(out=dump_sm["a"], in_=sm[:]),
                  (t, "smdump0"), 16)
            R.inc(S1("gcc"), e.dma_start(out=dump_smb, in_=smb[:]),
                  (t, "smdump0b"), 16)
        R.inc(S1("gcc"), e.dma_start(out=cc_bi[t], in_=smv("ccst", 32)),
              (t, "ccs"), 16)
        R.wait(e, (t, "ccs"))
        cc_run(e, t + "b", "AllReduce", OP.add, cc_bi[t], cc_bo[t])
        R.inc(S1("gcc"), e.dma_start(out=smv("xnall", 32), in_=cc_bo[t]),
              (t, "ccb_back"), 16)
        for k in range(NK):
            if k >= 2:
                R.wait(e, (t, "ak", k - 2))
            else:
                R.wait(e, (t, "evs", 1))
            R.inc(S2("gagl")[k % 2], e.dma_start(
                out=P["wsl2"](k),
                in_=W["agg_w"][k * 128:(k + 1) * 128, :]), (t, "agl", k), 16)
        for s in range(4):
            for k in range(NK):
                n = s * 24 + k
                if k >= 2:
                    R.wait(e, (t, "dk", s, k - 2))
                elif s > 0:
                    R.wait(e, (t, "dk", s - 1, 22 + k))
                else:
                    R.wait(e, (t, "ak", 14 + k))
                R.inc(S2("gdsl")[n % 2], e.dma_start(
                    out=P["wsl2"](NK + n)[:, 0:1024], in_=d1bf[k]),
                    (t, "dsl", s, k), 16)
            for k in range(8):
                n = s * 24 + 16 + k
                R.wait(e, (t, "dk", s, 14 + k))
                R.inc(S2("gdsl")[n % 2], e.dma_start(
                    out=P["wsl2"](NK + n), in_=d2bf[k]),
                    (t, "dsl", s, 16 + k), 16)
            R.wait(e, (t, "pev", "d2", s))
            R.inc(S1("gcc"), e.dma_start(out=cc_ti[t], in_=bmw),
                  (t, "ars", s), 16)
            R.wait(e, (t, "ars", s))
            cc_run(e, t + "s%d" % s, "AllReduce", OP.add, cc_ti[t], cc_to[t])
            R.inc(S1("gcc"), e.dma_start(out=bmw, in_=cc_to[t]),
                  (t, "arb", s), 16)
        for k in range(NK):
            n = NK + 4 * 24 + k
            if k >= 2:
                R.wait(e, (t, "bk", k - 2))
            else:
                R.wait(e, (t, "dk", 3, 22 + k))
            R.inc(S2("gbcl")[n % 2], e.dma_start(
                out=P["wsl2"](n),
                in_=W["bc_w"][k * 128:(k + 1) * 128, :]), (t, "bcl", k), 16)
        R.wait(e, (t, "bcr"))
        R.inc(S1("gcc"), e.dma_start(out=dump_sm[t], in_=sm[:]),
              (t, "smdump"), 16)

    def ffn_tensor(P, e):
        t = P["t"]
        R.wait(e, P["bank_evt"])
        for k in range(NK):
            R.wait(e, (t, "cast", k))
            for tq in range(NTQ):
                ins = e.matmul(bank(tq), onesbt[:],
                               aA[:, k * 2048 + tq * 512:k * 2048 + tq * 512 + 512],
                               start=(k == 0), stop=(k == NK - 1))
            if k == NK - 1:
                R.inc(SP, ins, (t, "smm", 0))
        R.wait(e, (t, "evs", 0))
        for k in range(NK):
            R.wait(e, (t, "sq", k))
            for tq in range(NTQ):
                ins = e.matmul(bank(tq), onesbt[:],
                               P["sqs"](k % 2)[:, tq * 512:tq * 512 + 512],
                               start=(k == 0), stop=(k == NK - 1))
            R.inc(SP, ins, (t, "s2", k))
        R.alias((t, "smm", 1), (t, "s2", NK - 1))
        R.wait(e, (t, "lnc1"))
        e.matmul(psA[:, 4 * 512:4 * 512 + 16], onesbt[:], smbv(384, 16),
                 start=True, stop=True)
        R.wait(e, (t, "lnc2"))
        R.inc(SP, e.matmul(psA[:, 4 * 512 + 16:4 * 512 + 32], onesbt[:],
                           smbv(640, 16), start=True, stop=True), (t, "lnmm"))
        for item in P["seg"]:
            if item[0] == "gate":
                ffn_t_gate(P, e, item[1])
            else:
                ffn_t_th(P, e, item[1])
        ffn_t_fac(P, e)

    def ffn_t_gate(P, e, rc):
        t = P["t"]
        if rc == 0:
            R.wait(e, (t, "apl", NK - 1))
            R.wait(e, P["gw_key"])
        for tq in range(NTQ):
            g = rc * NTQ + tq
            if g >= 2:
                R.wait(e, (t, "gsg", (g - 2) // 4, (g - 2) % 4))
            for k in range(NK):
                ins = e.matmul(
                    bank(g % 2),
                    aB[:, k * 2048 + rc * 128:k * 2048 + rc * 128 + 128],
                    aA[:, k * 2048 + tq * 512:k * 2048 + tq * 512 + 512],
                    start=(k == 0), stop=(k == NK - 1))
            R.inc(SP, ins, (t, "gmm", rc, tq))

    def ffn_t_fold(P, e, s):
        t = P["t"]
        if s == "agg":
            R.wait(e, (t, "pev", "agg"))
        else:
            R.wait(e, (t, "arb", s))
        for c in range(NK):
            ins = e.transpose(psA[:, 6 * 512 + 2 * c:6 * 512 + 2 * c + 2],
                              bmw[0:2, c * 128:(c + 1) * 128],
                              ident[0:2, 0:2])
        R.inc(SP, ins, (t, "fold", s))

    def ffn_t_th(P, e, it):
        t = P["t"]
        if it == "agg":
            R.wait(e, (t, "xna"))
            for k in range(NK):
                R.wait(e, (t, "agl", k))
                for q in range(4):
                    ins = e.matmul(
                        psA[0:2, 1024 + q * 512:1024 + q * 512 + 512],
                        smbv(512, 32)[:, 2 * k:2 * k + 2],
                        P["wsl2"](k)[:, q * 512:q * 512 + 512],
                        start=(k == 0), stop=(k == NK - 1))
                R.inc(SP, ins, (t, "ak", k))
        elif it == "bc":
            ffn_t_fold(P, e, 3)
            R.wait(e, (t, "thb", 4))
            for k in range(NK):
                R.wait(e, (t, "bcl", k))
                for q in range(4):
                    ins = e.matmul(
                        psA[0:2, 1024 + q * 512:1024 + q * 512 + 512],
                        smbv(0, 32)[:, 2 * k:2 * k + 2],
                        P["wsl2"](NK + 4 * 24 + k)[:, q * 512:q * 512 + 512],
                        start=(k == 0), stop=(k == NK - 1))
                R.inc(SP, ins, (t, "bk", k))
            R.wait(e, (t, "pev", "bc"))
            for c in range(NK):
                ins = e.transpose(psA[:, 6 * 512 + 2 * c:6 * 512 + 2 * c + 2],
                                  bmw[0:2, c * 128:(c + 1) * 128],
                                  ident[0:2, 0:2])
            R.inc(SP, ins, (t, "fold", "bc"))
        else:
            s = int(it[1])
            ffn_t_fold(P, e, "agg" if s == 0 else s - 1)
            R.wait(e, (t, "thb", s))
            R.wait(e, (t, "pev", "fold", s))
            e.matmul(psA[:, 6 * 512:6 * 512 + 32], onesbt[:], smbv(0, 32),
                     start=True, stop=True)
            R.wait(e, (t, "thsq", s))
            R.inc(SP, e.matmul(psA[:, 6 * 512 + 32:6 * 512 + 64], onesbt[:],
                               smbv(640, 32), start=True, stop=True),
                  (t, "lnm", s))
            R.wait(e, (t, "tnc", s))
            if s > 0:
                R.wait(e, (t, "pev", "d2", s - 1))
            for k in range(NK):
                R.wait(e, (t, "dsl", s, k))
                for q in range(2):
                    ins = e.matmul(
                        psA[0:2, 1024 + q * 512:1024 + q * 512 + 512],
                        smbv(128, 32)[:, 2 * k:2 * k + 2],
                        P["wsl2"](NK + s * 24 + k)[:, q * 512:q * 512 + 512],
                        start=(k == 0), stop=(k == NK - 1))
                R.inc(SP, ins, (t, "dk", s, k))
            R.alias((t, "pmmd1", s), (t, "dk", s, NK - 1))
            R.wait(e, (t, "tbb", s))
            if s > 0:
                R.wait(e, (t, "ttb", s - 1))
            tb = aC[0:2, 10240:11264]
            for j in range(8):
                ins = e.transpose(psB[:, 2 * j:2 * j + 2],
                                  tb[0:2, j * 128:(j + 1) * 128],
                                  identb[0:2, 0:2])
            R.inc(SP, ins, (t, "ttr", s))
            R.wait(e, (t, "ttb", s))
            for k in range(8):
                R.wait(e, (t, "dsl", s, 16 + k))
                for q in range(4):
                    ins = e.matmul(
                        psA[0:2, 1024 + q * 512:1024 + q * 512 + 512],
                        smbv(256, 16)[:, 2 * k:2 * k + 2],
                        P["wsl2"](NK + s * 24 + 16 + k)[:, q * 512:q * 512 + 512],
                        start=(k == 0), stop=(k == 7))
                R.inc(SP, ins, (t, "dk", s, 16 + k))
            R.alias((t, "pmmd2", s), (t, "dk", s, 23))

    def ffn_t_fac(P, e):
        t = P["t"]
        R.wait(e, (t, "pev", "bcf"))
        R.wait(e, (t, "ttb", 3))
        R.wait(e, (t, "lnr"))
        for rc in range(NK):
            R.wait(e, (t, "ubc", rc))
            for tq in range(NTQ):
                ins = e.matmul(bank(tq), onesbt[:],
                               P["ub"](rc % 2)[:, tq * 512:tq * 512 + 512],
                               start=(rc == 0), stop=(rc == NK - 1))
            R.wait(e, (t, "usq", rc))
            for tq in range(NTQ):
                dst = bank(4 + tq) if tq < 3 else psBf[:]
                ins = e.matmul(dst, onesbt[:],
                               P["sqs"](rc % 2)[:, tq * 512:tq * 512 + 512],
                               start=(rc == 0), stop=(rc == NK - 1))
            R.inc(SP, ins, (t, "cmm", rc))

    def ffn_scalar(P, e):
        t = P["t"]
        for k in range(NK):
            R.wait(e, (t, "cast", k))
            if k == 0:
                for ev in P.get("scr_evt", []):
                    R.wait(e, ev)
            if k >= 2:
                R.wait(e, (t, "s2", k - 2))
            R.inc(SA, e.activation(out=P["sqs"](k % 2),
                                   in_=aA[:, k * 2048:(k + 1) * 2048],
                                   func=AF.Square), (t, "sq", k))
        R.wait(e, (t, "lnc1"))
        R.inc(SA, e.activation(out=smbv(640, 128), in_=smbv(384, 128),
                               func=AF.Square), (t, "lnc2"))
        for item in P["seg"]:
            if item[0] == "gate":
                rc = item[1]
                if rc == 0:
                    R.wait(e, (t, "xps", NK - 2))
                    R.wait(e, (t, "xps", NK - 1))
                for tq in range(NTQ):
                    g = rc * NTQ + tq
                    R.wait(e, (t, "gmm", rc, tq))
                    if g >= 2:
                        R.wait(e, (t, "gsd", (g - 2) // 4, (g - 2) % 4))
                    R.inc(SA, e.activation(
                        out=P["mu"][:, (g % 2) * 512:(g % 2) * 512 + 512],
                        in_=bank(g % 2), func=AF.Tanh,
                        scale=0.5, bias=smc("gbh", rc)), (t, "gsg", rc, tq))
            else:
                it = item[1]
                if it in ("agg", "bc"):
                    continue
                s = int(it[1])
                R.wait(e, (t, "thb", s))
                R.inc(SA, e.activation(out=smbv(640, 128), in_=smbv(0, 128),
                                       func=AF.Square), (t, "thsq", s))
                R.wait(e, (t, "pev", "d1", s))
                av = aCf[0:2, 8832:9856]
                e.activation(out=av, in_=bmw[:, 0:1024], func=AF.Abs,
                             scale=0.7071067811865476)
                e.activation(out=aCf[0:2, 0:1024], in_=av, func=AF.Square)
                R.inc(SA, e.activation(out=bmw[:, 1024:2048],
                                       in_=aCf[0:2, 0:1024], func=AF.Exp,
                                       scale=-1.0), (t, "gel", s))
        for rc in range(NK):
            R.wait(e, (t, "ubc", rc))
            if rc >= 2:
                R.wait(e, (t, "cmm", rc - 2))
            R.inc(SA, e.activation(out=P["sqs"](rc % 2), in_=P["ub"](rc % 2),
                                   func=AF.Square), (t, "usq", rc))

    _EA = (0.254829592, -0.284496736, 1.421413741, -1.453152027, 1.061405429)
    _EP = 0.3275911

    def ffn_v_th(P, e, it):
        t = P["t"]
        if it == "agg":
            R.wait(e, (t, "ak", NK - 1))
            R.inc(SV, e.tensor_copy(out=bmw, in_=psA[0:2, 1024:3072]),
                  (t, "pev", "agg"))
            return
        if it == "bc":
            R.wait(e, (t, "fold", 3))
            e.tensor_tensor(out=smv("thf", 32),
                            in0=psA[:, 6 * 512:6 * 512 + 32],
                            in1=smv("thf", 32), op=OP.add)
            e.tensor_tensor(out=smv("thf", 32), in0=smv("thf", 32),
                            in1=smv("d2b", 32), op=OP.add)
            R.inc(SV, e.tensor_copy(out=smbv(0, 128), in_=sm[:, 192:320]),
                  (t, "thb", 4))
            R.wait(e, (t, "bk", NK - 1))
            R.inc(SV, e.tensor_copy(out=bmw, in_=psA[0:2, 1024:3072]),
                  (t, "pev", "bc"))
            R.wait(e, (t, "fold", "bc"))
            e.tensor_tensor(out=smv("bcf", 32), in0=psA[:, 6 * 512:6 * 512 + 32],
                            in1=smv("bcb", 32), op=OP.add)
            R.inc(SV, e.nop(), (t, "pev", "bcf"))
            e.tensor_scalar(out=smv("bcmy"), in0=evenodd(smv("bcf", 32), 0),
                            scalar1=smc("mb0"), scalar2=None, op0=OP.mult)
            e.tensor_scalar(out=smv("ftmp"), in0=evenodd(smv("bcf", 32), 1),
                            scalar1=smc("mb1"), scalar2=None, op0=OP.mult)
            R.inc(SV, e.tensor_tensor(out=smv("bcmy"), in0=smv("bcmy"),
                                      in1=smv("ftmp"), op=OP.add), (t, "bcr"))
            return
        s = int(it[1])
        R.wait(e, (t, "fold", "agg" if s == 0 else s - 1))
        if s == 0:
            e.tensor_tensor(out=smv("thf", 32),
                            in0=psA[:, 6 * 512:6 * 512 + 32],
                            in1=smv("aggb", 32), op=OP.add)
        else:
            e.tensor_tensor(out=smv("thf", 32),
                            in0=psA[:, 6 * 512:6 * 512 + 32],
                            in1=smv("thf", 32), op=OP.add)
            e.tensor_tensor(out=smv("thf", 32), in0=smv("thf", 32),
                            in1=smv("d2b", 32), op=OP.add)
        R.inc(SV, e.nop(), (t, "pev", "fold", s))
        R.inc(SV, e.tensor_copy(out=smbv(0, 128), in_=sm[:, 192:320]),
              (t, "thb", s))
        R.wait(e, (t, "lnm", s))
        th_ln(P, e, s)
        R.wait(e, (t, "pmmd1", s))
        R.inc(SV, e.tensor_tensor(out=bmw[:, 0:1024],
                                  in0=psA[0:2, 1024:2048],
                                  in1=d1brow[:], op=OP.add),
              (t, "pev", "d1", s))
        R.wait(e, (t, "gel", s))
        av = aCf[0:2, 8832:9856]
        w = bmw[:, 1024:2048]
        S1 = aCf[0:2, 0:1024]
        S2 = aCf[0:2, 1024:2048]
        S3 = aCf[0:2, 2048:3072]
        S4 = aCf[0:2, 3072:4096]
        e.tensor_scalar(out=S1, in0=av, scalar1=_EP, scalar2=1.0,
                        op0=OP.mult, op1=OP.add)
        e.reciprocal(out=S2, in_=S1)
        e.tensor_scalar(out=S3, in0=S2, scalar1=_EA[4], scalar2=_EA[3],
                        op0=OP.mult, op1=OP.add)
        for aK in (_EA[2], _EA[1], _EA[0]):
            e.tensor_tensor(out=S4, in0=S3, in1=S2, op=OP.mult)
            e.tensor_scalar(out=S3, in0=S4, scalar1=aK, scalar2=None,
                            op0=OP.add)
        e.tensor_tensor(out=S4, in0=S3, in1=S2, op=OP.mult)
        e.tensor_tensor(out=S1, in0=w, in1=S4, op=OP.mult)
        e.tensor_scalar(out=S2, in0=S1, scalar1=-1.0, scalar2=1.0,
                        op0=OP.mult, op1=OP.add)
        e.tensor_tensor(out=S3, in0=av, in1=S2, op=OP.mult)
        e.tensor_scalar(out=S4, in0=S3, scalar1=0.7071067811865476,
                        scalar2=None, op0=OP.mult)
        tb = aC[0:2, 10240:11264]
        R.inc(SV, e.scalar_tensor_tensor(out=tb, in0=bmw[:, 0:1024],
                                         scalar=0.5, in1=S4,
                                         op0=OP.mult, op1=OP.add),
              (t, "tbb", s))
        R.wait(e, (t, "ttr", s))
        R.inc(SV, e.tensor_copy(out=smbv(256, 16), in_=psB[:, 0:16]),
              (t, "ttb", s))
        R.wait(e, (t, "pmmd2", s))
        R.inc(SV, e.tensor_copy(out=bmw, in_=psA[0:2, 1024:3072]),
              (t, "pev", "d2", s))

    def ffn_vector(P, e):
        t = P["t"]
        R.wait(e, P["arena_evt"])
        for k in range(NK):
            R.wait(e, (t, "xl", k))
            e.tensor_copy(out=aA[:, k * 2048:(k + 1) * 2048], in_=P["xch"](k % 2))
            R.inc(SV, e.tensor_copy(out=smc("lastX", k),
                                    in_=P["xch"](k % 2)[:, 2047:2048]),
                  (t, "cast", k))
        R.wait(e, (t, "smm", 0))
        for tq in range(NTQ):
            ins = e.tensor_copy(out=P["gss"](0)[:, tq * 512:tq * 512 + 512],
                                in_=bank(tq))
        R.inc(SV, ins, (t, "evs", 0))
        R.wait(e, (t, "smm", 1))
        for tq in range(NTQ):
            ins = e.tensor_copy(out=P["gss"](1)[:, tq * 512:tq * 512 + 512],
                                in_=bank(tq))
        R.inc(SV, ins, (t, "evs", 1))
        e.tensor_scalar(out=P["mu"], in0=P["gss"](0), scalar1=INV_H,
                        scalar2=None, op0=OP.mult)
        e.tensor_scalar(out=P["rs"], in0=P["gss"](1), scalar1=INV_H,
                        scalar2=None, op0=OP.mult)
        e.tensor_tensor(out=P["gss"](0), in0=P["mu"], in1=P["mu"], op=OP.mult)
        e.tensor_tensor(out=P["gss"](1), in0=P["rs"], in1=P["gss"](0),
                        op=OP.subtract)
        rsqrt_newton(e, P["rs"], P["gss"](1), P["gss"](0), P["dscr"],
                     P["xch"](0), EPS_PRE)
        R.inc(SV, e.nop(), (t, "stat"))
        e.tensor_scalar(out=smv("xpl"), in0=smv("lastX"),
                        scalar1=P["mu"][:, 2047:2048],
                        scalar2=P["rs"][:, 2047:2048],
                        op0=OP.subtract, op1=OP.mult)
        R.inc(SV, e.tensor_copy(out=smbv(384, 128), in_=sm[:, 16:144]),
              (t, "lnc1"))
        R.wait(e, (t, "lnmm"))
        tln = smv("tln", 16)
        e.tensor_reduce(out=tln[:, 0:1].rearrange("p (a o) -> p a o", o=1),
                        in_=psA[:, 4 * 512:4 * 512 + 16].rearrange(
                            "p (a c) -> p a c", a=1),
                        axis=mybir.AxisListType.X, op=OP.add)
        R.inc(SV, e.tensor_reduce(
            out=tln[:, 1:2].rearrange("p (a o) -> p a o", o=1),
            in_=psA[:, 4 * 512 + 16:4 * 512 + 32].rearrange(
                "p (a c) -> p a c", a=1),
            axis=mybir.AxisListType.X, op=OP.add), (t, "lnr"))
        e.tensor_scalar(out=tln[:, 4:5], in0=tln[:, 0:1], scalar1=INV_H,
                        scalar2=None, op0=OP.mult)
        e.tensor_scalar(out=tln[:, 5:6], in0=tln[:, 1:2], scalar1=INV_H,
                        scalar2=None, op0=OP.mult)
        e.tensor_tensor(out=tln[:, 2:3], in0=tln[:, 4:5], in1=tln[:, 4:5],
                        op=OP.mult)
        e.tensor_tensor(out=tln[:, 6:7], in0=tln[:, 5:6], in1=tln[:, 2:3],
                        op=OP.subtract)
        ws = P["gss"](0)
        e.memset(ws[:, 0:640], 1.0)
        e.tensor_copy(out=ws[:, 6:7], in_=tln[:, 6:7])
        rsqrt_newton(e, ws[:, 384:512], ws[:, 0:128], ws[:, 128:256],
                     ws[:, 256:384], ws[:, 512:640], EPS_FFN)
        e.tensor_copy(out=tln[:, 7:8], in_=ws[:, 384 + 6:384 + 7])
        e.tensor_scalar(out=smv("xnorm"), in0=smv("xpl"), scalar1=tln[:, 4:5],
                        scalar2=tln[:, 7:8], op0=OP.subtract, op1=OP.mult)
        e.tensor_tensor(out=smv("xnorm"), in0=smv("xnorm"), in1=smv("ingam"),
                        op=OP.mult)
        e.tensor_tensor(out=smv("xnorm"), in0=smv("xnorm"), in1=smv("inbet"),
                        op=OP.add)
        e.tensor_scalar(out=evenodd(smv("ccst", 32), 0), in0=smv("xnorm"),
                        scalar1=smc("mkb0"), scalar2=None, op0=OP.mult)
        R.inc(SV, e.tensor_scalar(out=evenodd(smv("ccst", 32), 1),
                                  in0=smv("xnorm"), scalar1=smc("mkb1"),
                                  scalar2=None, op0=OP.mult), (t, "lnl"))
        R.wait(e, (t, "ccb_back"))
        R.inc(SV, e.tensor_copy(out=smbv(512, 128), in_=sm[:, 80:208]),
              (t, "xna"))
        for k in range(NK):
            R.wait(e, (t, "xl2", k))
            if k >= 2:
                R.wait(e, (t, "xps", k - 2))
            e.tensor_tensor(out=P["xch"](k % 2), in0=P["xch"](k % 2),
                            in1=P["mu"], op=OP.subtract)
            e.tensor_tensor(out=P["gss"](k % 2), in0=P["xch"](k % 2),
                            in1=P["rs"], op=OP.mult)
            R.inc(SV, e.tensor_tensor(out=aA[:, k * 2048:(k + 1) * 2048],
                                      in0=P["xch"](k % 2), in1=P["rs"],
                                      op=OP.mult), (t, "apl", k))
        for item in P["seg"]:
            if item[0] == "gate":
                rc = item[1]
                for tq in range(NTQ):
                    g = rc * NTQ + tq
                    R.wait(e, (t, "gsg", rc, tq))
                    if rc >= 2 and tq == 0:
                        R.wait(e, (t, "gst", rc - 2))
                    R.inc(SV, e.tensor_scalar(
                        out=P["gss"](rc % 2)[:, tq * 512:tq * 512 + 512],
                        in0=P["mu"][:, (g % 2) * 512:(g % 2) * 512 + 512],
                        scalar1=0.5, scalar2=0.5, op0=OP.mult, op1=OP.add),
                        (t, "gsd", rc, tq))
                R.alias((t, "gse", rc), (t, "gsd", rc, 3))
            else:
                ffn_v_th(P, e, item[1])
        for rc in range(NK):
            R.wait(e, (t, "gl", rc))
            R.wait(e, (t, "xpl2", rc))
            if rc == 0:
                R.wait(e, (t, "bcr"))
            e.tensor_scalar(out=P["mu"], in0=P["xch3"](rc),
                            scalar1=smv("bcmy")[:, rc:rc + 1], scalar2=None,
                            op0=OP.mult)
            R.inc(SV, e.tensor_tensor(out=P["gss"](rc % 2), in0=P["mu"],
                                      in1=P["gss"](rc % 2), op=OP.add),
                  (t, "uc", rc))
            if rc >= 2:
                R.wait(e, (t, "cmm", rc - 2))
            R.inc(SV, e.tensor_copy(out=P["ub"](rc % 2), in_=P["gss"](rc % 2)),
                  (t, "ubc", rc))
        R.wait(e, (t, "cmm", NK - 1))
        R.wait(e, (t, "ust", NK - 2))
        R.wait(e, (t, "ust", NK - 1))
        for tq in range(NTQ):
            e.tensor_copy(out=P["gss"](0)[:, tq * 512:tq * 512 + 512],
                          in_=bank(tq))
        for tq in range(NTQ):
            src = bank(4 + tq) if tq < 3 else psBf[:]
            ins = e.tensor_copy(out=P["gss"](1)[:, tq * 512:tq * 512 + 512],
                                in_=src)
        R.inc(SV, ins, (t, "cev"))
        e.tensor_scalar(out=P["mu"], in0=P["gss"](0), scalar1=INV_H,
                        scalar2=None, op0=OP.mult)
        e.tensor_scalar(out=P["rs"], in0=P["gss"](1), scalar1=INV_H,
                        scalar2=None, op0=OP.mult)
        e.tensor_tensor(out=P["gss"](0), in0=P["mu"], in1=P["mu"], op=OP.mult)
        e.tensor_tensor(out=P["gss"](1), in0=P["rs"], in1=P["gss"](0),
                        op=OP.subtract)
        rsqrt_newton(e, P["rs"], P["gss"](1), P["gss"](0), P["dscr"],
                     P["xch"](0), EPS_FFN)
        R.inc(SV, e.nop(), (t, "cst"))
        for rc in range(NK):
            R.wait(e, (t, "ul3", rc))
            R.wait(e, (t, "xl3", rc))
            e.tensor_tensor(out=P["xch3"](rc), in0=P["xch3"](rc),
                            in1=P["mu"], op=OP.subtract)
            R.inc(SV, e.tensor_tensor(out=P["xch3"](rc), in0=P["xch3"](rc),
                                      in1=P["rs"], op=OP.mult), (t, "fbe", rc))
            e.tensor_scalar(out=P["dscr"], in0=P["xch3"](rc),
                            scalar1=smc("g2", rc), scalar2=smc("b2", rc),
                            op0=OP.mult, op1=OP.add)
            R.inc(SV, e.tensor_tensor(out=P["xch3"](rc), in0=P["dscr"],
                                      in1=P["gss"](rc % 2), op=OP.add),
                  (t, "fbo", rc))

    def th_ln(P, e, s):
        t = P["t"]
        tln = smv("tln", 16)
        p6 = psA[:, 6 * 512:6 * 512 + 64]
        e.tensor_reduce(out=tln[:, 0:2].rearrange("p (b o) -> p b o", o=1),
                        in_=p6[:, 0:32].rearrange("p (c b) -> p b c", b=2),
                        axis=mybir.AxisListType.X, op=OP.add)
        e.tensor_reduce(out=tln[:, 2:4].rearrange("p (b o) -> p b o", o=1),
                        in_=p6[:, 32:64].rearrange("p (c b) -> p b c", b=2),
                        axis=mybir.AxisListType.X, op=OP.add)
        e.tensor_scalar(out=tln[:, 8:10], in0=tln[:, 0:2], scalar1=INV_H,
                        scalar2=None, op0=OP.mult)
        e.tensor_scalar(out=tln[:, 10:12], in0=tln[:, 2:4], scalar1=INV_H,
                        scalar2=None, op0=OP.mult)
        e.tensor_tensor(out=tln[:, 4:6], in0=tln[:, 8:10], in1=tln[:, 8:10],
                        op=OP.mult)
        e.tensor_tensor(out=tln[:, 12:14], in0=tln[:, 10:12], in1=tln[:, 4:6],
                        op=OP.subtract)
        ws = P["rs"]
        e.memset(ws[:, 0:640], 1.0)
        e.tensor_copy(out=ws[:, 12:14], in_=tln[:, 12:14])
        rsqrt_newton(e, ws[:, 384:512], ws[:, 0:128], ws[:, 128:256],
                     ws[:, 256:384], ws[:, 512:640], EPS_FFN)
        e.tensor_copy(out=tln[:, 14:16], in_=ws[:, 384 + 12:384 + 14])
        for b in range(2):
            dst = evenodd(smv("ftmp", 32), b)
            e.tensor_scalar(out=dst, in0=evenodd(smv("thf", 32), b),
                            scalar1=tln[:, 8 + b:9 + b],
                            scalar2=tln[:, 14 + b:15 + b],
                            op0=OP.subtract, op1=OP.mult)
            e.tensor_tensor(out=dst, in0=dst, in1=smv("png"), op=OP.mult)
            e.tensor_tensor(out=dst, in0=dst, in1=smv("pnb"), op=OP.add)
        R.inc(SV, e.tensor_copy(out=smbv(128, 128), in_=sm[:, 272:400]),
              (t, "tnc", s))

    def attn_sync(e):
        t = "a"
        for k in range(NK):
            R.wait(e, ("f1", "sto", k))
            if k >= 2:
                R.wait(e, (t, "cast", k - 2))
            else:
                R.wait(e, ("f1", "sto", NK - 3))
                R.wait(e, ("f1", "sto", NK - 2))
                R.wait(e, ("f1", "sto", NK - 1))
            R.inc(S2("sxl")[k % 2], e.dma_start(
                out=cf((k % 2) * 2048), in_=x1f[k * 128:(k + 1) * 128, :]),
                (t, "xl", k), 16)
        for k in range(NK):
            if k < 2:
                R.wait(e, (t, "cast", NK - 2 + k))
                R.wait(e, (t, "stat"))
            elif k < 4:
                R.wait(e, (t, "stat"))
            else:
                R.wait(e, (t, "apl", k - 4))
            R.inc((S2("sxl2") + S2("sxl2b"))[k % 4], e.dma_start(
                out=cf((k % 4) * 2048), in_=x1f[k * 128:(k + 1) * 128, :]),
                (t, "xl2", k), 16)
        for rc in range(NK):
            R.wait(e, (t, "scn", rc))
            R.inc(S2("sssp")[rc % 2], e.dma_start(out=Ssp[rc], in_=cf(4096)),
                  (t, "ssp", rc), 16)
            R.inc(S2("spsp")[rc % 2], e.dma_start(out=Psp[rc], in_=cf(6144)),
                  (t, "psp", rc), 16)
        for hc in range(NK):
            R.wait(e, (t, "lev", hc))
            R.inc(S2("slst")[hc % 2], e.dma_start(
                out=loc_sp[hc], in_=cf(12288 + (hc % 2) * 2048)),
                (t, "lst", hc), 16)
        for rc in range(NK):
            if rc < 2:
                R.wait(e, (t, "scn", NK - 1))
                R.wait(e, (t, "lst", NK - 2))
                R.wait(e, (t, "lst", NK - 1))
            else:
                R.wait(e, (t, "fix", rc - 2))
                R.wait(e, (t, "m1", rc - 2, NTQ - 1))
            R.inc(S2("spfs")[rc % 2], e.dma_start(
                out=cf(8192 + (rc % 2) * 2048), in_=Ssp[rc]),
                (t, "pfs", rc), 16)
            R.inc(S2("spfp")[rc % 2], e.dma_start(
                out=cf(12288 + (rc % 2) * 2048), in_=Psp[rc]),
                (t, "pfp", rc), 16)
        for hc in range(NK):
            if hc < 2:
                R.wait(e, (t, "ro", NK - 1, 3))
            else:
                R.wait(e, (t, "oub", hc - 2))
                R.wait(e, (t, "sto", hc - 2))
            R.inc(S2("slrl")[hc % 2], e.dma_start(
                out=cf(8192 + (hc % 2) * 2048), in_=loc_sp[hc]),
                (t, "lrl", hc), 16)
            R.inc(S2("sxr")[hc % 2], e.dma_start(
                out=cf(12288 + (hc % 2) * 2048),
                in_=x1f[hc * 128:(hc + 1) * 128, :]), (t, "xr", hc), 16)
            R.wait(e, (t, "oub", hc))
            R.inc(S2("ssto")[hc % 2], e.dma_start(
                out=x2f[hc * 128:(hc + 1) * 128, :],
                in_=cf(8192 + (hc % 2) * 2048)), (t, "sto", hc), 16)

    def attn_gpsimd(e):
        t = "a"
        WJ = [W["w_f"], W["w_i"], W["w_v"]]
        for lc in range(4):
            for j, w in enumerate([W["w_up"], W["w_gt"]]):
                if lc == 0 and j == 0:
                    R.wait(e, ("f1", "gmm", NK - 1, 3))
                key = (t, "b1p", lc % 2) if (j == 1 and lc >= 2) else None
                R.inc(S2("gb1l")[lc % 2], e.dma_start(
                    out=aB[:, 16384 + (lc * 2 + j) * 2048:
                           16384 + (lc * 2 + j) * 2048 + 2048],
                    in_=wsl_src(w, lc)), key, 16)
        for rc in range(NK):
            for j in range(3):
                if rc >= 2:
                    R.wait(e, (t, "mm", j, (rc - 2) * 4 + 3))
                else:
                    R.wait(e, ("f1", "gmm", NK - 1, 3))
                key = (t, "wl", rc) if j == 2 else None
                R.inc(S2("gwl")[rc % 2], e.dma_start(
                    out=aB[:, ((rc % 2) * 3 + j) * 2048:((rc % 2) * 3 + j) * 2048 + 2048],
                    in_=wsl_src(WJ[j], rc)), key, 16)
        R.wait(e, (t, "ab"))
        R.inc(S1("gcc"), e.dma_start(out=cc_gi, in_=smv("ABs", 32)),
              (t, "ags"), 16)
        R.wait(e, (t, "ags"))
        cc_run(e, "ag", "AllGather", OP.bypass, cc_gi, cc_go)
        R.wait(e, (t, "psp", NK - 2))
        R.wait(e, (t, "psp", NK - 1))
        R.inc(S1("gcc"), e.dma_start(
            out=cfw(6144, 256).rearrange("p (r c) -> p r c", c=32),
            in_=cc_go.rearrange("(r p) c -> p r c", p=128)), (t, "agb"), 16)
        for lc in range(4, NK):
            for j, w in enumerate([W["w_up"], W["w_gt"]]):
                if lc >= 6:
                    R.wait(e, (t, "hh", (lc - 2) * 4 + 3))
                else:
                    R.wait(e, (t, "scn", NK - 1))
                    R.wait(e, (t, "b1p", lc % 2))
                key = (t, "b1l", lc) if j == 1 else None
                R.inc(S2("gb1l")[lc % 2], e.dma_start(
                    out=cbw(((lc % 2) * 2 + j) * 2048, 2048),
                    in_=wsl_src(w, lc)), key, 16)
        for hc in range(NK):
            if hc >= 2:
                R.wait(e, (t, "dmm", hc - 2, 3))
            else:
                R.wait(e, (t, "gm2", 14 + hc, 3))
            R.inc(S2("gb2l")[hc % 2], e.dma_start(
                out=cbw((hc % 2) * 2048, 2048),
                in_=wsl_src(W["w_dn"], hc)), (t, "b2l", hc), 16)
        for rc in range(NK):
            if rc >= 2:
                R.wait(e, (t, "qmm", rc - 2, 3))
            else:
                R.wait(e, (t, "gm2", NK - 1, 3))
            R.inc(S2("gql")[rc % 2], e.dma_start(
                out=cbw(4096 + (rc % 2) * 2048, 2048),
                in_=wsl_src(W["w_q"], rc)), (t, "ql", rc), 16)
        for hc in range(NK):
            if hc >= 2:
                R.wait(e, (t, "rmm", hc - 2, 3))
            else:
                R.wait(e, (t, "ro", NK - 1, 3))
            R.inc(S2("grol")[hc % 2], e.dma_start(
                out=cbw((hc % 2) * 2048, 2048),
                in_=wsl_src(W["w_ro"], hc)), (t, "rol", hc), 16)

    def attn_tensor(e):
        t = "a"
        R.wait(e, ("f1", "cev"))
        for k in range(NK):
            R.wait(e, (t, "cast", k))
            for tq in range(NTQ):
                ins = e.matmul(bank(tq), onesbt[:],
                               aA[:, k * 2048 + tq * 512:k * 2048 + tq * 512 + 512],
                               start=(k == 0), stop=(k == NK - 1))
            if k == NK - 1:
                R.inc(SP, ins, (t, "smm", 0))
        R.wait(e, (t, "evs", 0))
        for k in range(NK):
            R.wait(e, (t, "sq", k))
            for tq in range(NTQ):
                ins = e.matmul(bank(tq), onesbt[:],
                               cbw(8192 + (k % 2) * 2048, 2048)[:, tq * 512:tq * 512 + 512],
                               start=(k == 0), stop=(k == NK - 1))
            R.inc(SP, ins, (t, "s2", k))
        R.alias((t, "smm", 1), (t, "s2", NK - 1))
        for rc in range(NK):
            R.wait(e, (t, "wl", rc))
            if rc == 0:
                R.wait(e, (t, "apl", NK - 1))
            for tq in range(NTQ):
                g = rc * 4 + tq
                for j in range(3):
                    if g >= 2:
                        R.wait(e, [(t, "evf", g - 2), (t, "evi", g - 2),
                                   (t, "evv", g - 2)][j])
                    for k in range(NK):
                        ins = e.matmul(
                            psA[:, (j * 2 + g % 2) * 512:(j * 2 + g % 2) * 512 + 512],
                            aB[:, ((rc % 2) * 3 + j) * 2048 + k * 128:
                               ((rc % 2) * 3 + j) * 2048 + k * 128 + 128],
                            aA[:, k * 2048 + tq * 512:k * 2048 + tq * 512 + 512],
                            start=(k == 0), stop=(k == NK - 1))
                    R.inc(SP, ins, (t, "mm", j, g))
        for lc in range(NK):
            R.wait(e, (t, "b1p", lc % 2) if lc < 4 else (t, "b1l", lc))
            for tq in range(NTQ):
                g = lc * 4 + tq
                for j in range(2):
                    if g < 2:
                        R.wait(e, (t, "evf" if j == 0 else "evi",
                                   4 * NK - 2 + g))
                    if g >= 2:
                        R.wait(e, (t, "hh", g - 2))
                    for k in range(NK):
                        lhsT = (aB[:, 16384 + (lc * 2 + j) * 2048 + k * 128:
                                   16384 + (lc * 2 + j) * 2048 + k * 128 + 128]
                                if lc < 4 else
                                cbw(((lc % 2) * 2 + j) * 2048 + k * 128, 128))
                        ins = e.matmul(
                            psA[:, (j * 2 + g % 2) * 512:(j * 2 + g % 2) * 512 + 512],
                            lhsT,
                            aA[:, k * 2048 + tq * 512:k * 2048 + tq * 512 + 512],
                            start=(k == 0), stop=(k == NK - 1))
                    R.inc(SP, ins, [(t, "umm", lc, tq), (t, "gm2", lc, tq)][j])
        for hc in range(NK):
            R.wait(e, (t, "b2l", hc))
            for tq in range(NTQ):
                g = hc * 4 + tq
                if g < 2:
                    R.wait(e, (t, "hh", 4 * NK - 2 + g))
                if g >= 2:
                    R.wait(e, (t, "lcp", g - 2))
                for k in range(NK):
                    ins = e.matmul(
                        bank(g % 2),
                        cbw((hc % 2) * 2048 + k * 128, 128),
                        aB[:, k * 2048 + tq * 512:k * 2048 + tq * 512 + 512],
                        start=(k == 0), stop=(k == NK - 1))
                R.inc(SP, ins, (t, "dmm", hc, tq))
        for rc in range(NK):
            R.wait(e, (t, "ql", rc))
            for tq in range(NTQ):
                g = rc * 4 + tq
                if g < 2:
                    R.wait(e, (t, "lcp", 4 * NK - 2 + g))
                if g >= 2:
                    R.wait(e, (t, "ro", (g - 2) // 4, (g - 2) % 4))
                for k in range(NK):
                    ins = e.matmul(
                        bank(g % 2),
                        cbw(4096 + (rc % 2) * 2048 + k * 128, 128),
                        aA[:, k * 2048 + tq * 512:k * 2048 + tq * 512 + 512],
                        start=(k == 0), stop=(k == NK - 1))
                R.inc(SP, ins, (t, "qmm", rc, tq))
        for hc in range(NK):
            R.wait(e, (t, "rol", hc))
            for tq in range(NTQ):
                g = hc * 4 + tq
                if g < 2:
                    R.wait(e, (t, "ro", NK - 1, 2 + g))
                if g >= 2:
                    R.wait(e, (t, "oo", g - 2))
                for k in range(NK):
                    ins = e.matmul(
                        bank(g % 2),
                        cbw((hc % 2) * 2048 + k * 128, 128),
                        aB[:, k * 2048 + tq * 512:k * 2048 + tq * 512 + 512],
                        start=(k == 0), stop=(k == NK - 1))
                R.inc(SP, ins, (t, "rmm", hc, tq))

    def attn_scalar(e):
        t = "a"
        for k in range(NK):
            R.wait(e, (t, "cast", k))
            if k >= 2:
                R.wait(e, (t, "s2", k - 2))
            R.inc(SA, e.activation(out=cbw(8192 + (k % 2) * 2048, 2048),
                                   in_=aA[:, k * 2048:(k + 1) * 2048],
                                   func=AF.Square), (t, "sq", k))
        for rc in range(NK):
            for tq in range(NTQ):
                g = rc * 4 + tq
                R.wait(e, (t, "mm", 0, g))
                if g == 0:
                    R.wait(e, (t, "apl", NK - 1))
                R.inc(SA, e.activation(
                    out=cf((rc % 2) * 2048)[:, tq * 512:tq * 512 + 512],
                    in_=psA[:, (g % 2) * 512:(g % 2) * 512 + 512],
                    func=AF.Tanh), (t, "evf", g))
                R.wait(e, (t, "mm", 1, g))
                if g >= 2:
                    R.wait(e, (t, "evv", g - 2))
                R.inc(SA, e.activation(
                    out=cfw(9728 + (g % 2) * 512, 512),
                    in_=psA[:, (2 + g % 2) * 512:(2 + g % 2) * 512 + 512],
                    func=AF.Sigmoid), (t, "evi", g))
                R.wait(e, (t, "mm", 2, g))
                R.inc(SA, e.activation(
                    out=cfw(8192 + (g % 2) * 512, 512),
                    in_=psA[:, (4 + g % 2) * 512:(4 + g % 2) * 512 + 512],
                    func=AF.Sigmoid), (t, "asv", g))
        for lc in range(NK):
            for tq in range(NTQ):
                g = lc * 4 + tq
                R.wait(e, (t, "gm2", lc, tq))
                if g == 0:
                    R.wait(e, (t, "evv", 4 * NK - 1))
                if g >= 1:
                    R.wait(e, (t, "hh", g - 1))
                R.inc(SA, e.activation(out=cfw(8192, 512),
                                       in_=psA[:, (2 + g % 2) * 512:(2 + g % 2) * 512 + 512],
                                       func=AF.Sigmoid), (t, "b1s", lc, tq))
        for hc in range(NK):
            for tq in range(NTQ):
                g = hc * 4 + tq
                R.wait(e, (t, "dmm", hc, tq))
                if g == 0:
                    R.wait(e, (t, "scn", NK - 1))
                if hc >= 2 and tq == 0:
                    R.wait(e, (t, "lst", hc - 2))
                R.inc(SA, e.copy(
                    out=cf(12288 + (hc % 2) * 2048)[:, tq * 512:tq * 512 + 512],
                    in_=bank(g % 2)), (t, "lcp", g))
            R.alias((t, "lev", hc), (t, "lcp", hc * 4 + 3))
        for rc in range(NK):
            for tq in range(NTQ):
                R.wait(e, (t, "m1", rc, tq))
                R.inc(SA, e.activation(out=cfw(512, 512), in_=cfw(0, 512),
                                       func=AF.Sigmoid), (t, "sg", rc, tq))

    def attn_vector(e):
        t = "a"
        R.wait(e, ("f1", "fbo", NK - 1))
        amu, ars = cf(8192), cf(10240)
        for k in range(NK):
            R.wait(e, (t, "xl", k))
            R.inc(SV, e.tensor_copy(out=aA[:, k * 2048:(k + 1) * 2048],
                                    in_=cf((k % 2) * 2048)), (t, "cast", k))
        R.wait(e, (t, "smm", 0))
        R.wait(e, ("f1", "sto", NK - 3))
        R.wait(e, ("f1", "sto", NK - 2))
        R.wait(e, ("f1", "sto", NK - 1))
        for tq in range(NTQ):
            ins = e.tensor_copy(out=cf(12288)[:, tq * 512:tq * 512 + 512],
                                in_=bank(tq))
        R.inc(SV, ins, (t, "evs", 0))
        R.wait(e, (t, "smm", 1))
        for tq in range(NTQ):
            ins = e.tensor_copy(out=cf(14336)[:, tq * 512:tq * 512 + 512],
                                in_=bank(tq))
        R.inc(SV, ins, (t, "evs", 1))
        e.tensor_scalar(out=amu, in0=cf(12288), scalar1=INV_H, scalar2=None,
                        op0=OP.mult)
        e.tensor_scalar(out=ars, in0=cf(14336), scalar1=INV_H, scalar2=None,
                        op0=OP.mult)
        e.tensor_tensor(out=cf(12288), in0=amu, in1=amu, op=OP.mult)
        e.tensor_tensor(out=cf(14336), in0=ars, in1=cf(12288), op=OP.subtract)
        rsqrt_newton(e, ars, cf(14336), cf(12288), cf(4096), cf(6144), EPS_PRE)
        R.inc(SV, e.nop(), (t, "stat"))
        for k in range(NK):
            R.wait(e, (t, "xl2", k))
            e.tensor_tensor(out=cf((k % 4) * 2048), in0=cf((k % 4) * 2048),
                            in1=amu, op=OP.subtract)
            R.inc(SV, e.tensor_tensor(out=aA[:, k * 2048:(k + 1) * 2048],
                                      in0=cf((k % 4) * 2048), in1=ars,
                                      op=OP.mult), (t, "apl", k))
        R.wait(e, ("f1", "smdump"))
        for rc in range(NK):
            for tq in range(NTQ):
                g = rc * 4 + tq
                R.wait(e, (t, "asv", g))
                e.tensor_tensor(out=cfw(9216, 512),
                                in0=psA[:, (4 + g % 2) * 512:(4 + g % 2) * 512 + 512],
                                in1=cfw(8192 + (g % 2) * 512, 512), op=OP.mult)
                R.wait(e, (t, "evi", g))
                R.inc(SV, e.tensor_tensor(
                    out=cfw(12288 + (rc % 2) * 2048 + tq * 512, 512),
                    in0=cfw(9728 + (g % 2) * 512, 512),
                    in1=cfw(9216, 512), op=OP.mult), (t, "evv", g))
            R.wait(e, (t, "evf", rc * 4 + 3))
            if rc >= 1:
                R.wait(e, (t, "ssp", rc - 1))
                R.wait(e, (t, "psp", rc - 1))
            e.tensor_tensor_scan(out=cf(4096), data0=cf((rc % 2) * 2048),
                                 data1=cf(12288 + (rc % 2) * 2048),
                                 initial=0.0, op0=OP.mult, op1=OP.add)
            e.tensor_copy(out=smc("ABs", 16 + rc), in_=cf(4096)[:, 2047:2048])
            e.tensor_tensor_scan(out=cf(6144), data0=cf((rc % 2) * 2048),
                                 data1=cf((rc % 2) * 2048),
                                 initial=1.0, op0=OP.mult, op1=OP.bypass)
            R.inc(SV, e.tensor_copy(out=smc("ABs", rc),
                                    in_=cf(6144)[:, 2047:2048]), (t, "scn", rc))
        R.inc(SV, e.nop(), (t, "ab"))
        R.wait(e, (t, "agb"))
        ag = cfw(6144, 256).rearrange("p (r c) -> p r c", c=32)
        e.tensor_copy(out=smv("s_in"), in_=smv("initf"))
        for j in range(NCORE):
            e.tensor_scalar(out=smv("ftmp"), in0=ag[:, j, 0:16],
                            scalar1=smc("predm", j), scalar2=smc("predm1", j),
                            op0=OP.mult, op1=OP.add)
            e.tensor_tensor(out=smv("s_in"), in0=smv("s_in"), in1=smv("ftmp"),
                            op=OP.mult)
            e.tensor_scalar(out=smv("ftmp2"), in0=ag[:, j, 16:32],
                            scalar1=smc("predm", j), scalar2=None, op0=OP.mult)
            e.tensor_tensor(out=smv("s_in"), in0=smv("s_in"), in1=smv("ftmp2"),
                            op=OP.add)
        R.inc(SV, e.nop(), (t, "sin"))
        for lc in range(NK):
            for tq in range(NTQ):
                g = lc * 4 + tq
                R.wait(e, (t, "b1s", lc, tq))
                e.tensor_tensor(out=cfw(8704, 512),
                                in0=psA[:, (2 + g % 2) * 512:(2 + g % 2) * 512 + 512],
                                in1=cfw(8192, 512), op=OP.mult)
                R.wait(e, (t, "umm", lc, tq))
                R.inc(SV, e.tensor_tensor(
                    out=aB[:, lc * 2048 + tq * 512:lc * 2048 + tq * 512 + 512],
                    in0=psA[:, (g % 2) * 512:(g % 2) * 512 + 512],
                    in1=cfw(8704, 512), op=OP.mult), (t, "hh", g))
        for rc in range(NK):
            R.wait(e, (t, "pfs", rc))
            R.wait(e, (t, "pfp", rc))
            if rc == 0:
                R.wait(e, (t, "sin"))
                R.wait(e, (t, "ssp", NK - 1))
            e.tensor_scalar(out=cf(4096), in0=cf(12288 + (rc % 2) * 2048),
                            scalar1=smc("s_in", rc), scalar2=None, op0=OP.mult)
            R.inc(SV, e.tensor_tensor(out=cf(8192 + (rc % 2) * 2048),
                                      in0=cf(4096),
                                      in1=cf(8192 + (rc % 2) * 2048),
                                      op=OP.add), (t, "fix", rc))
            for tq in range(NTQ):
                g = rc * 4 + tq
                R.wait(e, (t, "qmm", rc, tq))
                R.inc(SV, e.tensor_tensor(
                    out=cfw(0, 512), in0=bank(g % 2),
                    in1=cf(8192 + (rc % 2) * 2048)[:, tq * 512:tq * 512 + 512],
                    op=OP.mult), (t, "m1", rc, tq))
                R.wait(e, (t, "sg", rc, tq))
                R.inc(SV, e.tensor_tensor(
                    out=aB[:, rc * 2048 + tq * 512:rc * 2048 + tq * 512 + 512],
                    in0=cfw(0, 512), in1=cfw(512, 512), op=OP.mult),
                    (t, "ro", rc, tq))
        for hc in range(NK):
            R.wait(e, (t, "lrl", hc))
            for tq in range(NTQ):
                g = hc * 4 + tq
                R.wait(e, (t, "rmm", hc, tq))
                R.inc(SV, e.tensor_tensor(
                    out=cf(8192 + (hc % 2) * 2048)[:, tq * 512:tq * 512 + 512],
                    in0=bank(g % 2),
                    in1=cf(8192 + (hc % 2) * 2048)[:, tq * 512:tq * 512 + 512],
                    op=OP.add), (t, "oo", g))
            R.wait(e, (t, "xr", hc))
            R.inc(SV, e.tensor_tensor(out=cf(8192 + (hc % 2) * 2048),
                                      in0=cf(8192 + (hc % 2) * 2048),
                                      in1=cf(12288 + (hc % 2) * 2048),
                                      op=OP.add), (t, "oub", hc))
        R.alias(("a", "fbo", NK - 1), ("a", "oub", NK - 1))

    def tout_sync(e, Xfin, fin_evt):
        for tc in range(NK):
            for ev in fin_evt:
                R.wait(e, ev)
            if tc >= 4:
                R.wait(e, ("to", "pe", tc - 4, 3))
            src = Xfin[:, tc * 128:(tc + 1) * 128].rearrange(
                "(fc p) m -> p fc m", p=128)
            dst = cf((tc % 4) * 2048).rearrange("p (fc m) -> p fc m", m=128)
            R.inc((S2("sld") + S2("sldb"))[tc % 4], e.dma_start(out=dst, in_=src),
                  ("to", "ld", tc), 16)
            R.wait(e, ("to", "ev", tc, 3))
            R.inc(S2("sst")[tc % 2], e.dma_start(
                out=out_o[tc * 128:(tc + 1) * 128, :],
                in_=cf(8192 + (tc % 2) * 2048)), ("to", "st", tc), 16)
        R.wait(e, ("to", "st", NK - 2))
        R.wait(e, ("to", "st", NK - 1))

    def tout_tensor(e, bank_evt):
        R.wait(e, bank_evt)
        for tc in range(NK):
            R.wait(e, ("to", "ld", tc))
            for fc in range(NK):
                g4 = fc // 4
                gidx = tc * 4 + g4
                if gidx >= 2:
                    R.wait(e, ("to", "ev", (gidx - 2) // 4, (gidx - 2) % 4))
                ins = e.transpose(
                    bank(g4 % 2)[:, (fc % 4) * 128:(fc % 4) * 128 + 128],
                    cf((tc % 4) * 2048)[:, fc * 128:fc * 128 + 128],
                    ident[:])
                if fc % 4 == 3:
                    R.inc(SP, ins, ("to", "pe", tc, g4))

    def tout_vector(e):
        for tc in range(NK):
            for g4 in range(4):
                R.wait(e, ("to", "pe", tc, g4))
                if tc >= 2 and g4 == 0:
                    R.wait(e, ("to", "st", tc - 2))
                R.inc(SV, e.tensor_copy(
                    out=cf(8192 + (tc % 2) * 2048)[:, g4 * 512:g4 * 512 + 512],
                    in_=bank(g4 % 2)), ("to", "ev", tc, g4))

    f1 = make_ffn("f1", xf0, x1f, in_evt=lambda k: ("ti", "st", k),
                  bank_evt=("ti", "ev", NK - 1, 3), gw_evt=("f1", "gw"),
                  arena_evt=("ti", "pc"))
    f1["scr_evt"] = [("ti", "st", NK - 2), ("ti", "st", NK - 1)]
    f2 = make_ffn("f2", x2f, x3f, in_evt=lambda k: ("a", "sto", k),
                  bank_evt=("a", "oub", NK - 1), gw_evt=None,
                  arena_evt=("a", "oub", NK - 1))
    f2["gwl_evt"] = ("a", "rmm", NK - 1, 3)
    f1["gw_key"] = ("f1", "gw")

    if stage == "id":
        tout_x, tout_fin, tout_bank = xf0, \
            [("ti", "st", NK - 2), ("ti", "st", NK - 1)], ("ti", "ev", NK - 1, 3)
    elif stage == "f1":
        tout_x, tout_fin, tout_bank = x1f, \
            [("f1", "sto", NK - 3), ("f1", "sto", NK - 2),
             ("f1", "sto", NK - 1)], ("f1", "cev")
    elif stage == "f1a":
        tout_x, tout_fin, tout_bank = x2f, \
            [("a", "sto", NK - 2), ("a", "sto", NK - 1)], ("a", "oub", NK - 1)
    else:
        tout_x, tout_fin, tout_bank = x3f, \
            [("f2", "sto", NK - 3), ("f2", "sto", NK - 2),
             ("f2", "sto", NK - 1)], ("f2", "cev")

    def emit_sync(e):
        tin_sync(e)
        if stage != "id":
            ffn_sync(f1, e)
        if stage in ("f1a", "full"):
            attn_sync(e)
        if stage == "full":
            ffn_sync(f2, e)
        tout_sync(e, tout_x, tout_fin)

    def emit_gpsimd(e):
        tin_gpsimd(e)
        if stage != "id":
            ffn_gpsimd(f1, e)
        if stage in ("f1a", "full"):
            attn_gpsimd(e)
        if stage == "full":
            ffn_gpsimd(f2, e)

    def emit_tensor(e):
        tin_tensor(e)
        if stage != "id":
            ffn_tensor(f1, e)
        if stage in ("f1a", "full"):
            attn_tensor(e)
        if stage == "full":
            ffn_tensor(f2, e)
        tout_tensor(e, tout_bank)

    def emit_scalar(e):
        e = Ser(e, SA, R)
        tin_scalar(e)
        if stage != "id":
            ffn_scalar(f1, e)
        if stage in ("f1a", "full"):
            attn_scalar(e)
        if stage == "full":
            ffn_scalar(f2, e)

    def emit_vector(e):
        e = Ser(e, SV, R)
        tin_vector(e)
        if stage != "id":
            ffn_vector(f1, e)
        if stage in ("f1a", "full"):
            attn_vector(e)
        if stage == "full":
            ffn_vector(f2, e)
        tout_vector(e)

    emits = [emit_sync, emit_gpsimd, emit_tensor, emit_scalar, emit_vector]

    R.plan = True
    for fn in emits:
        fn(DummyEng())
    R.check()
    plan_counts = [(s, s.n) for s in allsems]
    for s in allsems:
        s.n = 0

    R.plan = False
    with ExitStack() as sctx:
        for s in allsems:
            s.h = sctx.enter_context(nc.semaphore(s.name))
        with nc.Block() as block:
            block.sync(emit_sync)
            block.gpsimd(emit_gpsimd)
            block.tensor(emit_tensor)
            block.scalar(emit_scalar)
            block.vector(emit_vector)
        for s, n0 in plan_counts:
            assert s.n == n0, f"pass drift on {s.name}: {s.n} != {n0}"
        ctx.close()
    return nc


_NC_CACHE = {}


def make_inmaps(inputs):
    g = lambda n: np.asarray(inputs[n], dtype=np.float32)
    x = g("L0_attn_x")
    import ml_dtypes
    bf = ml_dtypes.bfloat16

    def fmv16(v):
        return np.ascontiguousarray(v.reshape(16, 128).T)

    def fmv32(v):
        base = v.reshape(16, 128).T
        return np.ascontiguousarray(np.repeat(base, 2, axis=1))

    def aperm(v):
        return np.ascontiguousarray(
            v.reshape(16, 128, 16, 128).transpose(2, 1, 0, 3).reshape(2048, 2048))

    full = {
        "w_f": aperm(g("L0_attn_w_f")), "w_i": aperm(g("L0_attn_w_i")),
        "w_v": aperm(g("L0_attn_w_v")), "w_q": aperm(g("L0_attn_w_q")),
        "w_ro": aperm(g("L0_attn_w_ro")), "w_up": aperm(g("L0_attn_w_up")),
        "w_gt": aperm(g("L0_attn_w_gate")), "w_dn": aperm(g("L0_attn_w_down")),
        "agg_w": g("L0_ffn_agg_w"), "bc_w": g("L0_ffn_bc_w"),
        "gate_w": g("L0_ffn_gate_w"),
        "ingam": fmv16(g("L0_ffn_in_gamma")), "inbet": fmv16(g("L0_ffn_in_beta")),
        "outgam": fmv16(g("L0_ffn_out_gamma")),
        "outbet": fmv16(g("L0_ffn_out_beta")),
        "gateb": fmv16(g("L0_ffn_gate_b")),
        "initf": fmv16(g("L0_attn_init_state")),
        "aggb": fmv32(g("L0_ffn_agg_b")), "bcb": fmv32(g("L0_ffn_bc_b")),
        "d2b": fmv32(g("L0_ffn_d2_b")),
        "png": fmv16(g("L0_ffn_pn_gamma")), "pnb": fmv16(g("L0_ffn_pn_beta")),
        "idn32": np.eye(128, dtype=np.float32),
        "idnb": np.eye(128, dtype=np.float32).astype(bf),
        "onesb": np.ones((128, 128), dtype=np.float32).astype(bf),
    }
    d1w, d2w, d1b = g("L0_ffn_d1_w"), g("L0_ffn_d2_w"), g("L0_ffn_d1_b")

    in_maps = []
    for c in range(NCORE):
        b, s = c // 4, c % 4
        m = dict(full)
        m["xs"] = np.ascontiguousarray(x[b, s * T:(s + 1) * T, :])
        m["d1s"] = np.ascontiguousarray(d1w[:, c * 1024:(c + 1) * 1024])
        m["d2s"] = np.ascontiguousarray(d2w[c * 1024:(c + 1) * 1024, :])
        m["d1bs"] = np.ascontiguousarray(
            np.tile(d1b[c * 1024:(c + 1) * 1024], (2, 1)))
        own = 1.0 if s == 3 else 0.0
        m["mkb0"] = np.full((128, 1), own if b == 0 else 0.0, np.float32)
        m["mkb1"] = np.full((128, 1), own if b == 1 else 0.0, np.float32)
        m["mb0"] = np.full((128, 1), 1.0 if b == 0 else 0.0, np.float32)
        m["mb1"] = np.full((128, 1), 1.0 if b == 1 else 0.0, np.float32)
        pm = np.zeros((128, 8), np.float32)
        for j in range(NCORE):
            if j // 4 == b and j % 4 < s:
                pm[:, j] = 1.0
        m["predm"] = pm
        m["predm1"] = 1.0 - pm
        in_maps.append(m)
    return in_maps


def kernel(**inputs):
    in_maps = make_inmaps(inputs)
    nc = build_nc(STAGE)
    res = run_bass_kernel_spmd(nc, in_maps, core_ids=list(range(NCORE)))
    out = np.empty((2, 8192, 2048), np.float32)
    for c in range(NCORE):
        b, s = c // 4, c % 4
        out[b, s * T:(s + 1) * T, :] = res.results[c]["o"]
    return out


if __name__ == "__main__":
    nc = build_nc(STAGE)
    print("build ok")


# version 07
# speedup vs baseline: 1.0055x; 1.0055x over previous
import numpy as np
from contextlib import ExitStack

import concourse.bass as bass
import concourse.mybir as mybir
from concourse.bass_utils import run_bass_kernel_spmd

F32 = mybir.dt.float32
BF16 = mybir.dt.bfloat16
I32 = mybir.dt.int32
AF = mybir.ActivationFunctionType
OP = mybir.AluOpType

NCORE = 8
T = 2048
NK = 16
NTQ = 4
EPS_PRE = 1e-5
EPS_FFN = 1e-6
RG = [[0, 1, 2, 3, 4, 5, 6, 7]]
MAGIC = 0x5F3759DF + 1
INV_H = 1.0 / 2048.0

STAGE = "full"
DUMPS = []

TH_SEG = {1: "s0", 4: "s1", 7: "s2", 10: "s3", 13: "bc"}


class Sem:
    def __init__(self, name):
        self.name = name
        self.h = None
        self.n = 0


class DummyIns:
    def then_inc(self, *a, **k):
        return self


class DummyEng:
    def __getattr__(self, name):
        return lambda *a, **k: DummyIns()


class Rec:
    def __init__(self):
        self.ev = {}
        self.needed = set()
        self.plan = True

    serialized = ()

    def inc(self, sem, ins, key=None, v=1):
        if sem in self.serialized:
            if key is not None:
                if self.plan:
                    assert key not in self.ev, f"dup event {key}"
                    self.ev[key] = (sem, sem.n)
                else:
                    assert self.ev[key] == (sem, sem.n), f"replay drift at {key}"
            return ins
        sem.n += v
        if self.plan:
            if key is not None:
                assert key not in self.ev, f"dup event {key}"
                self.ev[key] = (sem, sem.n)
        else:
            ins.then_inc(sem.h, v)
            if key is not None:
                assert self.ev[key] == (sem, sem.n), f"replay drift at {key}"
        return ins

    def alias(self, key, src):
        if self.plan:
            assert key not in self.ev, key
            self.ev[key] = self.ev[src]

    def wait(self, e, key):
        if self.plan:
            self.needed.add(key)
        else:
            sem, val = self.ev[key]
            e.wait_ge(sem.h, val)

    def check(self):
        missing = self.needed - set(self.ev)
        assert not missing, \
            f"waits on unrecorded events: {sorted(map(str, missing))[:10]}"


def rsqrt_newton(e, out_f, src, scrA, scrB, scrC, eps, n_iter=3):
    e.tensor_scalar(out=scrB, in0=src, scalar1=eps, scalar2=-0.5,
                    op0=OP.add, op1=OP.mult)
    e.tensor_scalar(out=scrA, in0=scrB, scalar1=-2.0, scalar2=None,
                    op0=OP.mult)
    ai = scrA.bitcast(I32)
    oi = out_f.bitcast(I32)
    e.tensor_scalar(out=oi, in0=ai, scalar1=1, scalar2=None,
                    op0=OP.logical_shift_right)
    e.tensor_scalar(out=ai, in0=oi, scalar1=-1, scalar2=None,
                    op0=OP.bitwise_xor)
    e.tensor_scalar(out=oi, in0=ai, scalar1=MAGIC, scalar2=None, op0=OP.add)
    ins = None
    for _ in range(n_iter):
        e.tensor_tensor(out=scrA, in0=out_f, in1=out_f, op=OP.mult)
        e.tensor_tensor(out=scrA, in0=scrA, in1=scrB, op=OP.mult)
        e.tensor_scalar(out=scrC, in0=scrA, scalar1=1.5, scalar2=None, op0=OP.add)
        ins = e.tensor_tensor(out=out_f, in0=out_f, in1=scrC, op=OP.mult)
    return ins


SERIAL_OPS = {
    "tensor_copy", "tensor_tensor", "tensor_scalar", "tensor_single_scalar",
    "tensor_add", "tensor_sub", "tensor_mul", "tensor_max", "tensor_relu",
    "scalar_tensor_tensor", "tensor_tensor_scan", "tensor_reduce",
    "reduce_max", "reduce_sum", "memset", "memzero", "bn_stats", "bn_aggr",
    "reciprocal", "activation", "copy", "square", "sqrt", "mul", "add",
    "stream_shuffle", "select", "transpose",
}


class Ser:

    def __init__(self, e, sem, R):
        self._e, self._sem, self._R = e, sem, R

    def __getattr__(self, name):
        f = getattr(self._e, name)
        if name not in SERIAL_OPS:
            return f
        def g(*a, **k):
            if not self._R.plan:
                self._e.wait_ge(self._sem.h, self._sem.n)
            ins = f(*a, **k)
            self._sem.n += 1
            if not self._R.plan:
                ins.then_inc(self._sem.h, 1)
            return ins
        return g


def build_nc(stage):
    nc = bass.Bass("TRN2", target_bir_lowering=False)
    dtr = nc.dram_tensor
    ein, eout = "ExternalInput", "ExternalOutput"

    xs = dtr("xs", [T, 2048], F32, kind=ein).ap()
    W = {n: dtr(n, [2048, 2048], F32, kind=ein).ap()
         for n in ["w_f", "w_i", "w_v", "w_q", "w_ro", "w_up", "w_gt", "w_dn",
                   "agg_w", "bc_w", "gate_w"]}
    d1s = dtr("d1s", [2048, 1024], F32, kind=ein).ap()
    d2s = dtr("d2s", [1024, 2048], F32, kind=ein).ap()
    d1bs = dtr("d1bs", [2, 1024], F32, kind=ein).ap()
    FMW = dict(ingam=16, inbet=16, outgam=16, outbet=16, gateb=16, initf=16,
               aggb=32, bcb=32, d2b=32, png=16, pnb=16)
    fmv = {n: dtr(n, [128, w], F32, kind=ein).ap() for n, w in FMW.items()}
    msk = {n: dtr(n, [128, 1], F32, kind=ein).ap()
           for n in ["mkb0", "mkb1", "mb0", "mb1"]}
    predm = dtr("predm", [128, 8], F32, kind=ein).ap()
    predm1 = dtr("predm1", [128, 8], F32, kind=ein).ap()
    idn32 = dtr("idn32", [128, 128], F32, kind=ein).ap()
    idnb = dtr("idnb", [128, 128], BF16, kind=ein).ap()
    onesb = dtr("onesb", [128, 128], BF16, kind=ein).ap()
    out_o = dtr("o", [T, 2048], F32, kind=eout).ap()
    dump_sm = {t: dtr(f"dump_sm_{t}", [128, 592], F32, kind=eout).ap()
               for t in ("f1", "f2", "a")}
    dump_smb = dtr("dump_smb", [128, 768], BF16, kind=eout).ap()

    def idtr(name, shape, dt=F32):
        kind = "ExternalOutput" if name in DUMPS else "Internal"
        return dtr(name, shape, dt, kind=kind).ap()

    xf0 = idtr("xf0", [2048, T])
    x1f = idtr("x1f", [2048, T])
    x2f = idtr("x2f", [2048, T])
    x3f = idtr("x3f", [2048, T])
    Ssp = idtr("Ssp", [NK, 128, T])
    Psp = idtr("Psp", [NK, 128, T])
    xpre_sp = idtr("xpre_sp", [NK, 128, T])
    u_sp = idtr("u_sp", [NK, 128, T])
    gs_sp = idtr("gs_sp", [NK, 128, T])
    loc_sp = idtr("loc_sp", [NK, 128, T])
    d1bf = dtr("d1bf", [NK, 128, 1024], BF16).ap()
    d2bf = dtr("d2bf", [8, 128, 2048], BF16).ap()
    cc_bi = {t: dtr(f"cc_bi_{t}", [128, 32], F32).ap() for t in ("f1", "f2")}
    cc_bo = {t: dtr(f"cc_bo_{t}", [128, 32], F32).ap() for t in ("f1", "f2")}
    cc_ti = {t: dtr(f"cc_ti_{t}", [2, 2048], F32).ap() for t in ("f1", "f2")}
    cc_to = {t: dtr(f"cc_to_{t}", [2, 2048], F32).ap() for t in ("f1", "f2")}
    cc_gi = dtr("cc_gi", [128, 32], F32).ap()
    cc_go = dtr("cc_go", [NCORE * 128, 32], F32).ap()

    ctx = ExitStack()
    sbt = lambda n, s, d: ctx.enter_context(nc.sbuf_tensor(n, s, d))
    aA = sbt("aA", [128, 32768], BF16)
    aB = sbt("aB", [128, 32768], BF16)
    aC = sbt("aC", [128, 32768], BF16)
    d1brow = sbt("d1brow", [2, 1024], F32)
    sm = sbt("sm", [128, 592], F32)
    smb = sbt("smb", [128, 768], BF16)
    ident = sbt("ident", [128, 128], F32)
    identb = sbt("identb", [128, 128], BF16)
    onesbt = sbt("onesbt", [128, 128], BF16)
    psA = ctx.enter_context(nc.psum_tensor("psA", [128, 3584], F32))
    psB = ctx.enter_context(nc.psum_tensor("psB", [128, 1024], BF16))

    aCf = aC[:].bitcast(F32)
    psBf = psB[:].bitcast(F32)

    def bank(i):
        return psA[:, i * 512:(i + 1) * 512]

    def cf(a):
        return aCf[:, a:a + 2048]

    def cfw(a, w):
        return aCf[:, a:a + w]

    def cbw(a, w):
        return aC[:, a:a + w]

    SM = dict(lastX=0, xpl=16, xnorm=32, ccst=48, xnall=80, s_in=112, ABs=128,
              g2=160, b2=176, thf=192, bcf=224, bcmy=256, ftmp=272, ftmp2=288,
              mkb0=304, mkb1=305, mb0=306, mb1=307, predm=308, predm1=316,
              ingam=324, inbet=340, outgam=356, outbet=372, gateb=388,
              initf=404, aggb=420, bcb=452, d2b=484, png=516, pnb=532,
              tln=548, gbh=564)

    def smv(name, w=16):
        o = SM[name]
        return sm[:, o:o + w]

    def smc(name, j=0):
        return sm[:, SM[name] + j:SM[name] + j + 1]

    def smbv(o, w):
        return smb[:, o:o + w]

    def evenodd(ap32, b):
        return ap32.rearrange("p (c b) -> p c b", b=2)[:, :, b]

    bmw = aCf[0:2, 4096:6144]

    SP, SA, SV = (Sem(n) for n in ["SP", "SA", "SV"])
    CCs = {n: Sem("cc_" + n) for n in
           ["f1b", "f1s0", "f1s1", "f1s2", "f1s3",
            "f2b", "f2s0", "f2s1", "f2s2", "f2s3", "ag"]}
    _pools = {}

    def S2(name):
        if name not in _pools:
            _pools[name] = (Sem(name + "0"), Sem(name + "1"))
        return _pools[name]

    def S1(name):
        if name not in _pools:
            _pools[name] = Sem(name)
        return _pools[name]

    for nm in ["sld", "sst", "sxl", "sxl2", "sxps", "sgst", "sgl", "sxp2",
               "sust", "sul3", "sxl3", "ssto", "sssp", "spsp", "slst",
               "spfs", "spfp", "sxr", "slrl", "gwl", "gagl", "gdsl", "gbcl",
               "gql", "gb1l", "gb2l", "grol"]:
        S2(nm)
    for nm in ["scl", "gstg", "ggw", "gcc"]:
        S1(nm)
    allsems = [SP, SA, SV] + list(CCs.values())
    for v in _pools.values():
        if isinstance(v, tuple):
            allsems.extend(v)
        else:
            allsems.append(v)

    R = Rec()
    R.serialized = (SV, SA)

    def wsl_src(w, rc):
        return w[rc * 128:(rc + 1) * 128, :]

    def cc_run(e, name, kind, op, i, o):
        ins = e.collective_compute(kind, op, replica_groups=RG,
                                   ins=[i.opt()], outs=[o.opt()])
        if not R.plan:
            ins.then_inc(CCs[name].h)
            e.wait_ge(CCs[name].h, 1)

    def tin_sync(e):
        pairs = [(ident[:], idn32), (identb[:], idnb), (onesbt[:], onesb),
                 (smc("mkb0"), msk["mkb0"]), (smc("mkb1"), msk["mkb1"]),
                 (smc("mb0"), msk["mb0"]), (smc("mb1"), msk["mb1"]),
                 (smv("predm", 8), predm), (smv("predm1", 8), predm1),
                 (d1brow[:], d1bs)]
        for n in FMW:
            pairs.append((smv(n, FMW[n]), fmv[n]))
        R.wait(e, ("ti", "sminit"))
        for i, (dst, src) in enumerate(pairs):
            key = ("ti", "cl") if i == len(pairs) - 1 else None
            R.inc(S1("scl"), e.dma_start(out=dst, in_=src), key, 16)
        for hc in range(NK):
            par = hc % 2
            if hc < 2:
                R.wait(e, ("ti", "cl"))
            else:
                R.wait(e, ("ti", "pe", hc - 2, 3))
            src = xs[:, hc * 128:(hc + 1) * 128].rearrange(
                "(tc p) m -> p tc m", p=128)
            dst = cf(par * 2048).rearrange("p (tc m) -> p tc m", m=128)
            R.inc(S2("sld")[hc % 2], e.dma_start(out=dst, in_=src),
                  ("ti", "ld", hc), 16)
            R.wait(e, ("ti", "ev", hc, 3))
            R.inc(S2("sst")[hc % 2], e.dma_start(
                out=xf0[hc * 128:(hc + 1) * 128, :],
                in_=cf(4096 + par * 2048)), ("ti", "st", hc), 16)

    def tin_gpsimd(e):
        sg = S1("gstg")
        for k in range(NK):
            R.inc(sg, e.dma_start(out=aB[:, 0:1024],
                                  in_=d1s[k * 128:(k + 1) * 128, :]), None, 16)
            if not R.plan:
                e.wait_ge(sg.h, sg.n)
            R.inc(sg, e.dma_start(out=d1bf[k], in_=aB[:, 0:1024]), None, 16)
            if not R.plan:
                e.wait_ge(sg.h, sg.n)
        for k in range(8):
            R.inc(sg, e.dma_start(out=aB[:, 0:2048],
                                  in_=d2s[k * 128:(k + 1) * 128, :]), None, 16)
            if not R.plan:
                e.wait_ge(sg.h, sg.n)
            key = ("ti", "stg") if k == 7 else None
            R.inc(sg, e.dma_start(out=d2bf[k], in_=aB[:, 0:2048]), key, 16)
            if not R.plan:
                e.wait_ge(sg.h, sg.n)
        for k in range(NK):
            key = ("f1", "gw") if k == NK - 1 else None
            R.inc(S1("ggw"), e.dma_start(
                out=aB[:, k * 2048:(k + 1) * 2048],
                in_=W["gate_w"][k * 128:(k + 1) * 128, :]), key, 16)

    def tin_tensor(e):
        for hc in range(NK):
            par = hc % 2
            R.wait(e, ("ti", "ld", hc))
            for tc in range(NK):
                g4 = tc // 4
                gidx = hc * 4 + g4
                if gidx >= 2:
                    R.wait(e, ("ti", "ev", (gidx - 2) // 4, (gidx - 2) % 4))
                ins = e.transpose(
                    bank(g4 % 2)[:, (tc % 4) * 128:(tc % 4) * 128 + 128],
                    cf(par * 2048)[:, tc * 128:tc * 128 + 128],
                    ident[:])
                if tc % 4 == 3:
                    R.inc(SP, ins, ("ti", "pe", hc, g4))

    def tin_scalar(e):
        pass

    def tin_vector(e):
        e.memset(sm[:], 0.0)
        e.memset(smb[:], 0.0)
        R.inc(SV, e.nop(), ("ti", "sminit"))
        R.wait(e, ("ti", "cl"))
        e.tensor_scalar(out=smv("g2"), in0=smv("outgam"), scalar1=0.5,
                        scalar2=None, op0=OP.mult)
        e.tensor_scalar(out=smv("b2"), in0=smv("outbet"),
                        scalar1=0.5, scalar2=None, op0=OP.mult)
        R.inc(SV, e.tensor_scalar(out=smv("gbh"), in0=smv("gateb"),
                                  scalar1=0.5, scalar2=None, op0=OP.mult),
              ("ti", "pc"))
        for hc in range(NK):
            par = hc % 2
            for g4 in range(4):
                R.wait(e, ("ti", "pe", hc, g4))
                if hc >= 2 and g4 == 0:
                    R.wait(e, ("ti", "st", hc - 2))
                R.inc(SV, e.tensor_copy(
                    out=cf(4096 + par * 2048)[:, g4 * 512:g4 * 512 + 512],
                    in_=bank(g4 % 2)), ("ti", "ev", hc, g4))

    def make_ffn(t, X, O, in_evt, bank_evt, gw_evt, arena_evt):
        P = {}
        P["t"], P["X"], P["O"] = t, X, O
        P["in_evt"] = in_evt
        P["bank_evt"] = bank_evt
        P["gw_evt"] = gw_evt
        P["arena_evt"] = arena_evt
        P["xch"] = lambda p: cf(p * 2048)
        P["sqs"] = lambda p: cbw(8192 + p * 2048, 2048)
        P["dscr"] = cf(4096)
        P["mu"] = cf(6144)
        P["rs"] = cf(8192)
        P["gss"] = lambda p: cf(12288 + p * 2048)
        P["wsl2"] = lambda n: cbw(20480 + (n % 2) * 2048, 2048)
        P["ub"] = lambda p: aA[:, p * 2048:p * 2048 + 2048]
        P["ubsq"] = lambda p: P["sqs"](p)
        seg = []
        seg.append(("th", "agg"))
        for rc in range(NK):
            seg.append(("gate", rc))
            if rc in TH_SEG:
                seg.append(("th", TH_SEG[rc]))
        P["seg"] = seg
        return P

    def ffn_sync(P, e):
        t, X, O = P["t"], P["X"], P["O"]
        for k in range(NK):
            R.wait(e, P["in_evt"](k))
            if k == 0:
                R.wait(e, P["arena_evt"])
            if k >= 2:
                R.wait(e, (t, "cast", k - 2))
            R.inc(S2("sxl")[k % 2], e.dma_start(
                out=P["xch"](k % 2), in_=X[k * 128:(k + 1) * 128, :]),
                (t, "xl", k), 16)
        for k in range(NK):
            R.wait(e, (t, "cast", NK - 2 + k) if k < 2 else (t, "apl", k - 2))
            if k < 2:
                R.wait(e, (t, "stat"))
            R.inc(S2("sxl2")[k % 2], e.dma_start(
                out=P["xch"](k % 2), in_=X[k * 128:(k + 1) * 128, :]),
                (t, "xl2", k), 16)
            R.wait(e, (t, "apl", k))
            R.inc(S2("sxps")[k % 2], e.dma_start(
                out=xpre_sp[k], in_=P["gss"](k % 2)), (t, "xps", k), 16)
        for rc in range(NK):
            R.wait(e, (t, "gse", rc))
            R.inc(S2("sgst")[rc % 2], e.dma_start(
                out=gs_sp[rc], in_=P["gss"](rc % 2)), (t, "gst", rc), 16)
        for rc in range(NK):
            if rc >= 2:
                R.wait(e, (t, "ubc", rc - 2))
                R.wait(e, (t, "ust", rc - 2))
            else:
                R.wait(e, (t, "gst", NK - 2 + rc))
            R.inc(S2("sgl")[rc % 2], e.dma_start(
                out=P["xch"](rc % 2), in_=gs_sp[rc]), (t, "gl", rc), 16)
            R.inc(S2("sxp2")[rc % 2], e.dma_start(
                out=P["gss"](rc % 2), in_=xpre_sp[rc]), (t, "xpl2", rc), 16)
            R.wait(e, (t, "uc", rc))
            R.inc(S2("sust")[rc % 2], e.dma_start(
                out=u_sp[rc], in_=P["gss"](rc % 2)), (t, "ust", rc), 16)
        for rc in range(NK):
            if rc >= 2:
                R.wait(e, (t, "sto", rc - 2))
            else:
                R.wait(e, (t, "cst"))
                R.wait(e, (t, "ust", NK - 2 + rc))
            R.inc(S2("sul3")[rc % 2], e.dma_start(
                out=P["xch"](rc % 2), in_=u_sp[rc]), (t, "ul3", rc), 16)
            R.inc(S2("sxl3")[rc % 2], e.dma_start(
                out=P["gss"](rc % 2), in_=X[rc * 128:(rc + 1) * 128, :]),
                (t, "xl3", rc), 16)
            R.wait(e, (t, "fbo", rc))
            R.inc(S2("ssto")[rc % 2], e.dma_start(
                out=O[rc * 128:(rc + 1) * 128, :], in_=P["xch"](rc % 2)),
                (t, "sto", rc), 16)

    def ffn_gpsimd(P, e):
        t = P["t"]
        if P["gw_evt"] is None:
            R.wait(e, P["arena_evt"])
            for k in range(NK):
                key = (t, "gw") if k == NK - 1 else None
                R.inc(S1("ggw"), e.dma_start(
                    out=aB[:, k * 2048:(k + 1) * 2048],
                    in_=W["gate_w"][k * 128:(k + 1) * 128, :]), key, 16)
            gw_evt = (t, "gw")
        else:
            gw_evt = P["gw_evt"]
        P["gw_key"] = gw_evt
        R.wait(e, (t, "lnl"))
        if t == "f1":
            R.inc(S1("gcc"), e.dma_start(out=dump_sm["a"], in_=sm[:]),
                  (t, "smdump0"), 16)
            R.inc(S1("gcc"), e.dma_start(out=dump_smb, in_=smb[:]),
                  (t, "smdump0b"), 16)
        R.inc(S1("gcc"), e.dma_start(out=cc_bi[t], in_=smv("ccst", 32)),
              (t, "ccs"), 16)
        R.wait(e, (t, "ccs"))
        cc_run(e, t + "b", "AllReduce", OP.add, cc_bi[t], cc_bo[t])
        R.inc(S1("gcc"), e.dma_start(out=smv("xnall", 32), in_=cc_bo[t]),
              (t, "ccb_back"), 16)
        for k in range(NK):
            if k >= 2:
                R.wait(e, (t, "ak", k - 2))
            else:
                R.wait(e, (t, "evs", 1))
            R.inc(S2("gagl")[k % 2], e.dma_start(
                out=P["wsl2"](k),
                in_=W["agg_w"][k * 128:(k + 1) * 128, :]), (t, "agl", k), 16)
        for s in range(4):
            for k in range(NK):
                n = s * 24 + k
                if k >= 2:
                    R.wait(e, (t, "dk", s, k - 2))
                elif s > 0:
                    R.wait(e, (t, "dk", s - 1, 22 + k))
                else:
                    R.wait(e, (t, "ak", 14 + k))
                R.inc(S2("gdsl")[n % 2], e.dma_start(
                    out=P["wsl2"](NK + n)[:, 0:1024], in_=d1bf[k]),
                    (t, "dsl", s, k), 16)
            for k in range(8):
                n = s * 24 + 16 + k
                R.wait(e, (t, "dk", s, 14 + k))
                R.inc(S2("gdsl")[n % 2], e.dma_start(
                    out=P["wsl2"](NK + n), in_=d2bf[k]),
                    (t, "dsl", s, 16 + k), 16)
            R.wait(e, (t, "pev", "d2", s))
            R.inc(S1("gcc"), e.dma_start(out=cc_ti[t], in_=bmw),
                  (t, "ars", s), 16)
            R.wait(e, (t, "ars", s))
            cc_run(e, t + "s%d" % s, "AllReduce", OP.add, cc_ti[t], cc_to[t])
            R.inc(S1("gcc"), e.dma_start(out=bmw, in_=cc_to[t]),
                  (t, "arb", s), 16)
        for k in range(NK):
            n = NK + 4 * 24 + k
            if k >= 2:
                R.wait(e, (t, "bk", k - 2))
            else:
                R.wait(e, (t, "dk", 3, 22 + k))
            R.inc(S2("gbcl")[n % 2], e.dma_start(
                out=P["wsl2"](n),
                in_=W["bc_w"][k * 128:(k + 1) * 128, :]), (t, "bcl", k), 16)
        R.wait(e, (t, "bcr"))
        R.inc(S1("gcc"), e.dma_start(out=dump_sm[t], in_=sm[:]),
              (t, "smdump"), 16)

    def ffn_tensor(P, e):
        t = P["t"]
        R.wait(e, P["bank_evt"])
        for k in range(NK):
            R.wait(e, (t, "cast", k))
            for tq in range(NTQ):
                ins = e.matmul(bank(tq), onesbt[:],
                               aA[:, k * 2048 + tq * 512:k * 2048 + tq * 512 + 512],
                               start=(k == 0), stop=(k == NK - 1))
            if k == NK - 1:
                R.inc(SP, ins, (t, "smm", 0))
        R.wait(e, (t, "evs", 0))
        for k in range(NK):
            R.wait(e, (t, "sq", k))
            for tq in range(NTQ):
                ins = e.matmul(bank(tq), onesbt[:],
                               P["sqs"](k % 2)[:, tq * 512:tq * 512 + 512],
                               start=(k == 0), stop=(k == NK - 1))
            R.inc(SP, ins, (t, "s2", k))
        R.alias((t, "smm", 1), (t, "s2", NK - 1))
        R.wait(e, (t, "lnc1"))
        e.matmul(psA[:, 4 * 512:4 * 512 + 16], onesbt[:], smbv(384, 16),
                 start=True, stop=True)
        R.wait(e, (t, "lnc2"))
        R.inc(SP, e.matmul(psA[:, 4 * 512 + 16:4 * 512 + 32], onesbt[:],
                           smbv(640, 16), start=True, stop=True), (t, "lnmm"))
        for item in P["seg"]:
            if item[0] == "gate":
                ffn_t_gate(P, e, item[1])
            else:
                ffn_t_th(P, e, item[1])
        ffn_t_fac(P, e)

    def ffn_t_gate(P, e, rc):
        t = P["t"]
        if rc == 0:
            R.wait(e, (t, "apl", NK - 1))
            R.wait(e, P["gw_key"])
        for tq in range(NTQ):
            g = rc * NTQ + tq
            if g >= 2:
                R.wait(e, (t, "gsg", (g - 2) // 4, (g - 2) % 4))
            for k in range(NK):
                ins = e.matmul(
                    bank(g % 2),
                    aB[:, k * 2048 + rc * 128:k * 2048 + rc * 128 + 128],
                    aA[:, k * 2048 + tq * 512:k * 2048 + tq * 512 + 512],
                    start=(k == 0), stop=(k == NK - 1))
            R.inc(SP, ins, (t, "gmm", rc, tq))

    def ffn_t_fold(P, e, s):
        t = P["t"]
        if s == "agg":
            R.wait(e, (t, "pev", "agg"))
        else:
            R.wait(e, (t, "arb", s))
        for c in range(NK):
            ins = e.transpose(psA[:, 6 * 512 + 2 * c:6 * 512 + 2 * c + 2],
                              bmw[0:2, c * 128:(c + 1) * 128],
                              ident[0:2, 0:2])
        R.inc(SP, ins, (t, "fold", s))

    def ffn_t_th(P, e, it):
        t = P["t"]
        if it == "agg":
            R.wait(e, (t, "xna"))
            for k in range(NK):
                R.wait(e, (t, "agl", k))
                for q in range(4):
                    ins = e.matmul(
                        psA[0:2, 1024 + q * 512:1024 + q * 512 + 512],
                        smbv(512, 32)[:, 2 * k:2 * k + 2],
                        P["wsl2"](k)[:, q * 512:q * 512 + 512],
                        start=(k == 0), stop=(k == NK - 1))
                R.inc(SP, ins, (t, "ak", k))
        elif it == "bc":
            ffn_t_fold(P, e, 3)
            R.wait(e, (t, "thb", 4))
            for k in range(NK):
                R.wait(e, (t, "bcl", k))
                for q in range(4):
                    ins = e.matmul(
                        psA[0:2, 1024 + q * 512:1024 + q * 512 + 512],
                        smbv(0, 32)[:, 2 * k:2 * k + 2],
                        P["wsl2"](NK + 4 * 24 + k)[:, q * 512:q * 512 + 512],
                        start=(k == 0), stop=(k == NK - 1))
                R.inc(SP, ins, (t, "bk", k))
            R.wait(e, (t, "pev", "bc"))
            for c in range(NK):
                ins = e.transpose(psA[:, 6 * 512 + 2 * c:6 * 512 + 2 * c + 2],
                                  bmw[0:2, c * 128:(c + 1) * 128],
                                  ident[0:2, 0:2])
            R.inc(SP, ins, (t, "fold", "bc"))
        else:
            s = int(it[1])
            ffn_t_fold(P, e, "agg" if s == 0 else s - 1)
            R.wait(e, (t, "thb", s))
            R.wait(e, (t, "pev", "fold", s))
            e.matmul(psA[:, 6 * 512:6 * 512 + 32], onesbt[:], smbv(0, 32),
                     start=True, stop=True)
            R.wait(e, (t, "thsq", s))
            R.inc(SP, e.matmul(psA[:, 6 * 512 + 32:6 * 512 + 64], onesbt[:],
                               smbv(640, 32), start=True, stop=True),
                  (t, "lnm", s))
            R.wait(e, (t, "tnc", s))
            if s > 0:
                R.wait(e, (t, "pev", "d2", s - 1))
            for k in range(NK):
                R.wait(e, (t, "dsl", s, k))
                for q in range(2):
                    ins = e.matmul(
                        psA[0:2, 1024 + q * 512:1024 + q * 512 + 512],
                        smbv(128, 32)[:, 2 * k:2 * k + 2],
                        P["wsl2"](NK + s * 24 + k)[:, q * 512:q * 512 + 512],
                        start=(k == 0), stop=(k == NK - 1))
                R.inc(SP, ins, (t, "dk", s, k))
            R.alias((t, "pmmd1", s), (t, "dk", s, NK - 1))
            R.wait(e, (t, "tbb", s))
            if s > 0:
                R.wait(e, (t, "ttb", s - 1))
            tb = aC[0:2, 10240:11264]
            for j in range(8):
                ins = e.transpose(psB[:, 2 * j:2 * j + 2],
                                  tb[0:2, j * 128:(j + 1) * 128],
                                  identb[0:2, 0:2])
            R.inc(SP, ins, (t, "ttr", s))
            R.wait(e, (t, "ttb", s))
            for k in range(8):
                R.wait(e, (t, "dsl", s, 16 + k))
                for q in range(4):
                    ins = e.matmul(
                        psA[0:2, 1024 + q * 512:1024 + q * 512 + 512],
                        smbv(256, 16)[:, 2 * k:2 * k + 2],
                        P["wsl2"](NK + s * 24 + 16 + k)[:, q * 512:q * 512 + 512],
                        start=(k == 0), stop=(k == 7))
                R.inc(SP, ins, (t, "dk", s, 16 + k))
            R.alias((t, "pmmd2", s), (t, "dk", s, 23))

    def ffn_t_fac(P, e):
        t = P["t"]
        R.wait(e, (t, "pev", "bcf"))
        R.wait(e, (t, "ttb", 3))
        R.wait(e, (t, "lnr"))
        for rc in range(NK):
            R.wait(e, (t, "ubc", rc))
            for tq in range(NTQ):
                ins = e.matmul(bank(tq), onesbt[:],
                               P["ub"](rc % 2)[:, tq * 512:tq * 512 + 512],
                               start=(rc == 0), stop=(rc == NK - 1))
            R.wait(e, (t, "usq", rc))
            for tq in range(NTQ):
                dst = bank(4 + tq) if tq < 3 else psBf[:]
                ins = e.matmul(dst, onesbt[:],
                               P["sqs"](rc % 2)[:, tq * 512:tq * 512 + 512],
                               start=(rc == 0), stop=(rc == NK - 1))
            R.inc(SP, ins, (t, "cmm", rc))

    def ffn_scalar(P, e):
        t = P["t"]
        for k in range(NK):
            R.wait(e, (t, "cast", k))
            if k == 0:
                for ev in P.get("scr_evt", []):
                    R.wait(e, ev)
            if k >= 2:
                R.wait(e, (t, "s2", k - 2))
            R.inc(SA, e.activation(out=P["sqs"](k % 2),
                                   in_=aA[:, k * 2048:(k + 1) * 2048],
                                   func=AF.Square), (t, "sq", k))
        R.wait(e, (t, "lnc1"))
        R.inc(SA, e.activation(out=smbv(640, 128), in_=smbv(384, 128),
                               func=AF.Square), (t, "lnc2"))
        for item in P["seg"]:
            if item[0] == "gate":
                rc = item[1]
                if rc == 0:
                    R.wait(e, (t, "xps", NK - 2))
                    R.wait(e, (t, "xps", NK - 1))
                for tq in range(NTQ):
                    g = rc * NTQ + tq
                    R.wait(e, (t, "gmm", rc, tq))
                    if g >= 2:
                        R.wait(e, (t, "gsd", (g - 2) // 4, (g - 2) % 4))
                    R.inc(SA, e.activation(
                        out=P["mu"][:, (g % 2) * 512:(g % 2) * 512 + 512],
                        in_=bank(g % 2), func=AF.Tanh,
                        scale=0.5, bias=smc("gbh", rc)), (t, "gsg", rc, tq))
            else:
                it = item[1]
                if it in ("agg", "bc"):
                    continue
                s = int(it[1])
                R.wait(e, (t, "thb", s))
                R.inc(SA, e.activation(out=smbv(640, 128), in_=smbv(0, 128),
                                       func=AF.Square), (t, "thsq", s))
                R.wait(e, (t, "pev", "d1", s))
                av = aCf[0:2, 8832:9856]
                e.activation(out=av, in_=bmw[:, 0:1024], func=AF.Abs,
                             scale=0.7071067811865476)
                e.activation(out=aCf[0:2, 0:1024], in_=av, func=AF.Square)
                R.inc(SA, e.activation(out=bmw[:, 1024:2048],
                                       in_=aCf[0:2, 0:1024], func=AF.Exp,
                                       scale=-1.0), (t, "gel", s))
        for rc in range(NK):
            R.wait(e, (t, "gl", rc))
            if rc == 0:
                R.wait(e, (t, "bcr"))
            else:
                R.wait(e, (t, "uc", rc - 1))
            R.inc(SA, e.activation(out=P["mu"], in_=P["xch"](rc % 2),
                                   func=AF.Identity,
                                   scale=smv("bcmy")[:, rc:rc + 1]),
                  (t, "fas", rc))
            R.wait(e, (t, "ubc", rc))
            if rc >= 2:
                R.wait(e, (t, "cmm", rc - 2))
            R.inc(SA, e.activation(out=P["sqs"](rc % 2), in_=P["ub"](rc % 2),
                                   func=AF.Square), (t, "usq", rc))
        for rc in range(NK):
            R.wait(e, (t, "fbe", rc))
            if rc >= 1:
                R.wait(e, (t, "fbo", rc - 1))
            R.inc(SA, e.activation(out=P["dscr"], in_=P["xch"](rc % 2),
                                   func=AF.Identity, scale=smc("g2", rc),
                                   bias=smc("b2", rc)), (t, "fbs", rc))

    _EA = (0.254829592, -0.284496736, 1.421413741, -1.453152027, 1.061405429)
    _EP = 0.3275911

    def ffn_v_th(P, e, it):
        t = P["t"]
        if it == "agg":
            R.wait(e, (t, "ak", NK - 1))
            R.inc(SV, e.tensor_copy(out=bmw, in_=psA[0:2, 1024:3072]),
                  (t, "pev", "agg"))
            return
        if it == "bc":
            R.wait(e, (t, "fold", 3))
            e.tensor_tensor(out=smv("thf", 32),
                            in0=psA[:, 6 * 512:6 * 512 + 32],
                            in1=smv("thf", 32), op=OP.add)
            e.tensor_tensor(out=smv("thf", 32), in0=smv("thf", 32),
                            in1=smv("d2b", 32), op=OP.add)
            R.inc(SV, e.tensor_copy(out=smbv(0, 128), in_=sm[:, 192:320]),
                  (t, "thb", 4))
            R.wait(e, (t, "bk", NK - 1))
            R.inc(SV, e.tensor_copy(out=bmw, in_=psA[0:2, 1024:3072]),
                  (t, "pev", "bc"))
            R.wait(e, (t, "fold", "bc"))
            e.tensor_tensor(out=smv("bcf", 32), in0=psA[:, 6 * 512:6 * 512 + 32],
                            in1=smv("bcb", 32), op=OP.add)
            R.inc(SV, e.nop(), (t, "pev", "bcf"))
            e.tensor_scalar(out=smv("bcmy"), in0=evenodd(smv("bcf", 32), 0),
                            scalar1=smc("mb0"), scalar2=None, op0=OP.mult)
            e.tensor_scalar(out=smv("ftmp"), in0=evenodd(smv("bcf", 32), 1),
                            scalar1=smc("mb1"), scalar2=None, op0=OP.mult)
            R.inc(SV, e.tensor_tensor(out=smv("bcmy"), in0=smv("bcmy"),
                                      in1=smv("ftmp"), op=OP.add), (t, "bcr"))
            return
        s = int(it[1])
        R.wait(e, (t, "fold", "agg" if s == 0 else s - 1))
        if s == 0:
            e.tensor_tensor(out=smv("thf", 32),
                            in0=psA[:, 6 * 512:6 * 512 + 32],
                            in1=smv("aggb", 32), op=OP.add)
        else:
            e.tensor_tensor(out=smv("thf", 32),
                            in0=psA[:, 6 * 512:6 * 512 + 32],
                            in1=smv("thf", 32), op=OP.add)
            e.tensor_tensor(out=smv("thf", 32), in0=smv("thf", 32),
                            in1=smv("d2b", 32), op=OP.add)
        R.inc(SV, e.nop(), (t, "pev", "fold", s))
        R.inc(SV, e.tensor_copy(out=smbv(0, 128), in_=sm[:, 192:320]),
              (t, "thb", s))
        R.wait(e, (t, "lnm", s))
        th_ln(P, e, s)
        R.wait(e, (t, "pmmd1", s))
        R.inc(SV, e.tensor_tensor(out=bmw[:, 0:1024],
                                  in0=psA[0:2, 1024:2048],
                                  in1=d1brow[:], op=OP.add),
              (t, "pev", "d1", s))
        R.wait(e, (t, "gel", s))
        av = aCf[0:2, 8832:9856]
        w = bmw[:, 1024:2048]
        S1 = aCf[0:2, 0:1024]
        S2 = aCf[0:2, 1024:2048]
        S3 = aCf[0:2, 2048:3072]
        S4 = aCf[0:2, 3072:4096]
        e.tensor_scalar(out=S1, in0=av, scalar1=_EP, scalar2=1.0,
                        op0=OP.mult, op1=OP.add)
        e.reciprocal(out=S2, in_=S1)
        e.tensor_scalar(out=S3, in0=S2, scalar1=_EA[4], scalar2=_EA[3],
                        op0=OP.mult, op1=OP.add)
        for aK in (_EA[2], _EA[1], _EA[0]):
            e.tensor_tensor(out=S4, in0=S3, in1=S2, op=OP.mult)
            e.tensor_scalar(out=S3, in0=S4, scalar1=aK, scalar2=None,
                            op0=OP.add)
        e.tensor_tensor(out=S4, in0=S3, in1=S2, op=OP.mult)
        e.tensor_tensor(out=S1, in0=w, in1=S4, op=OP.mult)
        e.tensor_scalar(out=S2, in0=S1, scalar1=-1.0, scalar2=1.0,
                        op0=OP.mult, op1=OP.add)
        e.tensor_tensor(out=S3, in0=av, in1=S2, op=OP.mult)
        e.tensor_scalar(out=S4, in0=S3, scalar1=0.7071067811865476,
                        scalar2=None, op0=OP.mult)
        tb = aC[0:2, 10240:11264]
        R.inc(SV, e.scalar_tensor_tensor(out=tb, in0=bmw[:, 0:1024],
                                         scalar=0.5, in1=S4,
                                         op0=OP.mult, op1=OP.add),
              (t, "tbb", s))
        R.wait(e, (t, "ttr", s))
        R.inc(SV, e.tensor_copy(out=smbv(256, 16), in_=psB[:, 0:16]),
              (t, "ttb", s))
        R.wait(e, (t, "pmmd2", s))
        R.inc(SV, e.tensor_copy(out=bmw, in_=psA[0:2, 1024:3072]),
              (t, "pev", "d2", s))

    def ffn_vector(P, e):
        t = P["t"]
        R.wait(e, P["arena_evt"])
        for k in range(NK):
            R.wait(e, (t, "xl", k))
            e.tensor_copy(out=aA[:, k * 2048:(k + 1) * 2048], in_=P["xch"](k % 2))
            R.inc(SV, e.tensor_copy(out=smc("lastX", k),
                                    in_=P["xch"](k % 2)[:, 2047:2048]),
                  (t, "cast", k))
        R.wait(e, (t, "smm", 0))
        for tq in range(NTQ):
            ins = e.tensor_copy(out=P["gss"](0)[:, tq * 512:tq * 512 + 512],
                                in_=bank(tq))
        R.inc(SV, ins, (t, "evs", 0))
        R.wait(e, (t, "smm", 1))
        for tq in range(NTQ):
            ins = e.tensor_copy(out=P["gss"](1)[:, tq * 512:tq * 512 + 512],
                                in_=bank(tq))
        R.inc(SV, ins, (t, "evs", 1))
        e.tensor_scalar(out=P["mu"], in0=P["gss"](0), scalar1=INV_H,
                        scalar2=None, op0=OP.mult)
        e.tensor_scalar(out=P["rs"], in0=P["gss"](1), scalar1=INV_H,
                        scalar2=None, op0=OP.mult)
        e.tensor_tensor(out=P["gss"](0), in0=P["mu"], in1=P["mu"], op=OP.mult)
        e.tensor_tensor(out=P["gss"](1), in0=P["rs"], in1=P["gss"](0),
                        op=OP.subtract)
        rsqrt_newton(e, P["rs"], P["gss"](1), P["gss"](0), P["dscr"],
                     P["xch"](0), EPS_PRE)
        R.inc(SV, e.nop(), (t, "stat"))
        e.tensor_scalar(out=smv("xpl"), in0=smv("lastX"),
                        scalar1=P["mu"][:, 2047:2048],
                        scalar2=P["rs"][:, 2047:2048],
                        op0=OP.subtract, op1=OP.mult)
        R.inc(SV, e.tensor_copy(out=smbv(384, 128), in_=sm[:, 16:144]),
              (t, "lnc1"))
        R.wait(e, (t, "lnmm"))
        tln = smv("tln", 16)
        e.tensor_reduce(out=tln[:, 0:1].rearrange("p (a o) -> p a o", o=1),
                        in_=psA[:, 4 * 512:4 * 512 + 16].rearrange(
                            "p (a c) -> p a c", a=1),
                        axis=mybir.AxisListType.X, op=OP.add)
        R.inc(SV, e.tensor_reduce(
            out=tln[:, 1:2].rearrange("p (a o) -> p a o", o=1),
            in_=psA[:, 4 * 512 + 16:4 * 512 + 32].rearrange(
                "p (a c) -> p a c", a=1),
            axis=mybir.AxisListType.X, op=OP.add), (t, "lnr"))
        e.tensor_scalar(out=tln[:, 4:5], in0=tln[:, 0:1], scalar1=INV_H,
                        scalar2=None, op0=OP.mult)
        e.tensor_scalar(out=tln[:, 5:6], in0=tln[:, 1:2], scalar1=INV_H,
                        scalar2=None, op0=OP.mult)
        e.tensor_tensor(out=tln[:, 2:3], in0=tln[:, 4:5], in1=tln[:, 4:5],
                        op=OP.mult)
        e.tensor_tensor(out=tln[:, 6:7], in0=tln[:, 5:6], in1=tln[:, 2:3],
                        op=OP.subtract)
        ws = P["gss"](0)
        e.memset(ws[:, 0:640], 1.0)
        e.tensor_copy(out=ws[:, 6:7], in_=tln[:, 6:7])
        rsqrt_newton(e, ws[:, 384:512], ws[:, 0:128], ws[:, 128:256],
                     ws[:, 256:384], ws[:, 512:640], EPS_FFN)
        e.tensor_copy(out=tln[:, 7:8], in_=ws[:, 384 + 6:384 + 7])
        e.tensor_scalar(out=smv("xnorm"), in0=smv("xpl"), scalar1=tln[:, 4:5],
                        scalar2=tln[:, 7:8], op0=OP.subtract, op1=OP.mult)
        e.tensor_tensor(out=smv("xnorm"), in0=smv("xnorm"), in1=smv("ingam"),
                        op=OP.mult)
        e.tensor_tensor(out=smv("xnorm"), in0=smv("xnorm"), in1=smv("inbet"),
                        op=OP.add)
        e.tensor_scalar(out=evenodd(smv("ccst", 32), 0), in0=smv("xnorm"),
                        scalar1=smc("mkb0"), scalar2=None, op0=OP.mult)
        R.inc(SV, e.tensor_scalar(out=evenodd(smv("ccst", 32), 1),
                                  in0=smv("xnorm"), scalar1=smc("mkb1"),
                                  scalar2=None, op0=OP.mult), (t, "lnl"))
        R.wait(e, (t, "ccb_back"))
        R.inc(SV, e.tensor_copy(out=smbv(512, 128), in_=sm[:, 80:208]),
              (t, "xna"))
        for k in range(NK):
            R.wait(e, (t, "xl2", k))
            if k >= 2:
                R.wait(e, (t, "xps", k - 2))
            e.tensor_tensor(out=P["xch"](k % 2), in0=P["xch"](k % 2),
                            in1=P["mu"], op=OP.subtract)
            e.tensor_tensor(out=P["gss"](k % 2), in0=P["xch"](k % 2),
                            in1=P["rs"], op=OP.mult)
            R.inc(SV, e.tensor_tensor(out=aA[:, k * 2048:(k + 1) * 2048],
                                      in0=P["xch"](k % 2), in1=P["rs"],
                                      op=OP.mult), (t, "apl", k))
        for item in P["seg"]:
            if item[0] == "gate":
                rc = item[1]
                for tq in range(NTQ):
                    g = rc * NTQ + tq
                    R.wait(e, (t, "gsg", rc, tq))
                    if rc >= 2 and tq == 0:
                        R.wait(e, (t, "gst", rc - 2))
                    R.inc(SV, e.tensor_scalar(
                        out=P["gss"](rc % 2)[:, tq * 512:tq * 512 + 512],
                        in0=P["mu"][:, (g % 2) * 512:(g % 2) * 512 + 512],
                        scalar1=0.5, scalar2=0.5, op0=OP.mult, op1=OP.add),
                        (t, "gsd", rc, tq))
                R.alias((t, "gse", rc), (t, "gsd", rc, 3))
            else:
                ffn_v_th(P, e, item[1])
        for rc in range(NK):
            R.wait(e, (t, "xpl2", rc))
            R.wait(e, (t, "fas", rc))
            R.inc(SV, e.tensor_tensor(out=P["gss"](rc % 2), in0=P["mu"],
                                      in1=P["gss"](rc % 2), op=OP.add),
                  (t, "uc", rc))
            if rc >= 2:
                R.wait(e, (t, "cmm", rc - 2))
            R.inc(SV, e.tensor_copy(out=P["ub"](rc % 2), in_=P["gss"](rc % 2)),
                  (t, "ubc", rc))
        R.wait(e, (t, "cmm", NK - 1))
        R.wait(e, (t, "ust", NK - 2))
        R.wait(e, (t, "ust", NK - 1))
        for tq in range(NTQ):
            e.tensor_copy(out=P["gss"](0)[:, tq * 512:tq * 512 + 512],
                          in_=bank(tq))
        for tq in range(NTQ):
            src = bank(4 + tq) if tq < 3 else psBf[:]
            ins = e.tensor_copy(out=P["gss"](1)[:, tq * 512:tq * 512 + 512],
                                in_=src)
        R.inc(SV, ins, (t, "cev"))
        e.tensor_scalar(out=P["mu"], in0=P["gss"](0), scalar1=INV_H,
                        scalar2=None, op0=OP.mult)
        e.tensor_scalar(out=P["rs"], in0=P["gss"](1), scalar1=INV_H,
                        scalar2=None, op0=OP.mult)
        e.tensor_tensor(out=P["gss"](0), in0=P["mu"], in1=P["mu"], op=OP.mult)
        e.tensor_tensor(out=P["gss"](1), in0=P["rs"], in1=P["gss"](0),
                        op=OP.subtract)
        rsqrt_newton(e, P["rs"], P["gss"](1), P["gss"](0), P["dscr"],
                     P["xch"](0), EPS_FFN)
        R.inc(SV, e.nop(), (t, "cst"))
        for rc in range(NK):
            R.wait(e, (t, "ul3", rc))
            R.wait(e, (t, "xl3", rc))
            e.tensor_tensor(out=P["xch"](rc % 2), in0=P["xch"](rc % 2),
                            in1=P["mu"], op=OP.subtract)
            R.inc(SV, e.tensor_tensor(out=P["xch"](rc % 2), in0=P["xch"](rc % 2),
                                      in1=P["rs"], op=OP.mult), (t, "fbe", rc))
            R.wait(e, (t, "fbs", rc))
            R.inc(SV, e.tensor_tensor(out=P["xch"](rc % 2), in0=P["dscr"],
                                      in1=P["gss"](rc % 2), op=OP.add),
                  (t, "fbo", rc))

    def th_ln(P, e, s):
        t = P["t"]
        tln = smv("tln", 16)
        p6 = psA[:, 6 * 512:6 * 512 + 64]
        e.tensor_reduce(out=tln[:, 0:2].rearrange("p (b o) -> p b o", o=1),
                        in_=p6[:, 0:32].rearrange("p (c b) -> p b c", b=2),
                        axis=mybir.AxisListType.X, op=OP.add)
        e.tensor_reduce(out=tln[:, 2:4].rearrange("p (b o) -> p b o", o=1),
                        in_=p6[:, 32:64].rearrange("p (c b) -> p b c", b=2),
                        axis=mybir.AxisListType.X, op=OP.add)
        e.tensor_scalar(out=tln[:, 8:10], in0=tln[:, 0:2], scalar1=INV_H,
                        scalar2=None, op0=OP.mult)
        e.tensor_scalar(out=tln[:, 10:12], in0=tln[:, 2:4], scalar1=INV_H,
                        scalar2=None, op0=OP.mult)
        e.tensor_tensor(out=tln[:, 4:6], in0=tln[:, 8:10], in1=tln[:, 8:10],
                        op=OP.mult)
        e.tensor_tensor(out=tln[:, 12:14], in0=tln[:, 10:12], in1=tln[:, 4:6],
                        op=OP.subtract)
        ws = P["rs"]
        e.memset(ws[:, 0:640], 1.0)
        e.tensor_copy(out=ws[:, 12:14], in_=tln[:, 12:14])
        rsqrt_newton(e, ws[:, 384:512], ws[:, 0:128], ws[:, 128:256],
                     ws[:, 256:384], ws[:, 512:640], EPS_FFN)
        e.tensor_copy(out=tln[:, 14:16], in_=ws[:, 384 + 12:384 + 14])
        for b in range(2):
            dst = evenodd(smv("ftmp", 32), b)
            e.tensor_scalar(out=dst, in0=evenodd(smv("thf", 32), b),
                            scalar1=tln[:, 8 + b:9 + b],
                            scalar2=tln[:, 14 + b:15 + b],
                            op0=OP.subtract, op1=OP.mult)
            e.tensor_tensor(out=dst, in0=dst, in1=smv("png"), op=OP.mult)
            e.tensor_tensor(out=dst, in0=dst, in1=smv("pnb"), op=OP.add)
        R.inc(SV, e.tensor_copy(out=smbv(128, 128), in_=sm[:, 272:400]),
              (t, "tnc", s))

    def attn_sync(e):
        t = "a"
        for k in range(NK):
            R.wait(e, ("f1", "sto", k))
            if k >= 2:
                R.wait(e, (t, "cast", k - 2))
            else:
                R.wait(e, ("f1", "sto", NK - 2 + k))
            R.inc(S2("sxl")[k % 2], e.dma_start(
                out=cf((k % 2) * 2048), in_=x1f[k * 128:(k + 1) * 128, :]),
                (t, "xl", k), 16)
        for k in range(NK):
            R.wait(e, (t, "cast", NK - 2 + k) if k < 2 else (t, "apl", k - 2))
            if k < 2:
                R.wait(e, (t, "stat"))
            R.inc(S2("sxl2")[k % 2], e.dma_start(
                out=cf((k % 2) * 2048), in_=x1f[k * 128:(k + 1) * 128, :]),
                (t, "xl2", k), 16)
        for rc in range(NK):
            R.wait(e, (t, "scn", rc))
            R.inc(S2("sssp")[rc % 2], e.dma_start(out=Ssp[rc], in_=cf(4096)),
                  (t, "ssp", rc), 16)
            R.inc(S2("spsp")[rc % 2], e.dma_start(out=Psp[rc], in_=cf(6144)),
                  (t, "psp", rc), 16)
        for hc in range(NK):
            R.wait(e, (t, "lev", hc))
            R.inc(S2("slst")[hc % 2], e.dma_start(
                out=loc_sp[hc], in_=cf(12288 + (hc % 2) * 2048)),
                (t, "lst", hc), 16)
        for rc in range(NK):
            if rc < 2:
                R.wait(e, (t, "scn", NK - 1))
                R.wait(e, (t, "lst", NK - 2))
                R.wait(e, (t, "lst", NK - 1))
            else:
                R.wait(e, (t, "fix", rc - 2))
                R.wait(e, (t, "m1", rc - 2, NTQ - 1))
            R.inc(S2("spfs")[rc % 2], e.dma_start(
                out=cf(8192 + (rc % 2) * 2048), in_=Ssp[rc]),
                (t, "pfs", rc), 16)
            R.inc(S2("spfp")[rc % 2], e.dma_start(
                out=cf(12288 + (rc % 2) * 2048), in_=Psp[rc]),
                (t, "pfp", rc), 16)
        for hc in range(NK):
            if hc < 2:
                R.wait(e, (t, "ro", NK - 1, 3))
            else:
                R.wait(e, (t, "oub", hc - 2))
                R.wait(e, (t, "sto", hc - 2))
            R.inc(S2("slrl")[hc % 2], e.dma_start(
                out=cf(8192 + (hc % 2) * 2048), in_=loc_sp[hc]),
                (t, "lrl", hc), 16)
            R.inc(S2("sxr")[hc % 2], e.dma_start(
                out=cf(12288 + (hc % 2) * 2048),
                in_=x1f[hc * 128:(hc + 1) * 128, :]), (t, "xr", hc), 16)
            R.wait(e, (t, "oub", hc))
            R.inc(S2("ssto")[hc % 2], e.dma_start(
                out=x2f[hc * 128:(hc + 1) * 128, :],
                in_=cf(8192 + (hc % 2) * 2048)), (t, "sto", hc), 16)

    def attn_gpsimd(e):
        t = "a"
        WJ = [W["w_f"], W["w_i"], W["w_v"]]
        for rc in range(NK):
            for j in range(3):
                if rc >= 2:
                    R.wait(e, (t, "mm", j, (rc - 2) * 4 + 3))
                else:
                    R.wait(e, ("f1", "gmm", NK - 1, 3))
                key = (t, "wl", rc) if j == 2 else None
                R.inc(S2("gwl")[rc % 2], e.dma_start(
                    out=aB[:, ((rc % 2) * 3 + j) * 2048:((rc % 2) * 3 + j) * 2048 + 2048],
                    in_=wsl_src(WJ[j], rc)), key, 16)
        R.wait(e, (t, "ab"))
        R.inc(S1("gcc"), e.dma_start(out=cc_gi, in_=smv("ABs", 32)),
              (t, "ags"), 16)
        R.wait(e, (t, "ags"))
        cc_run(e, "ag", "AllGather", OP.bypass, cc_gi, cc_go)
        R.wait(e, (t, "psp", NK - 2))
        R.wait(e, (t, "psp", NK - 1))
        R.inc(S1("gcc"), e.dma_start(
            out=cfw(6144, 256).rearrange("p (r c) -> p r c", c=32),
            in_=cc_go.rearrange("(r p) c -> p r c", p=128)), (t, "agb"), 16)
        for lc in range(NK):
            for j, w in enumerate([W["w_up"], W["w_gt"]]):
                if lc >= 2:
                    R.wait(e, (t, "hh", (lc - 2) * 4 + 3))
                else:
                    R.wait(e, (t, "apl", NK - 1))
                key = (t, "b1l", lc) if j == 1 else None
                R.inc(S2("gb1l")[lc % 2], e.dma_start(
                    out=cbw(((lc % 2) * 2 + j) * 2048, 2048),
                    in_=wsl_src(w, lc)), key, 16)
        for hc in range(NK):
            if hc >= 2:
                R.wait(e, (t, "dmm", hc - 2, 3))
            else:
                R.wait(e, (t, "hh", NK * 4 - 1))
            R.inc(S2("gb2l")[hc % 2], e.dma_start(
                out=cbw((hc % 2) * 2048, 2048),
                in_=wsl_src(W["w_dn"], hc)), (t, "b2l", hc), 16)
        for rc in range(NK):
            if rc >= 2:
                R.wait(e, (t, "qmm", rc - 2, 3))
            else:
                R.wait(e, (t, "dmm", NK - 1, 3))
            R.inc(S2("gql")[rc % 2], e.dma_start(
                out=cbw(4096 + (rc % 2) * 2048, 2048),
                in_=wsl_src(W["w_q"], rc)), (t, "ql", rc), 16)
        for hc in range(NK):
            if hc >= 2:
                R.wait(e, (t, "rmm", hc - 2, 3))
            else:
                R.wait(e, (t, "ro", NK - 1, 3))
            R.inc(S2("grol")[hc % 2], e.dma_start(
                out=cbw((hc % 2) * 2048, 2048),
                in_=wsl_src(W["w_ro"], hc)), (t, "rol", hc), 16)

    def attn_tensor(e):
        t = "a"
        R.wait(e, ("f1", "cev"))
        for k in range(NK):
            R.wait(e, (t, "cast", k))
            for tq in range(NTQ):
                ins = e.matmul(bank(tq), onesbt[:],
                               aA[:, k * 2048 + tq * 512:k * 2048 + tq * 512 + 512],
                               start=(k == 0), stop=(k == NK - 1))
            if k == NK - 1:
                R.inc(SP, ins, (t, "smm", 0))
        R.wait(e, (t, "evs", 0))
        for k in range(NK):
            R.wait(e, (t, "sq", k))
            for tq in range(NTQ):
                ins = e.matmul(bank(tq), onesbt[:],
                               cbw(8192 + (k % 2) * 2048, 2048)[:, tq * 512:tq * 512 + 512],
                               start=(k == 0), stop=(k == NK - 1))
            R.inc(SP, ins, (t, "s2", k))
        R.alias((t, "smm", 1), (t, "s2", NK - 1))
        for rc in range(NK):
            R.wait(e, (t, "wl", rc))
            if rc == 0:
                R.wait(e, (t, "apl", NK - 1))
            for tq in range(NTQ):
                g = rc * 4 + tq
                for j in range(3):
                    if g >= 2:
                        R.wait(e, [(t, "evf", g - 2), (t, "evi", g - 2),
                                   (t, "evv", g - 2)][j])
                    for k in range(NK):
                        ins = e.matmul(
                            psA[:, (j * 2 + g % 2) * 512:(j * 2 + g % 2) * 512 + 512],
                            aB[:, ((rc % 2) * 3 + j) * 2048 + k * 128:
                               ((rc % 2) * 3 + j) * 2048 + k * 128 + 128],
                            aA[:, k * 2048 + tq * 512:k * 2048 + tq * 512 + 512],
                            start=(k == 0), stop=(k == NK - 1))
                    R.inc(SP, ins, (t, "mm", j, g))
        for lc in range(NK):
            R.wait(e, (t, "b1l", lc))
            for tq in range(NTQ):
                g = lc * 4 + tq
                for j in range(2):
                    if g < 2:
                        R.wait(e, (t, "evf" if j == 0 else "evi",
                                   4 * NK - 2 + g))
                    if g >= 2:
                        R.wait(e, (t, "hh", g - 2))
                    for k in range(NK):
                        ins = e.matmul(
                            psA[:, (j * 2 + g % 2) * 512:(j * 2 + g % 2) * 512 + 512],
                            cbw(((lc % 2) * 2 + j) * 2048 + k * 128, 128),
                            aA[:, k * 2048 + tq * 512:k * 2048 + tq * 512 + 512],
                            start=(k == 0), stop=(k == NK - 1))
                    R.inc(SP, ins, [(t, "umm", lc, tq), (t, "gm2", lc, tq)][j])
        for hc in range(NK):
            R.wait(e, (t, "b2l", hc))
            for tq in range(NTQ):
                g = hc * 4 + tq
                if g >= 2:
                    R.wait(e, (t, "lcp", g - 2))
                for k in range(NK):
                    ins = e.matmul(
                        bank(g % 2),
                        cbw((hc % 2) * 2048 + k * 128, 128),
                        aB[:, k * 2048 + tq * 512:k * 2048 + tq * 512 + 512],
                        start=(k == 0), stop=(k == NK - 1))
                R.inc(SP, ins, (t, "dmm", hc, tq))
        for rc in range(NK):
            R.wait(e, (t, "ql", rc))
            for tq in range(NTQ):
                g = rc * 4 + tq
                if g < 2:
                    R.wait(e, (t, "lcp", 4 * NK - 2 + g))
                if g >= 2:
                    R.wait(e, (t, "ro", (g - 2) // 4, (g - 2) % 4))
                for k in range(NK):
                    ins = e.matmul(
                        bank(g % 2),
                        cbw(4096 + (rc % 2) * 2048 + k * 128, 128),
                        aA[:, k * 2048 + tq * 512:k * 2048 + tq * 512 + 512],
                        start=(k == 0), stop=(k == NK - 1))
                R.inc(SP, ins, (t, "qmm", rc, tq))
        for hc in range(NK):
            R.wait(e, (t, "rol", hc))
            for tq in range(NTQ):
                g = hc * 4 + tq
                if g < 2:
                    R.wait(e, (t, "ro", NK - 1, 2 + g))
                if g >= 2:
                    R.wait(e, (t, "oo", g - 2))
                for k in range(NK):
                    ins = e.matmul(
                        bank(g % 2),
                        cbw((hc % 2) * 2048 + k * 128, 128),
                        aB[:, k * 2048 + tq * 512:k * 2048 + tq * 512 + 512],
                        start=(k == 0), stop=(k == NK - 1))
                R.inc(SP, ins, (t, "rmm", hc, tq))

    def attn_scalar(e):
        t = "a"
        for k in range(NK):
            R.wait(e, (t, "cast", k))
            if k >= 2:
                R.wait(e, (t, "s2", k - 2))
            R.inc(SA, e.activation(out=cbw(8192 + (k % 2) * 2048, 2048),
                                   in_=aA[:, k * 2048:(k + 1) * 2048],
                                   func=AF.Square), (t, "sq", k))
        for rc in range(NK):
            for tq in range(NTQ):
                g = rc * 4 + tq
                R.wait(e, (t, "mm", 0, g))
                if g == 0:
                    R.wait(e, (t, "apl", NK - 1))
                R.inc(SA, e.activation(
                    out=cf((rc % 2) * 2048)[:, tq * 512:tq * 512 + 512],
                    in_=psA[:, (g % 2) * 512:(g % 2) * 512 + 512],
                    func=AF.Tanh), (t, "evf", g))
                R.wait(e, (t, "mm", 1, g))
                if g >= 2:
                    R.wait(e, (t, "evv", g - 2))
                R.inc(SA, e.activation(
                    out=cfw(9728 + (g % 2) * 512, 512),
                    in_=psA[:, (2 + g % 2) * 512:(2 + g % 2) * 512 + 512],
                    func=AF.Sigmoid), (t, "evi", g))
                R.wait(e, (t, "mm", 2, g))
                R.inc(SA, e.activation(
                    out=cfw(8192 + (g % 2) * 512, 512),
                    in_=psA[:, (4 + g % 2) * 512:(4 + g % 2) * 512 + 512],
                    func=AF.Sigmoid), (t, "asv", g))
        for lc in range(NK):
            for tq in range(NTQ):
                g = lc * 4 + tq
                R.wait(e, (t, "gm2", lc, tq))
                if g == 0:
                    R.wait(e, (t, "ssp", NK - 2))
                    R.wait(e, (t, "ssp", NK - 1))
                if g >= 1:
                    R.wait(e, (t, "hh", g - 1))
                R.inc(SA, e.activation(out=cfw(4096, 512),
                                       in_=psA[:, (2 + g % 2) * 512:(2 + g % 2) * 512 + 512],
                                       func=AF.Sigmoid), (t, "b1s", lc, tq))
        for hc in range(NK):
            for tq in range(NTQ):
                g = hc * 4 + tq
                R.wait(e, (t, "dmm", hc, tq))
                if g == 0:
                    R.wait(e, (t, "scn", NK - 1))
                if hc >= 2 and tq == 0:
                    R.wait(e, (t, "lst", hc - 2))
                R.inc(SA, e.copy(
                    out=cf(12288 + (hc % 2) * 2048)[:, tq * 512:tq * 512 + 512],
                    in_=bank(g % 2)), (t, "lcp", g))
            R.alias((t, "lev", hc), (t, "lcp", hc * 4 + 3))
        for rc in range(NK):
            for tq in range(NTQ):
                R.wait(e, (t, "m1", rc, tq))
                R.inc(SA, e.activation(out=cfw(512, 512), in_=cfw(0, 512),
                                       func=AF.Sigmoid), (t, "sg", rc, tq))

    def attn_vector(e):
        t = "a"
        R.wait(e, ("f1", "fbo", NK - 1))
        amu, ars = cf(8192), cf(10240)
        for k in range(NK):
            R.wait(e, (t, "xl", k))
            R.inc(SV, e.tensor_copy(out=aA[:, k * 2048:(k + 1) * 2048],
                                    in_=cf((k % 2) * 2048)), (t, "cast", k))
        R.wait(e, (t, "smm", 0))
        for tq in range(NTQ):
            ins = e.tensor_copy(out=cf(12288)[:, tq * 512:tq * 512 + 512],
                                in_=bank(tq))
        R.inc(SV, ins, (t, "evs", 0))
        R.wait(e, (t, "smm", 1))
        for tq in range(NTQ):
            ins = e.tensor_copy(out=cf(14336)[:, tq * 512:tq * 512 + 512],
                                in_=bank(tq))
        R.inc(SV, ins, (t, "evs", 1))
        e.tensor_scalar(out=amu, in0=cf(12288), scalar1=INV_H, scalar2=None,
                        op0=OP.mult)
        e.tensor_scalar(out=ars, in0=cf(14336), scalar1=INV_H, scalar2=None,
                        op0=OP.mult)
        e.tensor_tensor(out=cf(12288), in0=amu, in1=amu, op=OP.mult)
        e.tensor_tensor(out=cf(14336), in0=ars, in1=cf(12288), op=OP.subtract)
        rsqrt_newton(e, ars, cf(14336), cf(12288), cf(4096), cf(6144), EPS_PRE)
        R.inc(SV, e.nop(), (t, "stat"))
        for k in range(NK):
            R.wait(e, (t, "xl2", k))
            e.tensor_tensor(out=cf((k % 2) * 2048), in0=cf((k % 2) * 2048),
                            in1=amu, op=OP.subtract)
            R.inc(SV, e.tensor_tensor(out=aA[:, k * 2048:(k + 1) * 2048],
                                      in0=cf((k % 2) * 2048), in1=ars,
                                      op=OP.mult), (t, "apl", k))
        R.wait(e, ("f1", "smdump"))
        for rc in range(NK):
            for tq in range(NTQ):
                g = rc * 4 + tq
                R.wait(e, (t, "asv", g))
                e.tensor_tensor(out=cfw(9216, 512),
                                in0=psA[:, (4 + g % 2) * 512:(4 + g % 2) * 512 + 512],
                                in1=cfw(8192 + (g % 2) * 512, 512), op=OP.mult)
                R.wait(e, (t, "evi", g))
                R.inc(SV, e.tensor_tensor(
                    out=cfw(12288 + (rc % 2) * 2048 + tq * 512, 512),
                    in0=cfw(9728 + (g % 2) * 512, 512),
                    in1=cfw(9216, 512), op=OP.mult), (t, "evv", g))
            R.wait(e, (t, "evf", rc * 4 + 3))
            if rc >= 1:
                R.wait(e, (t, "ssp", rc - 1))
                R.wait(e, (t, "psp", rc - 1))
            e.tensor_tensor_scan(out=cf(4096), data0=cf((rc % 2) * 2048),
                                 data1=cf(12288 + (rc % 2) * 2048),
                                 initial=0.0, op0=OP.mult, op1=OP.add)
            e.tensor_copy(out=smc("ABs", 16 + rc), in_=cf(4096)[:, 2047:2048])
            e.tensor_tensor_scan(out=cf(6144), data0=cf((rc % 2) * 2048),
                                 data1=cf((rc % 2) * 2048),
                                 initial=1.0, op0=OP.mult, op1=OP.bypass)
            R.inc(SV, e.tensor_copy(out=smc("ABs", rc),
                                    in_=cf(6144)[:, 2047:2048]), (t, "scn", rc))
        R.inc(SV, e.nop(), (t, "ab"))
        R.wait(e, (t, "agb"))
        ag = cfw(6144, 256).rearrange("p (r c) -> p r c", c=32)
        e.tensor_copy(out=smv("s_in"), in_=smv("initf"))
        for j in range(NCORE):
            e.tensor_scalar(out=smv("ftmp"), in0=ag[:, j, 0:16],
                            scalar1=smc("predm", j), scalar2=smc("predm1", j),
                            op0=OP.mult, op1=OP.add)
            e.tensor_tensor(out=smv("s_in"), in0=smv("s_in"), in1=smv("ftmp"),
                            op=OP.mult)
            e.tensor_scalar(out=smv("ftmp2"), in0=ag[:, j, 16:32],
                            scalar1=smc("predm", j), scalar2=None, op0=OP.mult)
            e.tensor_tensor(out=smv("s_in"), in0=smv("s_in"), in1=smv("ftmp2"),
                            op=OP.add)
        R.inc(SV, e.nop(), (t, "sin"))
        for lc in range(NK):
            for tq in range(NTQ):
                g = lc * 4 + tq
                R.wait(e, (t, "b1s", lc, tq))
                e.tensor_tensor(out=cfw(4608, 512),
                                in0=psA[:, (2 + g % 2) * 512:(2 + g % 2) * 512 + 512],
                                in1=cfw(4096, 512), op=OP.mult)
                R.wait(e, (t, "umm", lc, tq))
                R.inc(SV, e.tensor_tensor(
                    out=aB[:, lc * 2048 + tq * 512:lc * 2048 + tq * 512 + 512],
                    in0=psA[:, (g % 2) * 512:(g % 2) * 512 + 512],
                    in1=cfw(4608, 512), op=OP.mult), (t, "hh", g))
        for rc in range(NK):
            R.wait(e, (t, "pfs", rc))
            R.wait(e, (t, "pfp", rc))
            if rc == 0:
                R.wait(e, (t, "sin"))
            e.tensor_scalar(out=cf(4096), in0=cf(12288 + (rc % 2) * 2048),
                            scalar1=smc("s_in", rc), scalar2=None, op0=OP.mult)
            R.inc(SV, e.tensor_tensor(out=cf(8192 + (rc % 2) * 2048),
                                      in0=cf(4096),
                                      in1=cf(8192 + (rc % 2) * 2048),
                                      op=OP.add), (t, "fix", rc))
            for tq in range(NTQ):
                g = rc * 4 + tq
                R.wait(e, (t, "qmm", rc, tq))
                R.inc(SV, e.tensor_tensor(
                    out=cfw(0, 512), in0=bank(g % 2),
                    in1=cf(8192 + (rc % 2) * 2048)[:, tq * 512:tq * 512 + 512],
                    op=OP.mult), (t, "m1", rc, tq))
                R.wait(e, (t, "sg", rc, tq))
                R.inc(SV, e.tensor_tensor(
                    out=aB[:, rc * 2048 + tq * 512:rc * 2048 + tq * 512 + 512],
                    in0=cfw(0, 512), in1=cfw(512, 512), op=OP.mult),
                    (t, "ro", rc, tq))
        for hc in range(NK):
            R.wait(e, (t, "lrl", hc))
            for tq in range(NTQ):
                g = hc * 4 + tq
                R.wait(e, (t, "rmm", hc, tq))
                R.inc(SV, e.tensor_tensor(
                    out=cf(8192 + (hc % 2) * 2048)[:, tq * 512:tq * 512 + 512],
                    in0=bank(g % 2),
                    in1=cf(8192 + (hc % 2) * 2048)[:, tq * 512:tq * 512 + 512],
                    op=OP.add), (t, "oo", g))
            R.wait(e, (t, "xr", hc))
            R.inc(SV, e.tensor_tensor(out=cf(8192 + (hc % 2) * 2048),
                                      in0=cf(8192 + (hc % 2) * 2048),
                                      in1=cf(12288 + (hc % 2) * 2048),
                                      op=OP.add), (t, "oub", hc))
        R.alias(("a", "fbo", NK - 1), ("a", "oub", NK - 1))

    def tout_sync(e, Xfin, fin_evt):
        for tc in range(NK):
            for ev in fin_evt:
                R.wait(e, ev)
            if tc >= 2:
                R.wait(e, ("to", "pe", tc - 2, 3))
            src = Xfin[:, tc * 128:(tc + 1) * 128].rearrange(
                "(fc p) m -> p fc m", p=128)
            dst = cf((tc % 2) * 2048).rearrange("p (fc m) -> p fc m", m=128)
            R.inc(S2("sld")[tc % 2], e.dma_start(out=dst, in_=src),
                  ("to", "ld", tc), 16)
            R.wait(e, ("to", "ev", tc, 3))
            R.inc(S2("sst")[tc % 2], e.dma_start(
                out=out_o[tc * 128:(tc + 1) * 128, :],
                in_=cf(4096 + (tc % 2) * 2048)), ("to", "st", tc), 16)
        R.wait(e, ("to", "st", NK - 2))
        R.wait(e, ("to", "st", NK - 1))

    def tout_tensor(e, bank_evt):
        R.wait(e, bank_evt)
        for tc in range(NK):
            R.wait(e, ("to", "ld", tc))
            for fc in range(NK):
                g4 = fc // 4
                gidx = tc * 4 + g4
                if gidx >= 2:
                    R.wait(e, ("to", "ev", (gidx - 2) // 4, (gidx - 2) % 4))
                ins = e.transpose(
                    bank(g4 % 2)[:, (fc % 4) * 128:(fc % 4) * 128 + 128],
                    cf((tc % 2) * 2048)[:, fc * 128:fc * 128 + 128],
                    ident[:])
                if fc % 4 == 3:
                    R.inc(SP, ins, ("to", "pe", tc, g4))

    def tout_vector(e):
        for tc in range(NK):
            for g4 in range(4):
                R.wait(e, ("to", "pe", tc, g4))
                if tc >= 2 and g4 == 0:
                    R.wait(e, ("to", "st", tc - 2))
                R.inc(SV, e.tensor_copy(
                    out=cf(4096 + (tc % 2) * 2048)[:, g4 * 512:g4 * 512 + 512],
                    in_=bank(g4 % 2)), ("to", "ev", tc, g4))

    f1 = make_ffn("f1", xf0, x1f, in_evt=lambda k: ("ti", "st", k),
                  bank_evt=("ti", "ev", NK - 1, 3), gw_evt=("f1", "gw"),
                  arena_evt=("ti", "pc"))
    f1["scr_evt"] = [("ti", "st", NK - 2), ("ti", "st", NK - 1)]
    f2 = make_ffn("f2", x2f, x3f, in_evt=lambda k: ("a", "sto", k),
                  bank_evt=("a", "oub", NK - 1), gw_evt=None,
                  arena_evt=("a", "oub", NK - 1))
    f1["gw_key"] = ("f1", "gw")

    if stage == "id":
        tout_x, tout_fin, tout_bank = xf0, \
            [("ti", "st", NK - 2), ("ti", "st", NK - 1)], ("ti", "ev", NK - 1, 3)
    elif stage == "f1":
        tout_x, tout_fin, tout_bank = x1f, \
            [("f1", "sto", NK - 2), ("f1", "sto", NK - 1)], ("f1", "cev")
    elif stage == "f1a":
        tout_x, tout_fin, tout_bank = x2f, \
            [("a", "sto", NK - 2), ("a", "sto", NK - 1)], ("a", "oub", NK - 1)
    else:
        tout_x, tout_fin, tout_bank = x3f, \
            [("f2", "sto", NK - 2), ("f2", "sto", NK - 1)], ("f2", "cev")

    def emit_sync(e):
        tin_sync(e)
        if stage != "id":
            ffn_sync(f1, e)
        if stage in ("f1a", "full"):
            attn_sync(e)
        if stage == "full":
            ffn_sync(f2, e)
        tout_sync(e, tout_x, tout_fin)

    def emit_gpsimd(e):
        tin_gpsimd(e)
        if stage != "id":
            ffn_gpsimd(f1, e)
        if stage in ("f1a", "full"):
            attn_gpsimd(e)
        if stage == "full":
            ffn_gpsimd(f2, e)

    def emit_tensor(e):
        tin_tensor(e)
        if stage != "id":
            ffn_tensor(f1, e)
        if stage in ("f1a", "full"):
            attn_tensor(e)
        if stage == "full":
            ffn_tensor(f2, e)
        tout_tensor(e, tout_bank)

    def emit_scalar(e):
        e = Ser(e, SA, R)
        tin_scalar(e)
        if stage != "id":
            ffn_scalar(f1, e)
        if stage in ("f1a", "full"):
            attn_scalar(e)
        if stage == "full":
            ffn_scalar(f2, e)

    def emit_vector(e):
        e = Ser(e, SV, R)
        tin_vector(e)
        if stage != "id":
            ffn_vector(f1, e)
        if stage in ("f1a", "full"):
            attn_vector(e)
        if stage == "full":
            ffn_vector(f2, e)
        tout_vector(e)

    emits = [emit_sync, emit_gpsimd, emit_tensor, emit_scalar, emit_vector]

    R.plan = True
    for fn in emits:
        fn(DummyEng())
    R.check()
    plan_counts = [(s, s.n) for s in allsems]
    for s in allsems:
        s.n = 0

    R.plan = False
    with ExitStack() as sctx:
        for s in allsems:
            s.h = sctx.enter_context(nc.semaphore(s.name))
        with nc.Block() as block:
            block.sync(emit_sync)
            block.gpsimd(emit_gpsimd)
            block.tensor(emit_tensor)
            block.scalar(emit_scalar)
            block.vector(emit_vector)
        for s, n0 in plan_counts:
            assert s.n == n0, f"pass drift on {s.name}: {s.n} != {n0}"
        ctx.close()
    return nc


_NC_CACHE = {}


def make_inmaps(inputs):
    g = lambda n: np.asarray(inputs[n], dtype=np.float32)
    x = g("L0_attn_x")
    import ml_dtypes
    bf = ml_dtypes.bfloat16

    def fmv16(v):
        return np.ascontiguousarray(v.reshape(16, 128).T)

    def fmv32(v):
        base = v.reshape(16, 128).T
        return np.ascontiguousarray(np.repeat(base, 2, axis=1))

    def aperm(v):
        return np.ascontiguousarray(
            v.reshape(16, 128, 16, 128).transpose(2, 1, 0, 3).reshape(2048, 2048))

    full = {
        "w_f": aperm(g("L0_attn_w_f")), "w_i": aperm(g("L0_attn_w_i")),
        "w_v": aperm(g("L0_attn_w_v")), "w_q": aperm(g("L0_attn_w_q")),
        "w_ro": aperm(g("L0_attn_w_ro")), "w_up": aperm(g("L0_attn_w_up")),
        "w_gt": aperm(g("L0_attn_w_gate")), "w_dn": aperm(g("L0_attn_w_down")),
        "agg_w": g("L0_ffn_agg_w"), "bc_w": g("L0_ffn_bc_w"),
        "gate_w": g("L0_ffn_gate_w"),
        "ingam": fmv16(g("L0_ffn_in_gamma")), "inbet": fmv16(g("L0_ffn_in_beta")),
        "outgam": fmv16(g("L0_ffn_out_gamma")),
        "outbet": fmv16(g("L0_ffn_out_beta")),
        "gateb": fmv16(g("L0_ffn_gate_b")),
        "initf": fmv16(g("L0_attn_init_state")),
        "aggb": fmv32(g("L0_ffn_agg_b")), "bcb": fmv32(g("L0_ffn_bc_b")),
        "d2b": fmv32(g("L0_ffn_d2_b")),
        "png": fmv16(g("L0_ffn_pn_gamma")), "pnb": fmv16(g("L0_ffn_pn_beta")),
        "idn32": np.eye(128, dtype=np.float32),
        "idnb": np.eye(128, dtype=np.float32).astype(bf),
        "onesb": np.ones((128, 128), dtype=np.float32).astype(bf),
    }
    d1w, d2w, d1b = g("L0_ffn_d1_w"), g("L0_ffn_d2_w"), g("L0_ffn_d1_b")

    in_maps = []
    for c in range(NCORE):
        b, s = c // 4, c % 4
        m = dict(full)
        m["xs"] = np.ascontiguousarray(x[b, s * T:(s + 1) * T, :])
        m["d1s"] = np.ascontiguousarray(d1w[:, c * 1024:(c + 1) * 1024])
        m["d2s"] = np.ascontiguousarray(d2w[c * 1024:(c + 1) * 1024, :])
        m["d1bs"] = np.ascontiguousarray(
            np.tile(d1b[c * 1024:(c + 1) * 1024], (2, 1)))
        own = 1.0 if s == 3 else 0.0
        m["mkb0"] = np.full((128, 1), own if b == 0 else 0.0, np.float32)
        m["mkb1"] = np.full((128, 1), own if b == 1 else 0.0, np.float32)
        m["mb0"] = np.full((128, 1), 1.0 if b == 0 else 0.0, np.float32)
        m["mb1"] = np.full((128, 1), 1.0 if b == 1 else 0.0, np.float32)
        pm = np.zeros((128, 8), np.float32)
        for j in range(NCORE):
            if j // 4 == b and j % 4 < s:
                pm[:, j] = 1.0
        m["predm"] = pm
        m["predm1"] = 1.0 - pm
        in_maps.append(m)
    return in_maps


def kernel(**inputs):
    in_maps = make_inmaps(inputs)
    nc = build_nc(STAGE)
    res = run_bass_kernel_spmd(nc, in_maps, core_ids=list(range(NCORE)))
    out = np.empty((2, 8192, 2048), np.float32)
    for c in range(NCORE):
        b, s = c // 4, c % 4
        out[b, s * T:(s + 1) * T, :] = res.results[c]["o"]
    return out


if __name__ == "__main__":
    nc = build_nc(STAGE)
    print("build ok")


# version 08 (final)
# speedup vs baseline: 1.0090x; 1.0035x over previous
import numpy as np
from contextlib import ExitStack

import concourse.bass as bass
import concourse.mybir as mybir
from concourse.bass_utils import run_bass_kernel_spmd

F32 = mybir.dt.float32
BF16 = mybir.dt.bfloat16
I32 = mybir.dt.int32
AF = mybir.ActivationFunctionType
OP = mybir.AluOpType

NCORE = 8
T = 2048
NK = 16
NTQ = 4
EPS_PRE = 1e-5
EPS_FFN = 1e-6
RG = [[0, 1, 2, 3, 4, 5, 6, 7]]
MAGIC = 0x5F3759DF + 1
INV_H = 1.0 / 2048.0

STAGE = "full"
DUMPS = []

TH_SEG = {1: "s0", 4: "s1", 7: "s2", 10: "s3", 13: "bc"}


class Sem:
    def __init__(self, name):
        self.name = name
        self.h = None
        self.n = 0


class DummyIns:
    def then_inc(self, *a, **k):
        return self


class DummyEng:
    def __getattr__(self, name):
        return lambda *a, **k: DummyIns()


class Rec:
    def __init__(self):
        self.ev = {}
        self.needed = set()
        self.plan = True

    serialized = ()

    def inc(self, sem, ins, key=None, v=1):
        if sem in self.serialized:
            if key is not None:
                if self.plan:
                    assert key not in self.ev, f"dup event {key}"
                    self.ev[key] = (sem, sem.n)
                else:
                    assert self.ev[key] == (sem, sem.n), f"replay drift at {key}"
            return ins
        sem.n += v
        if self.plan:
            if key is not None:
                assert key not in self.ev, f"dup event {key}"
                self.ev[key] = (sem, sem.n)
        else:
            ins.then_inc(sem.h, v)
            if key is not None:
                assert self.ev[key] == (sem, sem.n), f"replay drift at {key}"
        return ins

    def alias(self, key, src):
        if self.plan:
            assert key not in self.ev, key
            self.ev[key] = self.ev[src]

    def wait(self, e, key):
        if self.plan:
            self.needed.add(key)
        else:
            sem, val = self.ev[key]
            e.wait_ge(sem.h, val)

    def check(self):
        missing = self.needed - set(self.ev)
        assert not missing, \
            f"waits on unrecorded events: {sorted(map(str, missing))[:10]}"


def rsqrt_newton(e, out_f, src, scrA, scrB, scrC, eps, n_iter=3):
    e.tensor_scalar(out=scrB, in0=src, scalar1=eps, scalar2=-0.5,
                    op0=OP.add, op1=OP.mult)
    e.tensor_scalar(out=scrA, in0=scrB, scalar1=-2.0, scalar2=None,
                    op0=OP.mult)
    ai = scrA.bitcast(I32)
    oi = out_f.bitcast(I32)
    e.tensor_scalar(out=oi, in0=ai, scalar1=1, scalar2=None,
                    op0=OP.logical_shift_right)
    e.tensor_scalar(out=ai, in0=oi, scalar1=-1, scalar2=None,
                    op0=OP.bitwise_xor)
    e.tensor_scalar(out=oi, in0=ai, scalar1=MAGIC, scalar2=None, op0=OP.add)
    ins = None
    for _ in range(n_iter):
        e.tensor_tensor(out=scrA, in0=out_f, in1=out_f, op=OP.mult)
        e.tensor_tensor(out=scrA, in0=scrA, in1=scrB, op=OP.mult)
        e.tensor_scalar(out=scrC, in0=scrA, scalar1=1.5, scalar2=None, op0=OP.add)
        ins = e.tensor_tensor(out=out_f, in0=out_f, in1=scrC, op=OP.mult)
    return ins


SERIAL_OPS = {
    "tensor_copy", "tensor_tensor", "tensor_scalar", "tensor_single_scalar",
    "tensor_add", "tensor_sub", "tensor_mul", "tensor_max", "tensor_relu",
    "scalar_tensor_tensor", "tensor_tensor_scan", "tensor_reduce",
    "reduce_max", "reduce_sum", "memset", "memzero", "bn_stats", "bn_aggr",
    "reciprocal", "activation", "copy", "square", "sqrt", "mul", "add",
    "stream_shuffle", "select", "transpose",
}


class Ser:

    def __init__(self, e, sem, R):
        self._e, self._sem, self._R = e, sem, R

    def __getattr__(self, name):
        f = getattr(self._e, name)
        if name not in SERIAL_OPS:
            return f
        def g(*a, **k):
            if not self._R.plan:
                self._e.wait_ge(self._sem.h, self._sem.n)
            ins = f(*a, **k)
            self._sem.n += 1
            if not self._R.plan:
                ins.then_inc(self._sem.h, 1)
            return ins
        return g


def build_nc(stage):
    nc = bass.Bass("TRN2", target_bir_lowering=False)
    dtr = nc.dram_tensor
    ein, eout = "ExternalInput", "ExternalOutput"

    xs = dtr("xs", [T, 2048], F32, kind=ein).ap()
    W = {n: dtr(n, [2048, 2048], F32, kind=ein).ap()
         for n in ["w_f", "w_i", "w_v", "w_q", "w_ro", "w_up", "w_gt", "w_dn",
                   "agg_w", "bc_w", "gate_w"]}
    d1s = dtr("d1s", [2048, 1024], F32, kind=ein).ap()
    d2s = dtr("d2s", [1024, 2048], F32, kind=ein).ap()
    d1bs = dtr("d1bs", [2, 1024], F32, kind=ein).ap()
    FMW = dict(ingam=16, inbet=16, outgam=16, outbet=16, gateb=16, initf=16,
               aggb=32, bcb=32, d2b=32, png=16, pnb=16)
    fmv = {n: dtr(n, [128, w], F32, kind=ein).ap() for n, w in FMW.items()}
    msk = {n: dtr(n, [128, 1], F32, kind=ein).ap()
           for n in ["mkb0", "mkb1", "mb0", "mb1"]}
    predm = dtr("predm", [128, 8], F32, kind=ein).ap()
    predm1 = dtr("predm1", [128, 8], F32, kind=ein).ap()
    idn32 = dtr("idn32", [128, 128], F32, kind=ein).ap()
    idnb = dtr("idnb", [128, 128], BF16, kind=ein).ap()
    onesb = dtr("onesb", [128, 128], BF16, kind=ein).ap()
    out_o = dtr("o", [T, 2048], F32, kind=eout).ap()
    dump_sm = {t: dtr(f"dump_sm_{t}", [128, 592], F32, kind=eout).ap()
               for t in ("f1", "f2", "a")}
    dump_smb = dtr("dump_smb", [128, 768], BF16, kind=eout).ap()

    def idtr(name, shape, dt=F32):
        kind = "ExternalOutput" if name in DUMPS else "Internal"
        return dtr(name, shape, dt, kind=kind).ap()

    xf0 = idtr("xf0", [2048, T])
    x1f = idtr("x1f", [2048, T])
    x2f = idtr("x2f", [2048, T])
    x3f = idtr("x3f", [2048, T])
    Ssp = idtr("Ssp", [NK, 128, T])
    Psp = idtr("Psp", [NK, 128, T])
    xpre_sp = idtr("xpre_sp", [NK, 128, T])
    u_sp = idtr("u_sp", [NK, 128, T])
    gs_sp = idtr("gs_sp", [NK, 128, T])
    loc_sp = idtr("loc_sp", [NK, 128, T])
    d1bf = dtr("d1bf", [NK, 128, 1024], BF16).ap()
    d2bf = dtr("d2bf", [8, 128, 2048], BF16).ap()
    cc_bi = {t: dtr(f"cc_bi_{t}", [128, 32], F32).ap() for t in ("f1", "f2")}
    cc_bo = {t: dtr(f"cc_bo_{t}", [128, 32], F32).ap() for t in ("f1", "f2")}
    cc_ti = {t: dtr(f"cc_ti_{t}", [2, 2048], F32).ap() for t in ("f1", "f2")}
    cc_to = {t: dtr(f"cc_to_{t}", [2, 2048], F32).ap() for t in ("f1", "f2")}
    cc_gi = dtr("cc_gi", [128, 32], F32).ap()
    cc_go = dtr("cc_go", [NCORE * 128, 32], F32).ap()

    ctx = ExitStack()
    sbt = lambda n, s, d: ctx.enter_context(nc.sbuf_tensor(n, s, d))
    aA = sbt("aA", [128, 32768], BF16)
    aB = sbt("aB", [128, 32768], BF16)
    aC = sbt("aC", [128, 32768], BF16)
    d1brow = sbt("d1brow", [2, 1024], F32)
    sm = sbt("sm", [128, 592], F32)
    smb = sbt("smb", [128, 768], BF16)
    ident = sbt("ident", [128, 128], F32)
    identb = sbt("identb", [128, 128], BF16)
    onesbt = sbt("onesbt", [128, 128], BF16)
    psA = ctx.enter_context(nc.psum_tensor("psA", [128, 3584], F32))
    psB = ctx.enter_context(nc.psum_tensor("psB", [128, 1024], BF16))

    aCf = aC[:].bitcast(F32)
    psBf = psB[:].bitcast(F32)

    def bank(i):
        return psA[:, i * 512:(i + 1) * 512]

    def cf(a):
        return aCf[:, a:a + 2048]

    def cfw(a, w):
        return aCf[:, a:a + w]

    def cbw(a, w):
        return aC[:, a:a + w]

    SM = dict(lastX=0, xpl=16, xnorm=32, ccst=48, xnall=80, s_in=112, ABs=128,
              g2=160, b2=176, thf=192, bcf=224, bcmy=256, ftmp=272, ftmp2=288,
              mkb0=304, mkb1=305, mb0=306, mb1=307, predm=308, predm1=316,
              ingam=324, inbet=340, outgam=356, outbet=372, gateb=388,
              initf=404, aggb=420, bcb=452, d2b=484, png=516, pnb=532,
              tln=548, gbh=564)

    def smv(name, w=16):
        o = SM[name]
        return sm[:, o:o + w]

    def smc(name, j=0):
        return sm[:, SM[name] + j:SM[name] + j + 1]

    def smbv(o, w):
        return smb[:, o:o + w]

    def evenodd(ap32, b):
        return ap32.rearrange("p (c b) -> p c b", b=2)[:, :, b]

    bmw = aCf[0:2, 4096:6144]

    SP, SA, SV = (Sem(n) for n in ["SP", "SA", "SV"])
    CCs = {n: Sem("cc_" + n) for n in
           ["f1b", "f1s0", "f1s1", "f1s2", "f1s3",
            "f2b", "f2s0", "f2s1", "f2s2", "f2s3", "ag"]}
    _pools = {}

    def S2(name):
        if name not in _pools:
            _pools[name] = (Sem(name + "0"), Sem(name + "1"))
        return _pools[name]

    def S1(name):
        if name not in _pools:
            _pools[name] = Sem(name)
        return _pools[name]

    for nm in ["sld", "sst", "sxl", "sxl2", "sxps", "sgst", "sgl", "sxp2",
               "sust", "sul3", "sxl3", "ssto", "sssp", "spsp", "slst",
               "spfs", "spfp", "sxr", "slrl", "gwl", "gagl", "gdsl", "gbcl",
               "gql", "gb1l", "gb2l", "grol"]:
        S2(nm)
    for nm in ["scl", "gstg", "ggw", "gcc"]:
        S1(nm)
    allsems = [SP, SA, SV] + list(CCs.values())
    for v in _pools.values():
        if isinstance(v, tuple):
            allsems.extend(v)
        else:
            allsems.append(v)

    R = Rec()
    R.serialized = (SV, SA)

    def wsl_src(w, rc):
        return w[rc * 128:(rc + 1) * 128, :]

    def cc_run(e, name, kind, op, i, o):
        ins = e.collective_compute(kind, op, replica_groups=RG,
                                   ins=[i.opt()], outs=[o.opt()])
        if not R.plan:
            ins.then_inc(CCs[name].h)
            e.wait_ge(CCs[name].h, 1)

    def tin_sync(e):
        pairs = [(ident[:], idn32), (identb[:], idnb), (onesbt[:], onesb),
                 (smc("mkb0"), msk["mkb0"]), (smc("mkb1"), msk["mkb1"]),
                 (smc("mb0"), msk["mb0"]), (smc("mb1"), msk["mb1"]),
                 (smv("predm", 8), predm), (smv("predm1", 8), predm1),
                 (d1brow[:], d1bs)]
        for n in FMW:
            pairs.append((smv(n, FMW[n]), fmv[n]))
        R.wait(e, ("ti", "sminit"))
        for i, (dst, src) in enumerate(pairs):
            key = ("ti", "cl") if i == len(pairs) - 1 else None
            R.inc(S1("scl"), e.dma_start(out=dst, in_=src), key, 16)
        for hc in range(NK):
            par = hc % 2
            if hc < 2:
                R.wait(e, ("ti", "cl"))
            else:
                R.wait(e, ("ti", "pe", hc - 2, 3))
            src = xs[:, hc * 128:(hc + 1) * 128].rearrange(
                "(tc p) m -> p tc m", p=128)
            dst = cf(par * 2048).rearrange("p (tc m) -> p tc m", m=128)
            R.inc(S2("sld")[hc % 2], e.dma_start(out=dst, in_=src),
                  ("ti", "ld", hc), 16)
            R.wait(e, ("ti", "ev", hc, 3))
            R.inc(S2("sst")[hc % 2], e.dma_start(
                out=xf0[hc * 128:(hc + 1) * 128, :],
                in_=cf(4096 + par * 2048)), ("ti", "st", hc), 16)

    def tin_gpsimd(e):
        sg = S1("gstg")
        for k in range(NK):
            R.inc(sg, e.dma_start(out=aB[:, 0:1024],
                                  in_=d1s[k * 128:(k + 1) * 128, :]), None, 16)
            if not R.plan:
                e.wait_ge(sg.h, sg.n)
            R.inc(sg, e.dma_start(out=d1bf[k], in_=aB[:, 0:1024]), None, 16)
            if not R.plan:
                e.wait_ge(sg.h, sg.n)
        for k in range(8):
            R.inc(sg, e.dma_start(out=aB[:, 0:2048],
                                  in_=d2s[k * 128:(k + 1) * 128, :]), None, 16)
            if not R.plan:
                e.wait_ge(sg.h, sg.n)
            key = ("ti", "stg") if k == 7 else None
            R.inc(sg, e.dma_start(out=d2bf[k], in_=aB[:, 0:2048]), key, 16)
            if not R.plan:
                e.wait_ge(sg.h, sg.n)
        for k in range(NK):
            key = ("f1", "gw") if k == NK - 1 else None
            R.inc(S1("ggw"), e.dma_start(
                out=aB[:, k * 2048:(k + 1) * 2048],
                in_=W["gate_w"][k * 128:(k + 1) * 128, :]), key, 16)

    def tin_tensor(e):
        for hc in range(NK):
            par = hc % 2
            R.wait(e, ("ti", "ld", hc))
            for tc in range(NK):
                g4 = tc // 4
                gidx = hc * 4 + g4
                if gidx >= 2:
                    R.wait(e, ("ti", "ev", (gidx - 2) // 4, (gidx - 2) % 4))
                ins = e.transpose(
                    bank(g4 % 2)[:, (tc % 4) * 128:(tc % 4) * 128 + 128],
                    cf(par * 2048)[:, tc * 128:tc * 128 + 128],
                    ident[:])
                if tc % 4 == 3:
                    R.inc(SP, ins, ("ti", "pe", hc, g4))

    def tin_scalar(e):
        pass

    def tin_vector(e):
        e.memset(sm[:], 0.0)
        e.memset(smb[:], 0.0)
        R.inc(SV, e.nop(), ("ti", "sminit"))
        R.wait(e, ("ti", "cl"))
        e.tensor_scalar(out=smv("g2"), in0=smv("outgam"), scalar1=0.5,
                        scalar2=None, op0=OP.mult)
        e.tensor_scalar(out=smv("b2"), in0=smv("outbet"),
                        scalar1=0.5, scalar2=None, op0=OP.mult)
        R.inc(SV, e.tensor_scalar(out=smv("gbh"), in0=smv("gateb"),
                                  scalar1=0.5, scalar2=None, op0=OP.mult),
              ("ti", "pc"))
        for hc in range(NK):
            par = hc % 2
            for g4 in range(4):
                R.wait(e, ("ti", "pe", hc, g4))
                if hc >= 2 and g4 == 0:
                    R.wait(e, ("ti", "st", hc - 2))
                R.inc(SV, e.tensor_copy(
                    out=cf(4096 + par * 2048)[:, g4 * 512:g4 * 512 + 512],
                    in_=bank(g4 % 2)), ("ti", "ev", hc, g4))

    def make_ffn(t, X, O, in_evt, bank_evt, gw_evt, arena_evt):
        P = {}
        P["t"], P["X"], P["O"] = t, X, O
        P["in_evt"] = in_evt
        P["bank_evt"] = bank_evt
        P["gw_evt"] = gw_evt
        P["arena_evt"] = arena_evt
        P["xch"] = lambda p: cf(p * 2048)
        P["sqs"] = lambda p: cbw(8192 + p * 2048, 2048)
        P["dscr"] = cf(4096)
        P["mu"] = cf(6144)
        P["rs"] = cf(8192)
        P["gss"] = lambda p: cf(12288 + p * 2048)
        P["wsl2"] = lambda n: cbw(20480 + (n % 2) * 2048, 2048)
        P["ub"] = lambda p: aA[:, p * 2048:p * 2048 + 2048]
        P["ubsq"] = lambda p: P["sqs"](p)
        seg = []
        seg.append(("th", "agg"))
        for rc in range(NK):
            seg.append(("gate", rc))
            if rc in TH_SEG:
                seg.append(("th", TH_SEG[rc]))
        P["seg"] = seg
        return P

    def ffn_sync(P, e):
        t, X, O = P["t"], P["X"], P["O"]
        for k in range(NK):
            R.wait(e, P["in_evt"](k))
            if k == 0:
                R.wait(e, P["arena_evt"])
            if k >= 2:
                R.wait(e, (t, "cast", k - 2))
            R.inc(S2("sxl")[k % 2], e.dma_start(
                out=P["xch"](k % 2), in_=X[k * 128:(k + 1) * 128, :]),
                (t, "xl", k), 16)
        for k in range(NK):
            R.wait(e, (t, "cast", NK - 2 + k) if k < 2 else (t, "apl", k - 2))
            if k < 2:
                R.wait(e, (t, "stat"))
            R.inc(S2("sxl2")[k % 2], e.dma_start(
                out=P["xch"](k % 2), in_=X[k * 128:(k + 1) * 128, :]),
                (t, "xl2", k), 16)
            R.wait(e, (t, "apl", k))
            R.inc(S2("sxps")[k % 2], e.dma_start(
                out=xpre_sp[k], in_=P["gss"](k % 2)), (t, "xps", k), 16)
        for rc in range(NK):
            R.wait(e, (t, "gse", rc))
            R.inc(S2("sgst")[rc % 2], e.dma_start(
                out=gs_sp[rc], in_=P["gss"](rc % 2)), (t, "gst", rc), 16)
        for rc in range(NK):
            if rc >= 2:
                R.wait(e, (t, "ubc", rc - 2))
                R.wait(e, (t, "ust", rc - 2))
            else:
                R.wait(e, (t, "gst", NK - 2 + rc))
            R.inc(S2("sgl")[rc % 2], e.dma_start(
                out=P["xch"](rc % 2), in_=gs_sp[rc]), (t, "gl", rc), 16)
            R.inc(S2("sxp2")[rc % 2], e.dma_start(
                out=P["gss"](rc % 2), in_=xpre_sp[rc]), (t, "xpl2", rc), 16)
            R.wait(e, (t, "uc", rc))
            R.inc(S2("sust")[rc % 2], e.dma_start(
                out=u_sp[rc], in_=P["gss"](rc % 2)), (t, "ust", rc), 16)
        for rc in range(NK):
            if rc >= 2:
                R.wait(e, (t, "sto", rc - 2))
            else:
                R.wait(e, (t, "cst"))
                R.wait(e, (t, "ust", NK - 2 + rc))
            R.inc(S2("sul3")[rc % 2], e.dma_start(
                out=P["xch"](rc % 2), in_=u_sp[rc]), (t, "ul3", rc), 16)
            R.inc(S2("sxl3")[rc % 2], e.dma_start(
                out=P["gss"](rc % 2), in_=X[rc * 128:(rc + 1) * 128, :]),
                (t, "xl3", rc), 16)
            R.wait(e, (t, "fbo", rc))
            R.inc(S2("ssto")[rc % 2], e.dma_start(
                out=O[rc * 128:(rc + 1) * 128, :], in_=P["xch"](rc % 2)),
                (t, "sto", rc), 16)

    def ffn_gpsimd(P, e):
        t = P["t"]
        if P["gw_evt"] is None:
            R.wait(e, P["arena_evt"])
            for k in range(NK):
                key = (t, "gw") if k == NK - 1 else None
                R.inc(S1("ggw"), e.dma_start(
                    out=aB[:, k * 2048:(k + 1) * 2048],
                    in_=W["gate_w"][k * 128:(k + 1) * 128, :]), key, 16)
            gw_evt = (t, "gw")
        else:
            gw_evt = P["gw_evt"]
        P["gw_key"] = gw_evt
        for k in range(2):
            R.wait(e, (t, "evs", 1))
            R.inc(S2("gagl")[k % 2], e.dma_start(
                out=P["wsl2"](k),
                in_=W["agg_w"][k * 128:(k + 1) * 128, :]), (t, "agl", k), 16)
        R.wait(e, (t, "lnl"))
        if t == "f1":
            R.inc(S1("gcc"), e.dma_start(out=dump_sm["a"], in_=sm[:]),
                  (t, "smdump0"), 16)
            R.inc(S1("gcc"), e.dma_start(out=dump_smb, in_=smb[:]),
                  (t, "smdump0b"), 16)
        R.inc(S1("gcc"), e.dma_start(out=cc_bi[t], in_=smv("ccst", 32)),
              (t, "ccs"), 16)
        R.wait(e, (t, "ccs"))
        cc_run(e, t + "b", "AllReduce", OP.add, cc_bi[t], cc_bo[t])
        R.inc(S1("gcc"), e.dma_start(out=smv("xnall", 32), in_=cc_bo[t]),
              (t, "ccb_back"), 16)
        for k in range(2, NK):
            R.wait(e, (t, "ak", k - 2))
            R.inc(S2("gagl")[k % 2], e.dma_start(
                out=P["wsl2"](k),
                in_=W["agg_w"][k * 128:(k + 1) * 128, :]), (t, "agl", k), 16)
        for s in range(4):
            for k in range(NK):
                n = s * 24 + k
                if k >= 2:
                    R.wait(e, (t, "dk", s, k - 2))
                elif s > 0:
                    R.wait(e, (t, "dk", s - 1, 22 + k))
                else:
                    R.wait(e, (t, "ak", 14 + k))
                R.inc(S2("gdsl")[n % 2], e.dma_start(
                    out=P["wsl2"](NK + n)[:, 0:1024], in_=d1bf[k]),
                    (t, "dsl", s, k), 16)
            for k in range(8):
                n = s * 24 + 16 + k
                R.wait(e, (t, "dk", s, 14 + k))
                R.inc(S2("gdsl")[n % 2], e.dma_start(
                    out=P["wsl2"](NK + n), in_=d2bf[k]),
                    (t, "dsl", s, 16 + k), 16)
            R.wait(e, (t, "pev", "d2", s))
            R.inc(S1("gcc"), e.dma_start(out=cc_ti[t], in_=bmw),
                  (t, "ars", s), 16)
            R.wait(e, (t, "ars", s))
            cc_run(e, t + "s%d" % s, "AllReduce", OP.add, cc_ti[t], cc_to[t])
            R.inc(S1("gcc"), e.dma_start(out=bmw, in_=cc_to[t]),
                  (t, "arb", s), 16)
        for k in range(NK):
            n = NK + 4 * 24 + k
            if k >= 2:
                R.wait(e, (t, "bk", k - 2))
            else:
                R.wait(e, (t, "dk", 3, 22 + k))
            R.inc(S2("gbcl")[n % 2], e.dma_start(
                out=P["wsl2"](n),
                in_=W["bc_w"][k * 128:(k + 1) * 128, :]), (t, "bcl", k), 16)
        R.wait(e, (t, "bcr"))
        R.inc(S1("gcc"), e.dma_start(out=dump_sm[t], in_=sm[:]),
              (t, "smdump"), 16)

    def ffn_tensor(P, e):
        t = P["t"]
        R.wait(e, P["bank_evt"])
        for k in range(NK):
            R.wait(e, (t, "cast", k))
            for tq in range(NTQ):
                ins = e.matmul(bank(tq), onesbt[:],
                               aA[:, k * 2048 + tq * 512:k * 2048 + tq * 512 + 512],
                               start=(k == 0), stop=(k == NK - 1))
            if k == NK - 1:
                R.inc(SP, ins, (t, "smm", 0))
        R.wait(e, (t, "evs", 0))
        for k in range(NK):
            R.wait(e, (t, "sq", k))
            for tq in range(NTQ):
                ins = e.matmul(bank(tq), onesbt[:],
                               P["sqs"](k % 2)[:, tq * 512:tq * 512 + 512],
                               start=(k == 0), stop=(k == NK - 1))
            R.inc(SP, ins, (t, "s2", k))
        R.alias((t, "smm", 1), (t, "s2", NK - 1))
        R.wait(e, (t, "lnc1"))
        e.matmul(psA[:, 4 * 512:4 * 512 + 16], onesbt[:], smbv(384, 16),
                 start=True, stop=True)
        R.wait(e, (t, "lnc2"))
        R.inc(SP, e.matmul(psA[:, 4 * 512 + 16:4 * 512 + 32], onesbt[:],
                           smbv(640, 16), start=True, stop=True), (t, "lnmm"))
        for item in P["seg"]:
            if item[0] == "gate":
                ffn_t_gate(P, e, item[1])
            else:
                ffn_t_th(P, e, item[1])
        ffn_t_fac(P, e)

    def ffn_t_gate(P, e, rc):
        t = P["t"]
        if rc == 0:
            R.wait(e, (t, "apl", NK - 1))
            R.wait(e, P["gw_key"])
        for tq in range(NTQ):
            g = rc * NTQ + tq
            if g >= 2:
                R.wait(e, (t, "gsg", (g - 2) // 4, (g - 2) % 4))
            for k in range(NK):
                ins = e.matmul(
                    bank(g % 2),
                    aB[:, k * 2048 + rc * 128:k * 2048 + rc * 128 + 128],
                    aA[:, k * 2048 + tq * 512:k * 2048 + tq * 512 + 512],
                    start=(k == 0), stop=(k == NK - 1))
            R.inc(SP, ins, (t, "gmm", rc, tq))

    def ffn_t_fold(P, e, s):
        t = P["t"]
        if s == "agg":
            R.wait(e, (t, "pev", "agg"))
        else:
            R.wait(e, (t, "arb", s))
        for c in range(NK):
            ins = e.transpose(psA[:, 6 * 512 + 2 * c:6 * 512 + 2 * c + 2],
                              bmw[0:2, c * 128:(c + 1) * 128],
                              ident[0:2, 0:2])
        R.inc(SP, ins, (t, "fold", s))

    def ffn_t_th(P, e, it):
        t = P["t"]
        if it == "agg":
            R.wait(e, (t, "xna"))
            for k in range(NK):
                R.wait(e, (t, "agl", k))
                for q in range(4):
                    ins = e.matmul(
                        psA[0:2, 1024 + q * 512:1024 + q * 512 + 512],
                        smbv(512, 32)[:, 2 * k:2 * k + 2],
                        P["wsl2"](k)[:, q * 512:q * 512 + 512],
                        start=(k == 0), stop=(k == NK - 1))
                R.inc(SP, ins, (t, "ak", k))
        elif it == "bc":
            ffn_t_fold(P, e, 3)
            R.wait(e, (t, "thb", 4))
            for k in range(NK):
                R.wait(e, (t, "bcl", k))
                for q in range(4):
                    ins = e.matmul(
                        psA[0:2, 1024 + q * 512:1024 + q * 512 + 512],
                        smbv(0, 32)[:, 2 * k:2 * k + 2],
                        P["wsl2"](NK + 4 * 24 + k)[:, q * 512:q * 512 + 512],
                        start=(k == 0), stop=(k == NK - 1))
                R.inc(SP, ins, (t, "bk", k))
            R.wait(e, (t, "pev", "bc"))
            for c in range(NK):
                ins = e.transpose(psA[:, 6 * 512 + 2 * c:6 * 512 + 2 * c + 2],
                                  bmw[0:2, c * 128:(c + 1) * 128],
                                  ident[0:2, 0:2])
            R.inc(SP, ins, (t, "fold", "bc"))
        else:
            s = int(it[1])
            ffn_t_fold(P, e, "agg" if s == 0 else s - 1)
            R.wait(e, (t, "thb", s))
            R.wait(e, (t, "pev", "fold", s))
            e.matmul(psA[:, 6 * 512:6 * 512 + 32], onesbt[:], smbv(0, 32),
                     start=True, stop=True)
            R.wait(e, (t, "thsq", s))
            R.inc(SP, e.matmul(psA[:, 6 * 512 + 32:6 * 512 + 64], onesbt[:],
                               smbv(640, 32), start=True, stop=True),
                  (t, "lnm", s))
            R.wait(e, (t, "tnc", s))
            if s > 0:
                R.wait(e, (t, "pev", "d2", s - 1))
            for k in range(NK):
                R.wait(e, (t, "dsl", s, k))
                for q in range(2):
                    ins = e.matmul(
                        psA[0:2, 1024 + q * 512:1024 + q * 512 + 512],
                        smbv(128, 32)[:, 2 * k:2 * k + 2],
                        P["wsl2"](NK + s * 24 + k)[:, q * 512:q * 512 + 512],
                        start=(k == 0), stop=(k == NK - 1))
                R.inc(SP, ins, (t, "dk", s, k))
            R.alias((t, "pmmd1", s), (t, "dk", s, NK - 1))
            R.wait(e, (t, "tbb", s))
            if s > 0:
                R.wait(e, (t, "ttb", s - 1))
            tb = aC[0:2, 10240:11264]
            for j in range(8):
                ins = e.transpose(psB[:, 2 * j:2 * j + 2],
                                  tb[0:2, j * 128:(j + 1) * 128],
                                  identb[0:2, 0:2])
            R.inc(SP, ins, (t, "ttr", s))
            R.wait(e, (t, "ttb", s))
            for k in range(8):
                R.wait(e, (t, "dsl", s, 16 + k))
                for q in range(4):
                    ins = e.matmul(
                        psA[0:2, 1024 + q * 512:1024 + q * 512 + 512],
                        smbv(256, 16)[:, 2 * k:2 * k + 2],
                        P["wsl2"](NK + s * 24 + 16 + k)[:, q * 512:q * 512 + 512],
                        start=(k == 0), stop=(k == 7))
                R.inc(SP, ins, (t, "dk", s, 16 + k))
            R.alias((t, "pmmd2", s), (t, "dk", s, 23))

    def ffn_t_fac(P, e):
        t = P["t"]
        R.wait(e, (t, "pev", "bcf"))
        R.wait(e, (t, "ttb", 3))
        R.wait(e, (t, "lnr"))
        for rc in range(NK):
            R.wait(e, (t, "ubc", rc))
            for tq in range(NTQ):
                ins = e.matmul(bank(tq), onesbt[:],
                               P["ub"](rc % 2)[:, tq * 512:tq * 512 + 512],
                               start=(rc == 0), stop=(rc == NK - 1))
            R.wait(e, (t, "usq", rc))
            for tq in range(NTQ):
                dst = bank(4 + tq) if tq < 3 else psBf[:]
                ins = e.matmul(dst, onesbt[:],
                               P["sqs"](rc % 2)[:, tq * 512:tq * 512 + 512],
                               start=(rc == 0), stop=(rc == NK - 1))
            R.inc(SP, ins, (t, "cmm", rc))

    def ffn_scalar(P, e):
        t = P["t"]
        for k in range(NK):
            R.wait(e, (t, "cast", k))
            if k == 0:
                for ev in P.get("scr_evt", []):
                    R.wait(e, ev)
            if k >= 2:
                R.wait(e, (t, "s2", k - 2))
            R.inc(SA, e.activation(out=P["sqs"](k % 2),
                                   in_=aA[:, k * 2048:(k + 1) * 2048],
                                   func=AF.Square), (t, "sq", k))
        R.wait(e, (t, "lnc1"))
        R.inc(SA, e.activation(out=smbv(640, 128), in_=smbv(384, 128),
                               func=AF.Square), (t, "lnc2"))
        for item in P["seg"]:
            if item[0] == "gate":
                rc = item[1]
                if rc == 0:
                    R.wait(e, (t, "xps", NK - 2))
                    R.wait(e, (t, "xps", NK - 1))
                for tq in range(NTQ):
                    g = rc * NTQ + tq
                    R.wait(e, (t, "gmm", rc, tq))
                    if g >= 2:
                        R.wait(e, (t, "gsd", (g - 2) // 4, (g - 2) % 4))
                    R.inc(SA, e.activation(
                        out=P["mu"][:, (g % 2) * 512:(g % 2) * 512 + 512],
                        in_=bank(g % 2), func=AF.Tanh,
                        scale=0.5, bias=smc("gbh", rc)), (t, "gsg", rc, tq))
            else:
                it = item[1]
                if it in ("agg", "bc"):
                    continue
                s = int(it[1])
                R.wait(e, (t, "thb", s))
                R.inc(SA, e.activation(out=smbv(640, 128), in_=smbv(0, 128),
                                       func=AF.Square), (t, "thsq", s))
                R.wait(e, (t, "pev", "d1", s))
                av = aCf[0:2, 8832:9856]
                e.activation(out=av, in_=bmw[:, 0:1024], func=AF.Abs,
                             scale=0.7071067811865476)
                e.activation(out=aCf[0:2, 0:1024], in_=av, func=AF.Square)
                R.inc(SA, e.activation(out=bmw[:, 1024:2048],
                                       in_=aCf[0:2, 0:1024], func=AF.Exp,
                                       scale=-1.0), (t, "gel", s))
        for rc in range(NK):
            R.wait(e, (t, "ubc", rc))
            if rc >= 2:
                R.wait(e, (t, "cmm", rc - 2))
            R.inc(SA, e.activation(out=P["sqs"](rc % 2), in_=P["ub"](rc % 2),
                                   func=AF.Square), (t, "usq", rc))

    _EA = (0.254829592, -0.284496736, 1.421413741, -1.453152027, 1.061405429)
    _EP = 0.3275911

    def ffn_v_th(P, e, it):
        t = P["t"]
        if it == "agg":
            R.wait(e, (t, "ak", NK - 1))
            R.inc(SV, e.tensor_copy(out=bmw, in_=psA[0:2, 1024:3072]),
                  (t, "pev", "agg"))
            return
        if it == "bc":
            R.wait(e, (t, "fold", 3))
            e.tensor_tensor(out=smv("thf", 32),
                            in0=psA[:, 6 * 512:6 * 512 + 32],
                            in1=smv("thf", 32), op=OP.add)
            e.tensor_tensor(out=smv("thf", 32), in0=smv("thf", 32),
                            in1=smv("d2b", 32), op=OP.add)
            R.inc(SV, e.tensor_copy(out=smbv(0, 128), in_=sm[:, 192:320]),
                  (t, "thb", 4))
            R.wait(e, (t, "bk", NK - 1))
            R.inc(SV, e.tensor_copy(out=bmw, in_=psA[0:2, 1024:3072]),
                  (t, "pev", "bc"))
            R.wait(e, (t, "fold", "bc"))
            e.tensor_tensor(out=smv("bcf", 32), in0=psA[:, 6 * 512:6 * 512 + 32],
                            in1=smv("bcb", 32), op=OP.add)
            R.inc(SV, e.nop(), (t, "pev", "bcf"))
            e.tensor_scalar(out=smv("bcmy"), in0=evenodd(smv("bcf", 32), 0),
                            scalar1=smc("mb0"), scalar2=None, op0=OP.mult)
            e.tensor_scalar(out=smv("ftmp"), in0=evenodd(smv("bcf", 32), 1),
                            scalar1=smc("mb1"), scalar2=None, op0=OP.mult)
            R.inc(SV, e.tensor_tensor(out=smv("bcmy"), in0=smv("bcmy"),
                                      in1=smv("ftmp"), op=OP.add), (t, "bcr"))
            return
        s = int(it[1])
        R.wait(e, (t, "fold", "agg" if s == 0 else s - 1))
        if s == 0:
            e.tensor_tensor(out=smv("thf", 32),
                            in0=psA[:, 6 * 512:6 * 512 + 32],
                            in1=smv("aggb", 32), op=OP.add)
        else:
            e.tensor_tensor(out=smv("thf", 32),
                            in0=psA[:, 6 * 512:6 * 512 + 32],
                            in1=smv("thf", 32), op=OP.add)
            e.tensor_tensor(out=smv("thf", 32), in0=smv("thf", 32),
                            in1=smv("d2b", 32), op=OP.add)
        R.inc(SV, e.nop(), (t, "pev", "fold", s))
        R.inc(SV, e.tensor_copy(out=smbv(0, 128), in_=sm[:, 192:320]),
              (t, "thb", s))
        R.wait(e, (t, "lnm", s))
        th_ln(P, e, s)
        R.wait(e, (t, "pmmd1", s))
        R.inc(SV, e.tensor_tensor(out=bmw[:, 0:1024],
                                  in0=psA[0:2, 1024:2048],
                                  in1=d1brow[:], op=OP.add),
              (t, "pev", "d1", s))
        R.wait(e, (t, "gel", s))
        av = aCf[0:2, 8832:9856]
        w = bmw[:, 1024:2048]
        S1 = aCf[0:2, 0:1024]
        S2 = aCf[0:2, 1024:2048]
        S3 = aCf[0:2, 2048:3072]
        S4 = aCf[0:2, 3072:4096]
        e.tensor_scalar(out=S1, in0=av, scalar1=_EP, scalar2=1.0,
                        op0=OP.mult, op1=OP.add)
        e.reciprocal(out=S2, in_=S1)
        e.tensor_scalar(out=S3, in0=S2, scalar1=_EA[4], scalar2=_EA[3],
                        op0=OP.mult, op1=OP.add)
        for aK in (_EA[2], _EA[1], _EA[0]):
            e.tensor_tensor(out=S4, in0=S3, in1=S2, op=OP.mult)
            e.tensor_scalar(out=S3, in0=S4, scalar1=aK, scalar2=None,
                            op0=OP.add)
        e.tensor_tensor(out=S4, in0=S3, in1=S2, op=OP.mult)
        e.tensor_tensor(out=S1, in0=w, in1=S4, op=OP.mult)
        e.tensor_scalar(out=S2, in0=S1, scalar1=-1.0, scalar2=1.0,
                        op0=OP.mult, op1=OP.add)
        e.tensor_tensor(out=S3, in0=av, in1=S2, op=OP.mult)
        e.tensor_scalar(out=S4, in0=S3, scalar1=0.7071067811865476,
                        scalar2=None, op0=OP.mult)
        tb = aC[0:2, 10240:11264]
        R.inc(SV, e.scalar_tensor_tensor(out=tb, in0=bmw[:, 0:1024],
                                         scalar=0.5, in1=S4,
                                         op0=OP.mult, op1=OP.add),
              (t, "tbb", s))
        R.wait(e, (t, "ttr", s))
        R.inc(SV, e.tensor_copy(out=smbv(256, 16), in_=psB[:, 0:16]),
              (t, "ttb", s))
        R.wait(e, (t, "pmmd2", s))
        R.inc(SV, e.tensor_copy(out=bmw, in_=psA[0:2, 1024:3072]),
              (t, "pev", "d2", s))

    def ffn_vector(P, e):
        t = P["t"]
        R.wait(e, P["arena_evt"])
        for k in range(NK):
            R.wait(e, (t, "xl", k))
            e.tensor_copy(out=aA[:, k * 2048:(k + 1) * 2048], in_=P["xch"](k % 2))
            R.inc(SV, e.tensor_copy(out=smc("lastX", k),
                                    in_=P["xch"](k % 2)[:, 2047:2048]),
                  (t, "cast", k))
        R.wait(e, (t, "smm", 0))
        for tq in range(NTQ):
            ins = e.tensor_copy(out=P["gss"](0)[:, tq * 512:tq * 512 + 512],
                                in_=bank(tq))
        R.inc(SV, ins, (t, "evs", 0))
        R.wait(e, (t, "smm", 1))
        for tq in range(NTQ):
            ins = e.tensor_copy(out=P["gss"](1)[:, tq * 512:tq * 512 + 512],
                                in_=bank(tq))
        R.inc(SV, ins, (t, "evs", 1))
        e.tensor_scalar(out=P["mu"], in0=P["gss"](0), scalar1=INV_H,
                        scalar2=None, op0=OP.mult)
        e.tensor_scalar(out=P["rs"], in0=P["gss"](1), scalar1=INV_H,
                        scalar2=None, op0=OP.mult)
        e.tensor_tensor(out=P["gss"](0), in0=P["mu"], in1=P["mu"], op=OP.mult)
        e.tensor_tensor(out=P["gss"](1), in0=P["rs"], in1=P["gss"](0),
                        op=OP.subtract)
        rsqrt_newton(e, P["rs"], P["gss"](1), P["gss"](0), P["dscr"],
                     P["xch"](0), EPS_PRE)
        R.inc(SV, e.nop(), (t, "stat"))
        e.tensor_scalar(out=smv("xpl"), in0=smv("lastX"),
                        scalar1=P["mu"][:, 2047:2048],
                        scalar2=P["rs"][:, 2047:2048],
                        op0=OP.subtract, op1=OP.mult)
        R.inc(SV, e.tensor_copy(out=smbv(384, 128), in_=sm[:, 16:144]),
              (t, "lnc1"))
        R.wait(e, (t, "lnmm"))
        tln = smv("tln", 16)
        e.tensor_reduce(out=tln[:, 0:1].rearrange("p (a o) -> p a o", o=1),
                        in_=psA[:, 4 * 512:4 * 512 + 16].rearrange(
                            "p (a c) -> p a c", a=1),
                        axis=mybir.AxisListType.X, op=OP.add)
        R.inc(SV, e.tensor_reduce(
            out=tln[:, 1:2].rearrange("p (a o) -> p a o", o=1),
            in_=psA[:, 4 * 512 + 16:4 * 512 + 32].rearrange(
                "p (a c) -> p a c", a=1),
            axis=mybir.AxisListType.X, op=OP.add), (t, "lnr"))
        e.tensor_scalar(out=tln[:, 4:5], in0=tln[:, 0:1], scalar1=INV_H,
                        scalar2=None, op0=OP.mult)
        e.tensor_scalar(out=tln[:, 5:6], in0=tln[:, 1:2], scalar1=INV_H,
                        scalar2=None, op0=OP.mult)
        e.tensor_tensor(out=tln[:, 2:3], in0=tln[:, 4:5], in1=tln[:, 4:5],
                        op=OP.mult)
        e.tensor_tensor(out=tln[:, 6:7], in0=tln[:, 5:6], in1=tln[:, 2:3],
                        op=OP.subtract)
        ws = P["gss"](0)
        e.memset(ws[:, 0:640], 1.0)
        e.tensor_copy(out=ws[:, 6:7], in_=tln[:, 6:7])
        rsqrt_newton(e, ws[:, 384:512], ws[:, 0:128], ws[:, 128:256],
                     ws[:, 256:384], ws[:, 512:640], EPS_FFN)
        e.tensor_copy(out=tln[:, 7:8], in_=ws[:, 384 + 6:384 + 7])
        e.tensor_scalar(out=smv("xnorm"), in0=smv("xpl"), scalar1=tln[:, 4:5],
                        scalar2=tln[:, 7:8], op0=OP.subtract, op1=OP.mult)
        e.tensor_tensor(out=smv("xnorm"), in0=smv("xnorm"), in1=smv("ingam"),
                        op=OP.mult)
        e.tensor_tensor(out=smv("xnorm"), in0=smv("xnorm"), in1=smv("inbet"),
                        op=OP.add)
        e.tensor_scalar(out=evenodd(smv("ccst", 32), 0), in0=smv("xnorm"),
                        scalar1=smc("mkb0"), scalar2=None, op0=OP.mult)
        R.inc(SV, e.tensor_scalar(out=evenodd(smv("ccst", 32), 1),
                                  in0=smv("xnorm"), scalar1=smc("mkb1"),
                                  scalar2=None, op0=OP.mult), (t, "lnl"))
        R.wait(e, (t, "ccb_back"))
        R.inc(SV, e.tensor_copy(out=smbv(512, 128), in_=sm[:, 80:208]),
              (t, "xna"))
        for k in range(NK):
            R.wait(e, (t, "xl2", k))
            if k >= 2:
                R.wait(e, (t, "xps", k - 2))
            e.tensor_tensor(out=P["xch"](k % 2), in0=P["xch"](k % 2),
                            in1=P["mu"], op=OP.subtract)
            e.tensor_tensor(out=P["gss"](k % 2), in0=P["xch"](k % 2),
                            in1=P["rs"], op=OP.mult)
            R.inc(SV, e.tensor_tensor(out=aA[:, k * 2048:(k + 1) * 2048],
                                      in0=P["xch"](k % 2), in1=P["rs"],
                                      op=OP.mult), (t, "apl", k))
        for item in P["seg"]:
            if item[0] == "gate":
                rc = item[1]
                for tq in range(NTQ):
                    g = rc * NTQ + tq
                    R.wait(e, (t, "gsg", rc, tq))
                    if rc >= 2 and tq == 0:
                        R.wait(e, (t, "gst", rc - 2))
                    R.inc(SV, e.tensor_scalar(
                        out=P["gss"](rc % 2)[:, tq * 512:tq * 512 + 512],
                        in0=P["mu"][:, (g % 2) * 512:(g % 2) * 512 + 512],
                        scalar1=0.5, scalar2=0.5, op0=OP.mult, op1=OP.add),
                        (t, "gsd", rc, tq))
                R.alias((t, "gse", rc), (t, "gsd", rc, 3))
            else:
                ffn_v_th(P, e, item[1])
        for rc in range(NK):
            R.wait(e, (t, "gl", rc))
            R.wait(e, (t, "xpl2", rc))
            if rc == 0:
                R.wait(e, (t, "bcr"))
            e.tensor_scalar(out=P["mu"], in0=P["xch"](rc % 2),
                            scalar1=smv("bcmy")[:, rc:rc + 1], scalar2=None,
                            op0=OP.mult)
            R.inc(SV, e.tensor_tensor(out=P["gss"](rc % 2), in0=P["mu"],
                                      in1=P["gss"](rc % 2), op=OP.add),
                  (t, "uc", rc))
            if rc >= 2:
                R.wait(e, (t, "cmm", rc - 2))
            R.inc(SV, e.tensor_copy(out=P["ub"](rc % 2), in_=P["gss"](rc % 2)),
                  (t, "ubc", rc))
        R.wait(e, (t, "cmm", NK - 1))
        R.wait(e, (t, "ust", NK - 2))
        R.wait(e, (t, "ust", NK - 1))
        for tq in range(NTQ):
            e.tensor_copy(out=P["gss"](0)[:, tq * 512:tq * 512 + 512],
                          in_=bank(tq))
        for tq in range(NTQ):
            src = bank(4 + tq) if tq < 3 else psBf[:]
            ins = e.tensor_copy(out=P["gss"](1)[:, tq * 512:tq * 512 + 512],
                                in_=src)
        R.inc(SV, ins, (t, "cev"))
        e.tensor_scalar(out=P["mu"], in0=P["gss"](0), scalar1=INV_H,
                        scalar2=None, op0=OP.mult)
        e.tensor_scalar(out=P["rs"], in0=P["gss"](1), scalar1=INV_H,
                        scalar2=None, op0=OP.mult)
        e.tensor_tensor(out=P["gss"](0), in0=P["mu"], in1=P["mu"], op=OP.mult)
        e.tensor_tensor(out=P["gss"](1), in0=P["rs"], in1=P["gss"](0),
                        op=OP.subtract)
        rsqrt_newton(e, P["rs"], P["gss"](1), P["gss"](0), P["dscr"],
                     P["xch"](0), EPS_FFN)
        R.inc(SV, e.nop(), (t, "cst"))
        for rc in range(NK):
            R.wait(e, (t, "ul3", rc))
            R.wait(e, (t, "xl3", rc))
            e.tensor_tensor(out=P["xch"](rc % 2), in0=P["xch"](rc % 2),
                            in1=P["mu"], op=OP.subtract)
            R.inc(SV, e.tensor_tensor(out=P["xch"](rc % 2), in0=P["xch"](rc % 2),
                                      in1=P["rs"], op=OP.mult), (t, "fbe", rc))
            e.tensor_scalar(out=P["dscr"], in0=P["xch"](rc % 2),
                            scalar1=smc("g2", rc), scalar2=smc("b2", rc),
                            op0=OP.mult, op1=OP.add)
            R.inc(SV, e.tensor_tensor(out=P["xch"](rc % 2), in0=P["dscr"],
                                      in1=P["gss"](rc % 2), op=OP.add),
                  (t, "fbo", rc))

    def th_ln(P, e, s):
        t = P["t"]
        tln = smv("tln", 16)
        p6 = psA[:, 6 * 512:6 * 512 + 64]
        e.tensor_reduce(out=tln[:, 0:2].rearrange("p (b o) -> p b o", o=1),
                        in_=p6[:, 0:32].rearrange("p (c b) -> p b c", b=2),
                        axis=mybir.AxisListType.X, op=OP.add)
        e.tensor_reduce(out=tln[:, 2:4].rearrange("p (b o) -> p b o", o=1),
                        in_=p6[:, 32:64].rearrange("p (c b) -> p b c", b=2),
                        axis=mybir.AxisListType.X, op=OP.add)
        e.tensor_scalar(out=tln[:, 8:10], in0=tln[:, 0:2], scalar1=INV_H,
                        scalar2=None, op0=OP.mult)
        e.tensor_scalar(out=tln[:, 10:12], in0=tln[:, 2:4], scalar1=INV_H,
                        scalar2=None, op0=OP.mult)
        e.tensor_tensor(out=tln[:, 4:6], in0=tln[:, 8:10], in1=tln[:, 8:10],
                        op=OP.mult)
        e.tensor_tensor(out=tln[:, 12:14], in0=tln[:, 10:12], in1=tln[:, 4:6],
                        op=OP.subtract)
        ws = P["rs"]
        e.memset(ws[:, 0:640], 1.0)
        e.tensor_copy(out=ws[:, 12:14], in_=tln[:, 12:14])
        rsqrt_newton(e, ws[:, 384:512], ws[:, 0:128], ws[:, 128:256],
                     ws[:, 256:384], ws[:, 512:640], EPS_FFN)
        e.tensor_copy(out=tln[:, 14:16], in_=ws[:, 384 + 12:384 + 14])
        for b in range(2):
            dst = evenodd(smv("ftmp", 32), b)
            e.tensor_scalar(out=dst, in0=evenodd(smv("thf", 32), b),
                            scalar1=tln[:, 8 + b:9 + b],
                            scalar2=tln[:, 14 + b:15 + b],
                            op0=OP.subtract, op1=OP.mult)
            e.tensor_tensor(out=dst, in0=dst, in1=smv("png"), op=OP.mult)
            e.tensor_tensor(out=dst, in0=dst, in1=smv("pnb"), op=OP.add)
        R.inc(SV, e.tensor_copy(out=smbv(128, 128), in_=sm[:, 272:400]),
              (t, "tnc", s))

    def attn_sync(e):
        t = "a"
        for k in range(NK):
            R.wait(e, ("f1", "sto", k))
            if k >= 2:
                R.wait(e, (t, "cast", k - 2))
            else:
                R.wait(e, ("f1", "sto", NK - 2 + k))
            R.inc(S2("sxl")[k % 2], e.dma_start(
                out=cf((k % 2) * 2048), in_=x1f[k * 128:(k + 1) * 128, :]),
                (t, "xl", k), 16)
        for k in range(NK):
            R.wait(e, (t, "cast", NK - 2 + k) if k < 2 else (t, "apl", k - 2))
            if k < 2:
                R.wait(e, (t, "stat"))
            R.inc(S2("sxl2")[k % 2], e.dma_start(
                out=cf((k % 2) * 2048), in_=x1f[k * 128:(k + 1) * 128, :]),
                (t, "xl2", k), 16)
        for rc in range(NK):
            R.wait(e, (t, "scn", rc))
            R.inc(S2("sssp")[rc % 2], e.dma_start(out=Ssp[rc], in_=cf(4096)),
                  (t, "ssp", rc), 16)
            R.inc(S2("spsp")[rc % 2], e.dma_start(out=Psp[rc], in_=cf(6144)),
                  (t, "psp", rc), 16)
        for hc in range(NK):
            R.wait(e, (t, "lev", hc))
            R.inc(S2("slst")[hc % 2], e.dma_start(
                out=loc_sp[hc], in_=cf(12288 + (hc % 2) * 2048)),
                (t, "lst", hc), 16)
        for rc in range(NK):
            if rc < 2:
                R.wait(e, (t, "scn", NK - 1))
                R.wait(e, (t, "lst", NK - 2))
                R.wait(e, (t, "lst", NK - 1))
            else:
                R.wait(e, (t, "fix", rc - 2))
                R.wait(e, (t, "m1", rc - 2, NTQ - 1))
            R.inc(S2("spfs")[rc % 2], e.dma_start(
                out=cf(8192 + (rc % 2) * 2048), in_=Ssp[rc]),
                (t, "pfs", rc), 16)
            R.inc(S2("spfp")[rc % 2], e.dma_start(
                out=cf(12288 + (rc % 2) * 2048), in_=Psp[rc]),
                (t, "pfp", rc), 16)
        for hc in range(NK):
            if hc < 2:
                R.wait(e, (t, "ro", NK - 1, 3))
            else:
                R.wait(e, (t, "oub", hc - 2))
                R.wait(e, (t, "sto", hc - 2))
            R.inc(S2("slrl")[hc % 2], e.dma_start(
                out=cf(8192 + (hc % 2) * 2048), in_=loc_sp[hc]),
                (t, "lrl", hc), 16)
            R.inc(S2("sxr")[hc % 2], e.dma_start(
                out=cf(12288 + (hc % 2) * 2048),
                in_=x1f[hc * 128:(hc + 1) * 128, :]), (t, "xr", hc), 16)
            R.wait(e, (t, "oub", hc))
            R.inc(S2("ssto")[hc % 2], e.dma_start(
                out=x2f[hc * 128:(hc + 1) * 128, :],
                in_=cf(8192 + (hc % 2) * 2048)), (t, "sto", hc), 16)

    def attn_gpsimd(e):
        t = "a"
        WJ = [W["w_f"], W["w_i"], W["w_v"]]
        for rc in range(NK):
            for j in range(3):
                if rc >= 2:
                    R.wait(e, (t, "mm", j, (rc - 2) * 4 + 3))
                else:
                    R.wait(e, ("f1", "gmm", NK - 1, 3))
                key = (t, "wl", rc) if j == 2 else None
                R.inc(S2("gwl")[rc % 2], e.dma_start(
                    out=aB[:, ((rc % 2) * 3 + j) * 2048:((rc % 2) * 3 + j) * 2048 + 2048],
                    in_=wsl_src(WJ[j], rc)), key, 16)
        R.wait(e, (t, "ab"))
        R.inc(S1("gcc"), e.dma_start(out=cc_gi, in_=smv("ABs", 32)),
              (t, "ags"), 16)
        R.wait(e, (t, "ags"))
        cc_run(e, "ag", "AllGather", OP.bypass, cc_gi, cc_go)
        R.wait(e, (t, "psp", NK - 2))
        R.wait(e, (t, "psp", NK - 1))
        R.inc(S1("gcc"), e.dma_start(
            out=cfw(6144, 256).rearrange("p (r c) -> p r c", c=32),
            in_=cc_go.rearrange("(r p) c -> p r c", p=128)), (t, "agb"), 16)
        for lc in range(NK):
            for j, w in enumerate([W["w_up"], W["w_gt"]]):
                if lc >= 2:
                    R.wait(e, (t, "hh", (lc - 2) * 4 + 3))
                else:
                    R.wait(e, (t, "apl", NK - 1))
                key = (t, "b1l", lc) if j == 1 else None
                R.inc(S2("gb1l")[lc % 2], e.dma_start(
                    out=cbw(((lc % 2) * 2 + j) * 2048, 2048),
                    in_=wsl_src(w, lc)), key, 16)
        for hc in range(NK):
            if hc >= 2:
                R.wait(e, (t, "dmm", hc - 2, 3))
            else:
                R.wait(e, (t, "hh", NK * 4 - 1))
            R.inc(S2("gb2l")[hc % 2], e.dma_start(
                out=cbw((hc % 2) * 2048, 2048),
                in_=wsl_src(W["w_dn"], hc)), (t, "b2l", hc), 16)
        for rc in range(NK):
            if rc >= 2:
                R.wait(e, (t, "qmm", rc - 2, 3))
            else:
                R.wait(e, (t, "dmm", NK - 1, 3))
            R.inc(S2("gql")[rc % 2], e.dma_start(
                out=cbw(4096 + (rc % 2) * 2048, 2048),
                in_=wsl_src(W["w_q"], rc)), (t, "ql", rc), 16)
        for hc in range(NK):
            if hc >= 2:
                R.wait(e, (t, "rmm", hc - 2, 3))
            else:
                R.wait(e, (t, "ro", NK - 1, 3))
            R.inc(S2("grol")[hc % 2], e.dma_start(
                out=cbw((hc % 2) * 2048, 2048),
                in_=wsl_src(W["w_ro"], hc)), (t, "rol", hc), 16)

    def attn_tensor(e):
        t = "a"
        R.wait(e, ("f1", "cev"))
        for k in range(NK):
            R.wait(e, (t, "cast", k))
            for tq in range(NTQ):
                ins = e.matmul(bank(tq), onesbt[:],
                               aA[:, k * 2048 + tq * 512:k * 2048 + tq * 512 + 512],
                               start=(k == 0), stop=(k == NK - 1))
            if k == NK - 1:
                R.inc(SP, ins, (t, "smm", 0))
        R.wait(e, (t, "evs", 0))
        for k in range(NK):
            R.wait(e, (t, "sq", k))
            for tq in range(NTQ):
                ins = e.matmul(bank(tq), onesbt[:],
                               cbw(8192 + (k % 2) * 2048, 2048)[:, tq * 512:tq * 512 + 512],
                               start=(k == 0), stop=(k == NK - 1))
            R.inc(SP, ins, (t, "s2", k))
        R.alias((t, "smm", 1), (t, "s2", NK - 1))
        for rc in range(NK):
            R.wait(e, (t, "wl", rc))
            if rc == 0:
                R.wait(e, (t, "apl", NK - 1))
            for tq in range(NTQ):
                g = rc * 4 + tq
                for j in range(3):
                    if g >= 2:
                        R.wait(e, [(t, "evf", g - 2), (t, "evi", g - 2),
                                   (t, "evv", g - 2)][j])
                    for k in range(NK):
                        ins = e.matmul(
                            psA[:, (j * 2 + g % 2) * 512:(j * 2 + g % 2) * 512 + 512],
                            aB[:, ((rc % 2) * 3 + j) * 2048 + k * 128:
                               ((rc % 2) * 3 + j) * 2048 + k * 128 + 128],
                            aA[:, k * 2048 + tq * 512:k * 2048 + tq * 512 + 512],
                            start=(k == 0), stop=(k == NK - 1))
                    R.inc(SP, ins, (t, "mm", j, g))
        for lc in range(NK):
            R.wait(e, (t, "b1l", lc))
            for tq in range(NTQ):
                g = lc * 4 + tq
                for j in range(2):
                    if g < 2:
                        R.wait(e, (t, "evf" if j == 0 else "evi",
                                   4 * NK - 2 + g))
                    if g >= 2:
                        R.wait(e, (t, "hh", g - 2))
                    for k in range(NK):
                        ins = e.matmul(
                            psA[:, (j * 2 + g % 2) * 512:(j * 2 + g % 2) * 512 + 512],
                            cbw(((lc % 2) * 2 + j) * 2048 + k * 128, 128),
                            aA[:, k * 2048 + tq * 512:k * 2048 + tq * 512 + 512],
                            start=(k == 0), stop=(k == NK - 1))
                    R.inc(SP, ins, [(t, "umm", lc, tq), (t, "gm2", lc, tq)][j])
        for hc in range(NK):
            R.wait(e, (t, "b2l", hc))
            for tq in range(NTQ):
                g = hc * 4 + tq
                if g >= 2:
                    R.wait(e, (t, "lcp", g - 2))
                for k in range(NK):
                    ins = e.matmul(
                        bank(g % 2),
                        cbw((hc % 2) * 2048 + k * 128, 128),
                        aB[:, k * 2048 + tq * 512:k * 2048 + tq * 512 + 512],
                        start=(k == 0), stop=(k == NK - 1))
                R.inc(SP, ins, (t, "dmm", hc, tq))
        for rc in range(NK):
            R.wait(e, (t, "ql", rc))
            for tq in range(NTQ):
                g = rc * 4 + tq
                if g < 2:
                    R.wait(e, (t, "lcp", 4 * NK - 2 + g))
                if g >= 2:
                    R.wait(e, (t, "ro", (g - 2) // 4, (g - 2) % 4))
                for k in range(NK):
                    ins = e.matmul(
                        bank(g % 2),
                        cbw(4096 + (rc % 2) * 2048 + k * 128, 128),
                        aA[:, k * 2048 + tq * 512:k * 2048 + tq * 512 + 512],
                        start=(k == 0), stop=(k == NK - 1))
                R.inc(SP, ins, (t, "qmm", rc, tq))
        for hc in range(NK):
            R.wait(e, (t, "rol", hc))
            for tq in range(NTQ):
                g = hc * 4 + tq
                if g < 2:
                    R.wait(e, (t, "ro", NK - 1, 2 + g))
                if g >= 2:
                    R.wait(e, (t, "oo", g - 2))
                for k in range(NK):
                    ins = e.matmul(
                        bank(g % 2),
                        cbw((hc % 2) * 2048 + k * 128, 128),
                        aB[:, k * 2048 + tq * 512:k * 2048 + tq * 512 + 512],
                        start=(k == 0), stop=(k == NK - 1))
                R.inc(SP, ins, (t, "rmm", hc, tq))

    def attn_scalar(e):
        t = "a"
        for k in range(NK):
            R.wait(e, (t, "cast", k))
            if k >= 2:
                R.wait(e, (t, "s2", k - 2))
            R.inc(SA, e.activation(out=cbw(8192 + (k % 2) * 2048, 2048),
                                   in_=aA[:, k * 2048:(k + 1) * 2048],
                                   func=AF.Square), (t, "sq", k))
        for rc in range(NK):
            for tq in range(NTQ):
                g = rc * 4 + tq
                R.wait(e, (t, "mm", 0, g))
                if g == 0:
                    R.wait(e, (t, "apl", NK - 1))
                R.inc(SA, e.activation(
                    out=cf((rc % 2) * 2048)[:, tq * 512:tq * 512 + 512],
                    in_=psA[:, (g % 2) * 512:(g % 2) * 512 + 512],
                    func=AF.Tanh), (t, "evf", g))
                R.wait(e, (t, "mm", 1, g))
                if g >= 2:
                    R.wait(e, (t, "evv", g - 2))
                R.inc(SA, e.activation(
                    out=cfw(9728 + (g % 2) * 512, 512),
                    in_=psA[:, (2 + g % 2) * 512:(2 + g % 2) * 512 + 512],
                    func=AF.Sigmoid), (t, "evi", g))
                R.wait(e, (t, "mm", 2, g))
                R.inc(SA, e.activation(
                    out=cfw(8192 + (g % 2) * 512, 512),
                    in_=psA[:, (4 + g % 2) * 512:(4 + g % 2) * 512 + 512],
                    func=AF.Sigmoid), (t, "asv", g))
        for lc in range(NK):
            for tq in range(NTQ):
                g = lc * 4 + tq
                R.wait(e, (t, "gm2", lc, tq))
                if g == 0:
                    R.wait(e, (t, "ssp", NK - 2))
                    R.wait(e, (t, "ssp", NK - 1))
                if g >= 1:
                    R.wait(e, (t, "hh", g - 1))
                R.inc(SA, e.activation(out=cfw(4096, 512),
                                       in_=psA[:, (2 + g % 2) * 512:(2 + g % 2) * 512 + 512],
                                       func=AF.Sigmoid), (t, "b1s", lc, tq))
        for hc in range(NK):
            for tq in range(NTQ):
                g = hc * 4 + tq
                R.wait(e, (t, "dmm", hc, tq))
                if g == 0:
                    R.wait(e, (t, "scn", NK - 1))
                if hc >= 2 and tq == 0:
                    R.wait(e, (t, "lst", hc - 2))
                R.inc(SA, e.copy(
                    out=cf(12288 + (hc % 2) * 2048)[:, tq * 512:tq * 512 + 512],
                    in_=bank(g % 2)), (t, "lcp", g))
            R.alias((t, "lev", hc), (t, "lcp", hc * 4 + 3))
        for rc in range(NK):
            for tq in range(NTQ):
                R.wait(e, (t, "m1", rc, tq))
                R.inc(SA, e.activation(out=cfw(512, 512), in_=cfw(0, 512),
                                       func=AF.Sigmoid), (t, "sg", rc, tq))

    def attn_vector(e):
        t = "a"
        R.wait(e, ("f1", "fbo", NK - 1))
        amu, ars = cf(8192), cf(10240)
        for k in range(NK):
            R.wait(e, (t, "xl", k))
            R.inc(SV, e.tensor_copy(out=aA[:, k * 2048:(k + 1) * 2048],
                                    in_=cf((k % 2) * 2048)), (t, "cast", k))
        R.wait(e, (t, "smm", 0))
        for tq in range(NTQ):
            ins = e.tensor_copy(out=cf(12288)[:, tq * 512:tq * 512 + 512],
                                in_=bank(tq))
        R.inc(SV, ins, (t, "evs", 0))
        R.wait(e, (t, "smm", 1))
        for tq in range(NTQ):
            ins = e.tensor_copy(out=cf(14336)[:, tq * 512:tq * 512 + 512],
                                in_=bank(tq))
        R.inc(SV, ins, (t, "evs", 1))
        e.tensor_scalar(out=amu, in0=cf(12288), scalar1=INV_H, scalar2=None,
                        op0=OP.mult)
        e.tensor_scalar(out=ars, in0=cf(14336), scalar1=INV_H, scalar2=None,
                        op0=OP.mult)
        e.tensor_tensor(out=cf(12288), in0=amu, in1=amu, op=OP.mult)
        e.tensor_tensor(out=cf(14336), in0=ars, in1=cf(12288), op=OP.subtract)
        rsqrt_newton(e, ars, cf(14336), cf(12288), cf(4096), cf(6144), EPS_PRE)
        R.inc(SV, e.nop(), (t, "stat"))
        for k in range(NK):
            R.wait(e, (t, "xl2", k))
            e.tensor_tensor(out=cf((k % 2) * 2048), in0=cf((k % 2) * 2048),
                            in1=amu, op=OP.subtract)
            R.inc(SV, e.tensor_tensor(out=aA[:, k * 2048:(k + 1) * 2048],
                                      in0=cf((k % 2) * 2048), in1=ars,
                                      op=OP.mult), (t, "apl", k))
        R.wait(e, ("f1", "smdump"))
        for rc in range(NK):
            for tq in range(NTQ):
                g = rc * 4 + tq
                R.wait(e, (t, "asv", g))
                e.tensor_tensor(out=cfw(9216, 512),
                                in0=psA[:, (4 + g % 2) * 512:(4 + g % 2) * 512 + 512],
                                in1=cfw(8192 + (g % 2) * 512, 512), op=OP.mult)
                R.wait(e, (t, "evi", g))
                R.inc(SV, e.tensor_tensor(
                    out=cfw(12288 + (rc % 2) * 2048 + tq * 512, 512),
                    in0=cfw(9728 + (g % 2) * 512, 512),
                    in1=cfw(9216, 512), op=OP.mult), (t, "evv", g))
            R.wait(e, (t, "evf", rc * 4 + 3))
            if rc >= 1:
                R.wait(e, (t, "ssp", rc - 1))
                R.wait(e, (t, "psp", rc - 1))
            e.tensor_tensor_scan(out=cf(4096), data0=cf((rc % 2) * 2048),
                                 data1=cf(12288 + (rc % 2) * 2048),
                                 initial=0.0, op0=OP.mult, op1=OP.add)
            e.tensor_copy(out=smc("ABs", 16 + rc), in_=cf(4096)[:, 2047:2048])
            e.tensor_tensor_scan(out=cf(6144), data0=cf((rc % 2) * 2048),
                                 data1=cf((rc % 2) * 2048),
                                 initial=1.0, op0=OP.mult, op1=OP.bypass)
            R.inc(SV, e.tensor_copy(out=smc("ABs", rc),
                                    in_=cf(6144)[:, 2047:2048]), (t, "scn", rc))
        R.inc(SV, e.nop(), (t, "ab"))
        R.wait(e, (t, "agb"))
        ag = cfw(6144, 256).rearrange("p (r c) -> p r c", c=32)
        e.tensor_copy(out=smv("s_in"), in_=smv("initf"))
        for j in range(NCORE):
            e.tensor_scalar(out=smv("ftmp"), in0=ag[:, j, 0:16],
                            scalar1=smc("predm", j), scalar2=smc("predm1", j),
                            op0=OP.mult, op1=OP.add)
            e.tensor_tensor(out=smv("s_in"), in0=smv("s_in"), in1=smv("ftmp"),
                            op=OP.mult)
            e.tensor_scalar(out=smv("ftmp2"), in0=ag[:, j, 16:32],
                            scalar1=smc("predm", j), scalar2=None, op0=OP.mult)
            e.tensor_tensor(out=smv("s_in"), in0=smv("s_in"), in1=smv("ftmp2"),
                            op=OP.add)
        R.inc(SV, e.nop(), (t, "sin"))
        for lc in range(NK):
            for tq in range(NTQ):
                g = lc * 4 + tq
                R.wait(e, (t, "b1s", lc, tq))
                e.tensor_tensor(out=cfw(4608, 512),
                                in0=psA[:, (2 + g % 2) * 512:(2 + g % 2) * 512 + 512],
                                in1=cfw(4096, 512), op=OP.mult)
                R.wait(e, (t, "umm", lc, tq))
                R.inc(SV, e.tensor_tensor(
                    out=aB[:, lc * 2048 + tq * 512:lc * 2048 + tq * 512 + 512],
                    in0=psA[:, (g % 2) * 512:(g % 2) * 512 + 512],
                    in1=cfw(4608, 512), op=OP.mult), (t, "hh", g))
        for rc in range(NK):
            R.wait(e, (t, "pfs", rc))
            R.wait(e, (t, "pfp", rc))
            if rc == 0:
                R.wait(e, (t, "sin"))
            e.tensor_scalar(out=cf(4096), in0=cf(12288 + (rc % 2) * 2048),
                            scalar1=smc("s_in", rc), scalar2=None, op0=OP.mult)
            R.inc(SV, e.tensor_tensor(out=cf(8192 + (rc % 2) * 2048),
                                      in0=cf(4096),
                                      in1=cf(8192 + (rc % 2) * 2048),
                                      op=OP.add), (t, "fix", rc))
            for tq in range(NTQ):
                g = rc * 4 + tq
                R.wait(e, (t, "qmm", rc, tq))
                R.inc(SV, e.tensor_tensor(
                    out=cfw(0, 512), in0=bank(g % 2),
                    in1=cf(8192 + (rc % 2) * 2048)[:, tq * 512:tq * 512 + 512],
                    op=OP.mult), (t, "m1", rc, tq))
                R.wait(e, (t, "sg", rc, tq))
                R.inc(SV, e.tensor_tensor(
                    out=aB[:, rc * 2048 + tq * 512:rc * 2048 + tq * 512 + 512],
                    in0=cfw(0, 512), in1=cfw(512, 512), op=OP.mult),
                    (t, "ro", rc, tq))
        for hc in range(NK):
            R.wait(e, (t, "lrl", hc))
            for tq in range(NTQ):
                g = hc * 4 + tq
                R.wait(e, (t, "rmm", hc, tq))
                R.inc(SV, e.tensor_tensor(
                    out=cf(8192 + (hc % 2) * 2048)[:, tq * 512:tq * 512 + 512],
                    in0=bank(g % 2),
                    in1=cf(8192 + (hc % 2) * 2048)[:, tq * 512:tq * 512 + 512],
                    op=OP.add), (t, "oo", g))
            R.wait(e, (t, "xr", hc))
            R.inc(SV, e.tensor_tensor(out=cf(8192 + (hc % 2) * 2048),
                                      in0=cf(8192 + (hc % 2) * 2048),
                                      in1=cf(12288 + (hc % 2) * 2048),
                                      op=OP.add), (t, "oub", hc))
        R.alias(("a", "fbo", NK - 1), ("a", "oub", NK - 1))

    def tout_sync(e, Xfin, fin_evt):
        for tc in range(NK):
            for ev in fin_evt:
                R.wait(e, ev)
            if tc >= 2:
                R.wait(e, ("to", "pe", tc - 2, 3))
            src = Xfin[:, tc * 128:(tc + 1) * 128].rearrange(
                "(fc p) m -> p fc m", p=128)
            dst = cf((tc % 2) * 2048).rearrange("p (fc m) -> p fc m", m=128)
            R.inc(S2("sld")[tc % 2], e.dma_start(out=dst, in_=src),
                  ("to", "ld", tc), 16)
            R.wait(e, ("to", "ev", tc, 3))
            R.inc(S2("sst")[tc % 2], e.dma_start(
                out=out_o[tc * 128:(tc + 1) * 128, :],
                in_=cf(4096 + (tc % 2) * 2048)), ("to", "st", tc), 16)
        R.wait(e, ("to", "st", NK - 2))
        R.wait(e, ("to", "st", NK - 1))

    def tout_tensor(e, bank_evt):
        R.wait(e, bank_evt)
        for tc in range(NK):
            R.wait(e, ("to", "ld", tc))
            for fc in range(NK):
                g4 = fc // 4
                gidx = tc * 4 + g4
                if gidx >= 2:
                    R.wait(e, ("to", "ev", (gidx - 2) // 4, (gidx - 2) % 4))
                ins = e.transpose(
                    bank(g4 % 2)[:, (fc % 4) * 128:(fc % 4) * 128 + 128],
                    cf((tc % 2) * 2048)[:, fc * 128:fc * 128 + 128],
                    ident[:])
                if fc % 4 == 3:
                    R.inc(SP, ins, ("to", "pe", tc, g4))

    def tout_vector(e):
        for tc in range(NK):
            for g4 in range(4):
                R.wait(e, ("to", "pe", tc, g4))
                if tc >= 2 and g4 == 0:
                    R.wait(e, ("to", "st", tc - 2))
                R.inc(SV, e.tensor_copy(
                    out=cf(4096 + (tc % 2) * 2048)[:, g4 * 512:g4 * 512 + 512],
                    in_=bank(g4 % 2)), ("to", "ev", tc, g4))

    f1 = make_ffn("f1", xf0, x1f, in_evt=lambda k: ("ti", "st", k),
                  bank_evt=("ti", "ev", NK - 1, 3), gw_evt=("f1", "gw"),
                  arena_evt=("ti", "pc"))
    f1["scr_evt"] = [("ti", "st", NK - 2), ("ti", "st", NK - 1)]
    f2 = make_ffn("f2", x2f, x3f, in_evt=lambda k: ("a", "sto", k),
                  bank_evt=("a", "oub", NK - 1), gw_evt=None,
                  arena_evt=("a", "oub", NK - 1))
    f1["gw_key"] = ("f1", "gw")

    if stage == "id":
        tout_x, tout_fin, tout_bank = xf0, \
            [("ti", "st", NK - 2), ("ti", "st", NK - 1)], ("ti", "ev", NK - 1, 3)
    elif stage == "f1":
        tout_x, tout_fin, tout_bank = x1f, \
            [("f1", "sto", NK - 2), ("f1", "sto", NK - 1)], ("f1", "cev")
    elif stage == "f1a":
        tout_x, tout_fin, tout_bank = x2f, \
            [("a", "sto", NK - 2), ("a", "sto", NK - 1)], ("a", "oub", NK - 1)
    else:
        tout_x, tout_fin, tout_bank = x3f, \
            [("f2", "sto", NK - 2), ("f2", "sto", NK - 1)], ("f2", "cev")

    def emit_sync(e):
        tin_sync(e)
        if stage != "id":
            ffn_sync(f1, e)
        if stage in ("f1a", "full"):
            attn_sync(e)
        if stage == "full":
            ffn_sync(f2, e)
        tout_sync(e, tout_x, tout_fin)

    def emit_gpsimd(e):
        tin_gpsimd(e)
        if stage != "id":
            ffn_gpsimd(f1, e)
        if stage in ("f1a", "full"):
            attn_gpsimd(e)
        if stage == "full":
            ffn_gpsimd(f2, e)

    def emit_tensor(e):
        tin_tensor(e)
        if stage != "id":
            ffn_tensor(f1, e)
        if stage in ("f1a", "full"):
            attn_tensor(e)
        if stage == "full":
            ffn_tensor(f2, e)
        tout_tensor(e, tout_bank)

    def emit_scalar(e):
        e = Ser(e, SA, R)
        tin_scalar(e)
        if stage != "id":
            ffn_scalar(f1, e)
        if stage in ("f1a", "full"):
            attn_scalar(e)
        if stage == "full":
            ffn_scalar(f2, e)

    def emit_vector(e):
        e = Ser(e, SV, R)
        tin_vector(e)
        if stage != "id":
            ffn_vector(f1, e)
        if stage in ("f1a", "full"):
            attn_vector(e)
        if stage == "full":
            ffn_vector(f2, e)
        tout_vector(e)

    emits = [emit_sync, emit_gpsimd, emit_tensor, emit_scalar, emit_vector]

    R.plan = True
    for fn in emits:
        fn(DummyEng())
    R.check()
    plan_counts = [(s, s.n) for s in allsems]
    for s in allsems:
        s.n = 0

    R.plan = False
    with ExitStack() as sctx:
        for s in allsems:
            s.h = sctx.enter_context(nc.semaphore(s.name))
        with nc.Block() as block:
            block.sync(emit_sync)
            block.gpsimd(emit_gpsimd)
            block.tensor(emit_tensor)
            block.scalar(emit_scalar)
            block.vector(emit_vector)
        for s, n0 in plan_counts:
            assert s.n == n0, f"pass drift on {s.name}: {s.n} != {n0}"
        ctx.close()
    return nc


_NC_CACHE = {}


def make_inmaps(inputs):
    g = lambda n: np.asarray(inputs[n], dtype=np.float32)
    x = g("L0_attn_x")
    import ml_dtypes
    bf = ml_dtypes.bfloat16

    def fmv16(v):
        return np.ascontiguousarray(v.reshape(16, 128).T)

    def fmv32(v):
        base = v.reshape(16, 128).T
        return np.ascontiguousarray(np.repeat(base, 2, axis=1))

    def aperm(v):
        return np.ascontiguousarray(
            v.reshape(16, 128, 16, 128).transpose(2, 1, 0, 3).reshape(2048, 2048))

    full = {
        "w_f": aperm(g("L0_attn_w_f")), "w_i": aperm(g("L0_attn_w_i")),
        "w_v": aperm(g("L0_attn_w_v")), "w_q": aperm(g("L0_attn_w_q")),
        "w_ro": aperm(g("L0_attn_w_ro")), "w_up": aperm(g("L0_attn_w_up")),
        "w_gt": aperm(g("L0_attn_w_gate")), "w_dn": aperm(g("L0_attn_w_down")),
        "agg_w": g("L0_ffn_agg_w"), "bc_w": g("L0_ffn_bc_w"),
        "gate_w": g("L0_ffn_gate_w"),
        "ingam": fmv16(g("L0_ffn_in_gamma")), "inbet": fmv16(g("L0_ffn_in_beta")),
        "outgam": fmv16(g("L0_ffn_out_gamma")),
        "outbet": fmv16(g("L0_ffn_out_beta")),
        "gateb": fmv16(g("L0_ffn_gate_b")),
        "initf": fmv16(g("L0_attn_init_state")),
        "aggb": fmv32(g("L0_ffn_agg_b")), "bcb": fmv32(g("L0_ffn_bc_b")),
        "d2b": fmv32(g("L0_ffn_d2_b")),
        "png": fmv16(g("L0_ffn_pn_gamma")), "pnb": fmv16(g("L0_ffn_pn_beta")),
        "idn32": np.eye(128, dtype=np.float32),
        "idnb": np.eye(128, dtype=np.float32).astype(bf),
        "onesb": np.ones((128, 128), dtype=np.float32).astype(bf),
    }
    d1w, d2w, d1b = g("L0_ffn_d1_w"), g("L0_ffn_d2_w"), g("L0_ffn_d1_b")

    in_maps = []
    for c in range(NCORE):
        b, s = c // 4, c % 4
        m = dict(full)
        m["xs"] = np.ascontiguousarray(x[b, s * T:(s + 1) * T, :])
        m["d1s"] = np.ascontiguousarray(d1w[:, c * 1024:(c + 1) * 1024])
        m["d2s"] = np.ascontiguousarray(d2w[c * 1024:(c + 1) * 1024, :])
        m["d1bs"] = np.ascontiguousarray(
            np.tile(d1b[c * 1024:(c + 1) * 1024], (2, 1)))
        own = 1.0 if s == 3 else 0.0
        m["mkb0"] = np.full((128, 1), own if b == 0 else 0.0, np.float32)
        m["mkb1"] = np.full((128, 1), own if b == 1 else 0.0, np.float32)
        m["mb0"] = np.full((128, 1), 1.0 if b == 0 else 0.0, np.float32)
        m["mb1"] = np.full((128, 1), 1.0 if b == 1 else 0.0, np.float32)
        pm = np.zeros((128, 8), np.float32)
        for j in range(NCORE):
            if j // 4 == b and j % 4 < s:
                pm[:, j] = 1.0
        m["predm"] = pm
        m["predm1"] = 1.0 - pm
        in_maps.append(m)
    return in_maps


def kernel(**inputs):
    in_maps = make_inmaps(inputs)
    nc = build_nc(STAGE)
    res = run_bass_kernel_spmd(nc, in_maps, core_ids=list(range(NCORE)))
    out = np.empty((2, 8192, 2048), np.float32)
    for c in range(NCORE):
        b, s = c // 4, c % 4
        out[b, s * T:(s + 1) * T, :] = res.results[c]["o"]
    return out


if __name__ == "__main__":
    nc = build_nc(STAGE)
    print("build ok")
